# version 11
# baseline (speedup 1.0000x reference)
import math
from contextlib import ExitStack

import numpy as np
import ml_dtypes

import concourse.bass as bass
import concourse.mybir as mybir
from concourse.bass_utils import run_bass_kernel_spmd

F32 = mybir.dt.float32
BF16 = mybir.dt.bfloat16
I32 = mybir.dt.int32
AF = mybir.ActivationFunctionType
ALU = mybir.AluOpType
AX = mybir.AxisListType
BF = ml_dtypes.bfloat16

D = 4096
NB = 4
S = 2048
TOK = NB * S
TT = 512
NCH = D // 128
NCORE = 8

ENGINES = ("sync", "vector", "scalar", "tensor", "gpsimd")
COMPUTE = ("vector", "scalar", "tensor", "gpsimd")


class Sem:
    __slots__ = ("h", "v")

    def __init__(self, handle):
        self.h = handle
        self.v = 0

    def inc(self, n=1):
        self.v += n
        return self.v


class Prog:
    def __init__(self, name):
        self.name = name
        self.nc = bass.Bass("TRN2", target_bir_lowering=False)
        self.es = ExitStack()
        self.es.__enter__()
        self.ops = {e: [] for e in ENGINES}
        self._uid = 0
        self.sems = {}
        self._open_cms = {}
        self._alloc_seq = {}
        self._sem_free = []
        self._sem_count = 0
        self._consts_frozen = False

    def uid(self):
        self._uid += 1
        return self._uid

    def din(self, name, shape, dtype):
        return self.nc.dram_tensor(name, list(shape), dtype, kind="ExternalInput").ap()

    def dout(self, name, shape, dtype):
        return self.nc.dram_tensor(name, list(shape), dtype, kind="ExternalOutput").ap()

    def dtmp(self, name, shape, dtype):
        return self.nc.dram_tensor(name, list(shape), dtype, kind="Internal").ap()

    def sbuf(self, name, shape, dtype):
        return self.sbuf_rel(name, shape, dtype)

    def psum(self, name, shape, dtype=F32):
        cm = self.nc.psum_tensor(f"{name}_{self.uid()}", list(shape), dtype)
        h = cm.__enter__()
        self._open_cms[id(h)] = cm
        self._alloc_seq[id(h)] = self._uid
        return h

    def sbuf_rel(self, name, shape, dtype):
        cm = self.nc.sbuf_tensor(f"{name}_{self.uid()}", list(shape), dtype)
        h = cm.__enter__()
        self._open_cms[id(h)] = cm
        self._alloc_seq[id(h)] = self._uid
        return h

    def release(self, *handles):
        hs = sorted(handles, key=lambda h: self._alloc_seq[id(h)], reverse=True)
        for h in hs:
            cm = self._open_cms.pop(id(h))
            self._alloc_seq.pop(id(h))
            cm.__exit__(None, None, None)

    def sem(self, name):
        if name not in self.sems:
            if self._sem_free:
                self.sems[name] = self._sem_free.pop()
            else:
                self._sem_count += 1
                self.sems[name] = Sem(self.es.enter_context(
                    self.nc.semaphore(f"sm{self._sem_count}")))
        return self.sems[name]

    def reset_sems(self, keep=()):
        for n in list(self.sems):
            if n.startswith("_ch_") or n.startswith("_bar_") or n in keep:
                continue
            self._sem_free.append(self.sems.pop(n))

    def scope_begin(self):
        return self._uid

    def scope_end(self, mark):
        hids = [hid for hid, sq in self._alloc_seq.items() if sq > mark]
        hids.sort(key=lambda hid: self._alloc_seq[hid], reverse=True)
        for hid in hids:
            cm = self._open_cms.pop(hid)
            self._alloc_seq.pop(hid)
            cm.__exit__(None, None, None)

    def sublayer_end(self, mark):
        self.barrier()
        self.reset_sems()
        self.scope_end(mark)

    def constf(self, value, parts=128):
        key = ("constf", float(value), parts)
        if not hasattr(self, "_consts"):
            self._consts = {}
        if key not in self._consts:
            assert not self._consts_frozen, f"constf {key} not prewarmed"
            t = self.sbuf_rel("cst", [parts, 1], F32)
            self.emit("vector", "memset", t[:, :], float(value))
            self.barrier()
            self._consts[key] = t[:, :]
        return self._consts[key]

    def wait(self, eng, sem, val):
        if val > 0:
            self.ops[eng].append(("wait", sem.h, val))

    CHAINED = ("vector", "scalar", "gpsimd")

    def emit(self, eng, method, *args, sem=None, inc=1, nochain=False, **kwargs):
        if not hasattr(self, "_chain"):
            self._chain = {}
        chain = (not nochain) and eng in self.CHAINED and method != "dma_start"
        if chain:
            prev = self._chain.get(eng)
            if prev is not None:
                self.ops[eng].append(("wait", prev[0], prev[1]))
            if sem is None:
                sem = self.sem(f"_ch_{eng}")
        tgt = sem.inc(inc) if sem is not None else None
        rec = {"m": method, "a": args, "k": kwargs,
               "sems": ([(sem.h, inc)] if sem is not None else [])}
        if sem is not None:
            rec["tok"] = (sem.h, tgt)
        self.ops[eng].append(rec)
        if chain:
            self._chain[eng] = rec["tok"]
        if not hasattr(self, "_last_op"):
            self._last_op = {}
        self._last_op[eng] = rec
        return tgt

    def dma(self, eng, out, in_, sem=None, **kw):
        return self.emit(eng, "dma_start", sem=sem, inc=16, out=out, in_=in_, **kw)

    def barrier(self, engines=ENGINES):
        if not hasattr(self, "_last_op"):
            self._last_op = {}
        toks = {}
        for eng in engines:
            rec = self._last_op.get(eng)
            if rec is not None and "tok" in rec:
                toks[eng] = rec["tok"]
                continue
            b = self.sem(f"_bar_{eng}")
            if rec is None or rec.get("_bar"):
                rec = {"m": "nop", "a": (), "k": {}, "sems": []}
                self.ops[eng].append(rec)
                self._last_op[eng] = rec
            rec["_bar"] = True
            step = 16 if rec["m"] == "dma_start" else 1
            rec["sems"].append((b.h, step))
            b.inc(step)
            rec["tok"] = (b.h, b.v)
            toks[eng] = rec["tok"]
        for eng in engines:
            for e2, (h, v) in toks.items():
                if e2 != eng:
                    self.ops[eng].append(("wait", h, v))
        return None

    def finalize(self):
        def replay(e, recs):
            for r in recs:
                if isinstance(r, tuple):
                    e.wait_ge(r[1], r[2])
                else:
                    ins = getattr(e, r["m"])(*r["a"], **r["k"])
                    for (h, i) in r["sems"]:
                        ins.then_inc(h, i)

        with self.nc.Block() as block:
            for eng in ENGINES:
                recs = self.ops[eng]
                if recs:
                    getattr(block, eng)(lambda e, recs=recs: replay(e, recs))
        for hid in sorted(self._open_cms, key=lambda k: self._alloc_seq[k], reverse=True):
            self._open_cms[hid].__exit__(None, None, None)
        self._open_cms.clear()
        self.es.__exit__(None, None, None)
        return self.nc


class Common:
    def __init__(self, p: Prog):
        self.p = p
        self.ones128_bf = p.sbuf("c_on128", [128, 1], BF16)
        self.ones1_bf = p.sbuf("c_on1", [1, 128], BF16)
        self.ident_bf = p.sbuf("c_idb", [128, 128], BF16)
        self.ident_f = p.sbuf("c_idf", [128, 128], F32)
        s = p.sem("_cinit")
        p.emit("vector", "memset", self.ones128_bf[:], 1.0)
        p.emit("vector", "memset", self.ones1_bf[:], 1.0)
        p.emit("vector", "memset", self.ident_bf[:], 1.0)
        v = p.emit("vector", "memset", self.ident_f[:], 1.0, sem=s)
        p.wait("gpsimd", s, v)
        p.emit("gpsimd", "affine_select", self.ident_bf[:], self.ident_bf[:],
               [[-1, 128]], ALU.is_equal, 0.0, base=0, channel_multiplier=1)
        p.emit("gpsimd", "affine_select", self.ident_f[:], self.ident_f[:],
               [[-1, 128]], ALU.is_equal, 0.0, base=0, channel_multiplier=1, sem=s)
        p.barrier()



def emit_ln(p, cm, x_dram, nt, xhat, stats):
    ntile = nt // TT
    sD = [p.sem("ln_d0"), p.sem("ln_d1")]
    sV = p.sem("ln_v")
    sA = p.sem("ln_a")
    sT = p.sem("ln_t")
    xs = [p.sbuf_rel("ln_xs", [128, nt], F32) for _ in range(2)]
    xb = [p.sbuf_rel("ln_xb", [128, nt], BF16) for _ in range(2)]
    xq = [p.sbuf_rel("ln_xq", [128, nt], BF16) for _ in range(2)]
    ps_su = [p.psum("ln_psu", [1, TT]) for _ in range(ntile)]
    ps_sq = [p.psum("ln_psq", [1, TT]) for _ in range(ntile)]
    m_row = p.sbuf_rel("ln_mrow", [1, nt], BF16)
    r_row = p.sbuf_rel("ln_rrow", [1, nt], BF16)
    scr = p.sbuf_rel("ln_scr", [1, nt], F32)

    sT0 = sT.v
    for c in range(NCH):
        b = c % 2
        if c >= 2:
            p.wait("sync", sT, sT0 + c - 1)
        w = p.dma("sync", xs[b][:, :], x_dram[c * 128:(c + 1) * 128, :], sem=sD[b])
        p.wait("vector", sD[b], w)
        v = p.emit("vector", "tensor_copy", xb[b][:, :], xs[b][:, :], sem=sV)
        p.wait("scalar", sV, v)
        a = p.emit("scalar", "activation", xq[b][:, :], xb[b][:, :], AF.Square, sem=sA)
        p.wait("tensor", sA, a)
        for it in range(ntile):
            sl = slice(it * TT, it * TT + TT)
            p.emit("tensor", "matmul", ps_su[it][0:1, :], cm.ones128_bf[:], xb[b][:, sl],
                   start=(c == 0), stop=(c == NCH - 1))
            p.emit("tensor", "matmul", ps_sq[it][0:1, :], cm.ones128_bf[:], xq[b][:, sl],
                   start=(c == 0), stop=(c == NCH - 1),
                   sem=(sT if it == ntile - 1 else None))
    p.barrier()

    def sl4(row, it):
        t0 = it * TT
        return stats[0:1, row * nt + t0:row * nt + t0 + TT]

    for it in range(ntile):
        sc = scr[0:1, it * TT:(it + 1) * TT]
        p.emit("vector", "tensor_scalar_mul", sl4(0, it), ps_su[it][0:1, :], 1.0 / D)
        p.emit("vector", "tensor_scalar_mul", sl4(3, it), ps_sq[it][0:1, :], 1.0 / D)
        p.emit("vector", "tensor_mul", sc, sl4(0, it), sl4(0, it))
        p.emit("vector", "tensor_sub", sl4(3, it), sl4(3, it), sc)
    p.barrier()
    for it in range(ntile):
        p.emit("scalar", "activation", sl4(1, it), sl4(3, it), AF.Sqrt, bias=p.constf(1e-5, 1), scale=1.0)
    p.barrier()
    for it in range(ntile):
        p.emit("vector", "reciprocal", sl4(1, it), sl4(1, it))
    p.barrier()
    for it in range(ntile):
        p.emit("scalar", "activation", scr[0:1, it * TT:(it + 1) * TT], sl4(3, it), AF.Sqrt)
    p.barrier()
    for it in range(ntile):
        sc = scr[0:1, it * TT:(it + 1) * TT]
        p.emit("vector", "tensor_mul", sl4(2, it), sc, sl4(1, it))
        p.emit("vector", "tensor_scalar_mul", sl4(2, it), sl4(2, it), 64.0)
        p.emit("vector", "tensor_copy", m_row[0:1, it * TT:(it + 1) * TT], sl4(0, it))
        p.emit("vector", "tensor_copy", r_row[0:1, it * TT:(it + 1) * TT], sl4(1, it))
    p.barrier()
    ps_m = [p.psum("ln_pm", [128, TT]) for _ in range(ntile)]
    ps_r = [p.psum("ln_pr", [128, TT]) for _ in range(ntile)]
    sT2 = p.sem("ln_t2")
    for it in range(ntile):
        t0 = it * TT
        p.emit("tensor", "matmul", ps_m[it][:, :], cm.ones1_bf[:], m_row[0:1, t0:t0 + TT],
               start=True, stop=True)
        p.emit("tensor", "matmul", ps_r[it][:, :], cm.ones1_bf[:], r_row[0:1, t0:t0 + TT],
               start=True, stop=True, sem=sT2)
    sD2 = [p.sem("ln_d2a"), p.sem("ln_d2b")]
    sV5 = p.sem("ln_v5")
    p.wait("vector", sT2, sT2.v)
    sV50 = sV5.v
    for c in range(NCH):
        b = c % 2
        if c >= 2:
            p.wait("scalar", sV5, sV50 + ntile * (c - 1))
        w = p.dma("scalar", xs[b][:, :], x_dram[c * 128:(c + 1) * 128, :], sem=sD2[b])
        p.wait("vector", sD2[b], w)
        for it in range(ntile):
            t0 = it * TT
            p.emit("vector", "tensor_sub", xs[b][:, t0:t0 + TT], xs[b][:, t0:t0 + TT], ps_m[it][:, :])
            p.emit("vector", "tensor_mul", xhat[:, c * nt + t0:c * nt + t0 + TT],
                   xs[b][:, t0:t0 + TT], ps_r[it][:, :], sem=sV5)
    p.barrier()
    p.release(*xs, *xb, *xq, *ps_su, *ps_sq, *ps_m, *ps_r, m_row, r_row, scr)



class WStream:

    def __init__(self, p, name, w_dram, nk, cast_eng="scalar", src_bf16=False):
        self.p = p
        self.w = w_dram
        self.nk = nk
        self.O = w_dram.shape[0]
        assert w_dram.shape[1] == nk * 128, (w_dram.shape, nk)
        self.cast_eng = cast_eng
        self.src_bf16 = src_bf16
        if not src_bf16:
            self.wf = [p.sbuf_rel(f"wsf_{name}", [128, 128 * nk], F32) for _ in range(2)]
        self.wb = [p.sbuf_rel(f"wsb_{name}", [128, 128 * nk], BF16) for _ in range(2)]
        self.sW = [p.sem(f"wsw_{name}0"), p.sem(f"wsw_{name}1")]
        self.sG = p.sem(f"wsg_{name}")
        self.free_toks = []
        self.oi = 0

    def load(self, oc):
        p = self.p
        b = self.oi % 2
        tok = None
        if self.oi >= 2:
            assert len(self.free_toks) >= self.oi - 1, "free() not called"
            tok = self.free_toks[self.oi - 2]
            p.ops["sync"].append(("wait", tok[0], tok[1]))
        dst = self.wb if self.src_bf16 else self.wf
        if self.src_bf16 and tok is not None:
            p.ops["sync"].append(("wait", tok[0], tok[1]))
        w = p.dma("sync", dst[b][:, :], self.w[oc * 128:(oc + 1) * 128, :],
                  sem=self.sW[b])
        if self.src_bf16:
            g = self.sG.inc(0) if False else None
            self.oi += 1
            return b, (self.sW[b], w)
        p.wait(self.cast_eng, self.sW[b], w)
        if tok is not None:
            p.ops[self.cast_eng].append(("wait", tok[0], tok[1]))
        if self.cast_eng == "scalar":
            g = p.emit("scalar", "activation", self.wb[b][:, :], self.wf[b][:, :],
                       AF.Copy, sem=self.sG)
        else:
            g = p.emit(self.cast_eng, "tensor_copy", self.wb[b][:, :],
                       self.wf[b][:, :], sem=self.sG)
        self.oi += 1
        return b, g

    def load_tok(self, t):
        return t

    def free(self, tok):
        self.free_toks.append(tok)

    def done(self):
        self.p.barrier()
        self.p.release(*self.wf, *self.wb)


def tile_ws(wT):
    K_, O_ = wT.shape
    nk, noc = K_ // 128, O_ // 128
    return np.ascontiguousarray(
        wT.reshape(nk, 128, noc, 128).transpose(2, 1, 0, 3).reshape(noc * 128, nk * 128))


def load_const_weight(p, name, w_dram, nk, ocols, sW, casts):
    f = p.sbuf_rel(f"cw_f_{name}", [128, nk * ocols], F32)
    bq = p.sbuf(f"cw_b_{name}", [128, nk * ocols], BF16)
    w = p.dma("sync", f[:, :].rearrange("p (c o) -> p c o", o=ocols),
              w_dram.rearrange("(c p) o -> p c o", p=128), sem=sW)
    casts.append((f, bq, w))
    return f, bq



def prewarm(p):
    cm = Common(p)
    for v, parts in ((1e-5, 1), (-20.0, 128), (0.5, 128), (1e-12, 128),
                     (1.0 / math.sqrt(128), 128), (1.0 / math.sqrt(64), 128)):
        p.constf(v, parts)
    p._consts_frozen = True
    return cm


def build_l0_ffn(nt=1024):
    p = Prog("l0ffn")
    x_d = p.din("x", [D, nt], F32)
    bwT = p.din("bwT", [D, D], F32)
    bb2 = p.din("bb2", [128, NCH], F32)
    a0T = p.din("a0T", [D, 64], F32)
    b0T = p.din("b0T", [D, 1024], F32)
    r0T = p.din("r0T", [D, 16], F32)
    xo_d = p.dout("xo", [D, nt], F32)
    cm = prewarm(p)
    emit_l0_ffn(p, cm, x_d, bwT, bb2, a0T, b0T, r0T, xo_d, nt)
    return p


def emit_l0_ffn(p, cm, x_d, bwT, bb2, a0T, b0T, r0T, xo_d, nt=1024):
    ntile = nt // TT

    xhat = p.sbuf("xhat", [128, NCH * nt], BF16)
    stats = p.sbuf("stats", [1, 4 * nt], F32)
    emit_ln(p, cm, x_d, nt, xhat, stats)

    sW = p.sem("f0_w")
    casts = []
    rwt_f, rwt_b = load_const_weight(p, "rw", r0T, NCH, 16, sW, casts)
    a0_f, a0_b = load_const_weight(p, "a0", a0T, NCH, 64, sW, casts)
    bb_f = p.sbuf("bb_f", [128, NCH], F32)
    wv = p.dma("sync", bb_f[:, :], bb2, sem=sW)
    sG = p.sem("f0_g")
    p.wait("gpsimd", sW, sW.v)
    for f, bq, w in casts:
        p.emit("gpsimd", "tensor_copy", bq[:, :], f[:, :], sem=sG)
    p.wait("vector", sW, sW.v)
    p.emit("vector", "tensor_scalar_mul", bb_f[:, :], bb_f[:, :], 0.5)
    p.barrier()

    cb_bf = p.sbuf("cb_bf", [16, nt], BF16)
    selb = p.sbuf("selb", [16, 16 * 64], BF16)
    sinit = p.sem("f0_sel")
    v = p.emit("vector", "memset", selb[:, :], 1.0, sem=sinit)
    p.wait("gpsimd", sinit, v)
    p.emit("gpsimd", "affine_select", selb[:, :], selb[:, :],
           [[1, 16], [0, 64]], ALU.is_equal, 0.0, base=0, channel_multiplier=-1, sem=sinit)
    hb = p.sbuf("hb", [64, nt], BF16)
    cher = p.sbuf("cher", [128, 8 * nt], BF16)
    ps_r = p.psum("ps_r", [16, TT])
    ps_h = p.psum("ps_h", [64, TT])
    ps_tr = p.psum("ps_tr", [128, 16])
    ps_cb = p.psum("ps_cb", [16, 128])
    ps_bc = p.psum("ps_bc", [64, TT])
    lsb = p.sbuf_rel("lsb", [16, TT], F32)
    ltm = p.sbuf_rel("ltm", [128, 16], F32)
    ex = p.sbuf_rel("ex", [128, 16], F32)
    msk = p.sbuf_rel("msk", [128, 16], F32)
    cbtm = p.sbuf_rel("cbtm", [128, 16], F32)
    top8 = p.sbuf_rel("top8", [128, 8], F32)
    ssum = p.sbuf_rel("ssum", [128, 1], F32)
    den = p.sbuf_rel("den", [128, 1], F32)

    sT = p.sem("f0_t")
    sV = p.sem("f0_v")
    sA = p.sem("f0_a")
    for it in range(ntile):
        t0 = it * TT
        for c in range(NCH):
            p.emit("tensor", "matmul", ps_r[:, :], rwt_b[:, c * 16:(c + 1) * 16],
                   xhat[:, c * nt + t0:c * nt + t0 + TT], start=(c == 0), stop=(c == NCH - 1))
        for c in range(NCH):
            p.emit("tensor", "matmul", ps_h[:, :], a0_b[:, c * 64:(c + 1) * 64],
                   xhat[:, c * nt + t0:c * nt + t0 + TT], start=(c == 0), stop=(c == NCH - 1),
                   sem=sT)
        p.wait("vector", sT, sT.v)
        p.emit("vector", "tensor_copy", lsb[:, :], ps_r[:, :])
        p.emit("vector", "tensor_copy", hb[:, t0:t0 + TT], ps_h[:, :], sem=sV)
        p.barrier()
        for g in range(TT // 128):
            gt = t0 + g * 128
            v = p.emit("tensor", "transpose", ps_tr[:, :], lsb[0:16, g * 128:(g + 1) * 128],
                       cm.ident_f[0:16, 0:16], sem=sT)
            p.wait("vector", sT, v)
            v = p.emit("vector", "tensor_copy", ltm[:, :], ps_tr[:, 0:16], sem=sV)
            p.wait("scalar", sV, v)
            a = p.emit("scalar", "activation", ex[:, :], ltm[:, :], AF.Exp, bias=p.constf(-20.0),
                       scale=1.0, accum_out=ssum[:, :], sem=sA)
            p.wait("vector", sA, a)
            v = p.emit("vector", "max", out=top8[:, :], in_=ex[:, :], sem=sV)
            p.wait("vector", sV, v)
            p.emit("vector", "tensor_add", den[:, :], top8[:, 0:1], top8[:, 1:2])
            p.emit("vector", "tensor_scalar_mul", ssum[:, :], ssum[:, :], 1e-6)
            v = p.emit("vector", "tensor_add", den[:, :], den[:, :], ssum[:, :], sem=sV)
            p.wait("vector", sV, v)
            v = p.emit("vector", "reciprocal", den[:, :], den[:, :], sem=sV)
            p.wait("vector", sV, v)
            p.emit("vector", "tensor_scalar_mul", den[:, :], den[:, :], 0.5)
            p.emit("vector", "tensor_scalar", msk[:, :], ex[:, :], top8[:, 1:2], None,
                   ALU.is_ge, ALU.bypass)
            v = p.emit("vector", "scalar_tensor_tensor", out=cbtm[:, :], in0=msk[:, :],
                       scalar=den[:, :], in1=ex[:, :], op0=ALU.mult, op1=ALU.mult, sem=sV)
            p.wait("tensor", sV, v)
            v = p.emit("tensor", "transpose", ps_cb[:, :], cbtm[:, :], cm.ident_f[:, :], sem=sT)
            p.wait("vector", sT, v)
            p.emit("vector", "tensor_copy", cb_bf[:, gt:gt + 128], ps_cb[0:16, 0:128], sem=sV)
        p.barrier()
        for c in range(8):
            for half in range(2):
                e_idx = 2 * c + half
                v = p.emit("tensor", "matmul", ps_bc[:, :], selb[:, e_idx * 64:(e_idx + 1) * 64],
                           cb_bf[0:16, t0:t0 + TT], start=True, stop=True, sem=sT)
                p.wait("vector", sT, v)
                v = p.emit("vector", "tensor_mul",
                           cher[half * 64:(half + 1) * 64, c * nt + t0:c * nt + t0 + TT],
                           hb[:, t0:t0 + TT], ps_bc[:, :], sem=sV)
                p.wait("tensor", sV, v)
        p.barrier()
    p.release(ps_r, ps_h, ps_tr, ps_cb, ps_bc, lsb, ltm, ex, msk, cbtm, top8, ssum, den)

    ws = WStream(p, "bw", bwT, NCH)
    wsb = WStream(p, "b0", b0T, 8)
    ps_y = [p.psum("ps_y", [128, TT]) for _ in range(2)]
    lastA = [0, 0]
    tl = p.sbuf("tl", [128, TT], F32)
    xstage = p.sbuf("xst", [128, TT], F32)
    sDi = p.sem("f0_di")
    sDo = p.sem("f0_do")
    sV2 = p.sem("f0_v2")
    sA2 = p.sem("f0_a2")
    sT2 = p.sem("f0_t2")
    lds = {0: (ws.load(0), wsb.load(0)), 1: (ws.load(1), wsb.load(1))}
    for oc in range(NCH):
        (b1, g1), (b2, g2) = lds[oc]
        p.wait("tensor", ws.sG, g1)
        p.wait("tensor", wsb.sG, g2)
        for it in range(ntile):
            t0 = it * TT
            pi = (oc * ntile + it) % 2
            ps = ps_y[pi]
            p.wait("tensor", sA2, lastA[pi])
            for c in range(NCH):
                p.emit("tensor", "matmul", ps[:, :], ws.wb[b1][:, c * 128:(c + 1) * 128],
                       xhat[:, c * nt + t0:c * nt + t0 + TT], start=(c == 0), stop=False)
            for c in range(8):
                p.emit("tensor", "matmul", ps[:, :], wsb.wb[b2][:, c * 128:(c + 1) * 128],
                       cher[:, c * nt + t0:c * nt + t0 + TT], start=False, stop=(c == 7),
                       sem=(sT2 if c == 7 else None))
            p.wait("vector", sT2, sT2.v)
            p.wait("vector", sV2, sV2.v)
            a = p.emit("vector", "tensor_scalar", tl[:, :], ps[:, :], 0.5,
                       bb_f[:, oc:oc + 1], ALU.mult, ALU.add, sem=sA2)
            lastA[pi] = a
            w = p.dma("sync", xstage[:, :], x_d[oc * 128:(oc + 1) * 128, t0:t0 + TT], sem=sDi)
            p.wait("vector", sA2, a)
            p.wait("vector", sDi, w)
            v = p.emit("vector", "tensor_add", xstage[:, :], xstage[:, :], tl[:, :], sem=sV2)
            p.wait("scalar", sV2, v)
            ow = p.dma("scalar", xo_d[oc * 128:(oc + 1) * 128, t0:t0 + TT], xstage[:, :], sem=sDo)
            p.wait("sync", sDo, ow)
        ws.free((sT2.h, sT2.v))
        wsb.free((sT2.h, sT2.v))
        if oc + 2 < NCH:
            lds[oc + 2] = (ws.load(oc + 2), wsb.load(oc + 2))
    p.wait("scalar", sDo, sDo.v)
    p.barrier(("scalar",))



_PROG_CACHE = {}


def _get_prog(key, builder, *args):
    if key not in _PROG_CACHE:
        _PROG_CACHE[key] = builder(*args).finalize()
    return _PROG_CACHE[key]


def _run(nc, in_maps):
    return run_bass_kernel_spmd(nc, in_maps, core_ids=list(range(NCORE))).results


def _shard_tokens(xF):
    return [np.ascontiguousarray(xF[:, c * 1024:(c + 1) * 1024]) for c in range(NCORE)]


def _unshard(results, key):
    return np.concatenate([np.asarray(r[key], np.float32) for r in results], axis=1)


def l0_ffn_host(xF, w):
    nc = _get_prog("l0ffn", build_l0_ffn)
    base = {"bwT": w["bwT"], "bb2": w["bb2"], "a0T": w["a0T"], "b0T": w["b0T"], "r0T": w["r0T"]}
    in_maps = [dict(base, x=s) for s in _shard_tokens(xF)]
    return _unshard(_run(nc, in_maps), "xo")


def _rope_perm(n_heads, hd):
    per = np.concatenate([np.arange(0, hd, 2), np.arange(1, hd, 2)])
    return np.concatenate([h * hd + per for h in range(n_heads)])


def prep_weights(i):
    w = {}
    w["bwT"] = tile_ws(i["L0_ffn_base_w"].T)
    w["bb2"] = np.ascontiguousarray(i["L0_ffn_base_b"].reshape(NCH, 128).T)
    w["a0T"] = np.ascontiguousarray(i["L0_ffn_A"].T)
    w["b0T"] = tile_ws(np.transpose(i["L0_ffn_B"], (0, 2, 1)).reshape(1024, D))
    w["r0T"] = np.ascontiguousarray(i["L0_ffn_router_w"].T)
    perm0 = _rope_perm(32, 128)
    w["l0_wqT"] = tile_ws(i["L0_attn_wq"].T[:, perm0])
    w["l0_wkT"] = tile_ws(i["L0_attn_wk"].T[:, perm0])
    w["l0_wvT"] = tile_ws(i["L0_attn_wv"].T)
    w["l0_woT"] = tile_ws(i["L0_attn_wo"].T)
    th = np.arange(S, dtype=np.float64)
    w["l0_cos"] = np.ascontiguousarray(np.broadcast_to(np.cos(th)[None, :], (64, S))).astype(np.float32)
    w["l0_sin"] = np.ascontiguousarray(np.broadcast_to(np.sin(th)[None, :], (64, S))).astype(np.float32)

    w["a1T"] = tile_ws(np.transpose(i["L1_ffn_A"], (2, 0, 1)).reshape(D, 1024))
    w["b1T"] = tile_ws(np.transpose(i["L1_ffn_B"], (0, 2, 1)).reshape(1024, D))
    w["wr1T"] = np.ascontiguousarray(i["L1_ffn_Wr"].T)

    def l1_perm(n_rows):
        nchunk = n_rows // 128
        pm = np.zeros(n_rows, np.int64)
        for c in range(nchunk):
            h0, h1 = 2 * c, 2 * c + 1
            pm[c * 128 + 0: c * 128 + 32] = h0 * 64 + np.arange(0, 64, 2)
            pm[c * 128 + 32: c * 128 + 64] = h1 * 64 + np.arange(0, 64, 2)
            pm[c * 128 + 64: c * 128 + 96] = h0 * 64 + np.arange(1, 64, 2)
            pm[c * 128 + 96: c * 128 + 128] = h1 * 64 + np.arange(1, 64, 2)
        return pm
    qpm = l1_perm(1024)
    kpm = l1_perm(256)
    w["l1_qpm"] = qpm
    w["l1_kpm"] = kpm
    w["l1_wqkvT"] = tile_ws(i["L1_attn_w_qkv"].T)
    w["l1_woT"] = tile_ws(i["L1_attn_w_o"].T)
    def blk1(cw, nch):
        out = np.zeros((3, nch, 128, 128), np.float32)
        for d in range(3):
            for c in range(nch):
                for half in range(2):
                    o0 = c * 128 + half * 64
                    out[d, c, half * 64:(half + 1) * 64, half * 64:(half + 1) * 64] =                         cw[o0:o0 + 64, :, d].T
        return out.reshape(3 * nch * 128, 128)
    w["l1_qc1blk"] = blk1(i["L1_attn_qc1_w"], 8)
    w["l1_kc1blk"] = blk1(i["L1_attn_kc1_w"], 2)
    def c2T(cw, pm):
        n = cw.shape[0]
        out = np.zeros((3, n, n), np.float32)
        for d in range(3):
            out[d] = cw[:, :, d].T[:, pm]
        return out.reshape(3 * n, n)
    w["l1_qc2T"] = tile_ws(c2T(i["L1_attn_qc2_w"], qpm))
    w["l1_kc2T"] = tile_ws(c2T(i["L1_attn_kc2_w"], kpm))
    w["l1_qc1b"] = np.ascontiguousarray(i["L1_attn_qc1_b"].reshape(8, 128).T)
    w["l1_kc1b"] = np.ascontiguousarray(i["L1_attn_kc1_b"].reshape(2, 128).T)
    w["l1_qc2b"] = np.ascontiguousarray(i["L1_attn_qc2_b"][qpm].reshape(8, 128).T)
    w["l1_kc2b"] = np.ascontiguousarray(i["L1_attn_kc2_b"][kpm].reshape(2, 128).T)
    def msel(n_in, pm_in, coef):
        tgt = np.zeros(128, np.int64)
        tgt[0:32] = np.arange(0, 64, 2); tgt[32:64] = np.arange(0, 64, 2)
        tgt[64:96] = np.arange(1, 64, 2); tgt[96:128] = np.arange(1, 64, 2)
        out = np.zeros((n_in, 128), np.float32)
        din = pm_in % 64 if False else None
        for r in range(n_in):
            out[r, :] = (tgt == (r % 64)) * coef
        return out
    w["l1_mselk"] = msel(256, None, 1.0 / 8.0)
    w["l1_mselq"] = msel(1024, None, 1.0 / 32.0)
    def headof(n_rows):
        ho = np.zeros(n_rows, np.int64)
        for c in range(n_rows // 128):
            ho[c * 128 + 0:c * 128 + 32] = 2 * c
            ho[c * 128 + 32:c * 128 + 64] = 2 * c + 1
            ho[c * 128 + 64:c * 128 + 96] = 2 * c
            ho[c * 128 + 96:c * 128 + 128] = 2 * c + 1
        return ho
    hoq = headof(1024)
    hok = headof(256)
    w["l1_hselq"] = np.ascontiguousarray((hoq[:, None] == np.arange(16)[None, :]).astype(np.float32))
    w["l1_hbcq"] = np.ascontiguousarray(w["l1_hselq"].T)
    w["l1_hselk"] = np.ascontiguousarray((hok[:, None] == np.arange(4)[None, :]).astype(np.float32))
    w["l1_hbck"] = np.ascontiguousarray(w["l1_hselk"].T)
    w["l1_kt"] = np.ascontiguousarray(i["L1_attn_key_temp"].reshape(1, 1))
    inv = 1.0 / 10000.0 ** (np.arange(0, 64, 2, dtype=np.float64) / 64.0)
    ang = np.arange(S, dtype=np.float64)[:, None] * inv[None, :]
    c1 = np.cos(ang).T.astype(np.float32)
    s1 = np.sin(ang).T.astype(np.float32)
    w["l1_cos"] = np.ascontiguousarray(np.concatenate([c1, c1, c1, c1], axis=0))
    w["l1_sin"] = np.ascontiguousarray(np.concatenate([s1, s1, s1, s1], axis=0))

    w["a2T"] = np.ascontiguousarray(np.transpose(i["L2_ffn_A"], (2, 0, 1)).reshape(D, 128))
    w["b2T"] = tile_ws(np.transpose(i["L2_ffn_Bw"], (0, 2, 1)).reshape(128, D))
    w["rt2T"] = np.ascontiguousarray(i["L2_ffn_route_w"].T)
    wqkv2 = i["L2_attn_w_qkv"]
    permq2 = _rope_perm(32, 128)
    permk2 = _rope_perm(8, 128)
    cols = np.concatenate([permq2, 4096 + permk2, np.arange(5120, 6144)])
    w["l2_wqkvT"] = tile_ws(wqkv2[:, cols])
    w["l2_woT"] = tile_ws(i["L2_attn_w_o"])
    inv2 = 1.0 / 10000.0 ** (np.arange(0, 128, 2, dtype=np.float64) / 128.0)
    ang2 = np.arange(S, dtype=np.float64)[None, :] * inv2[:, None]
    w["l2_cos"] = np.ascontiguousarray(np.cos(ang2)).astype(np.float32)
    w["l2_sin"] = np.ascontiguousarray(np.sin(ang2)).astype(np.float32)

    w["l3_wqT"] = tile_ws(i["L3_attn_wq"])
    w["l3_wkvT"] = tile_ws(np.concatenate([i["L3_attn_wk"], i["L3_attn_wv"]], axis=1))
    w["l3_woT"] = tile_ws(i["L3_attn_wo"])
    w["l3_bias2"] = np.ascontiguousarray(i["L3_ffn_bias"].reshape(NCH, 128).T)
    return w


def attn_host(xF, w, wq_key, wk_key, wv_key, wo_key, cos_key, sin_key,
              progA, progB, progC):
    ncA = progA
    shards = _shard_tokens(xF)
    in_maps = []
    for b in range(NB):
        for r in range(2):
            sl = slice(r * 1024, (r + 1) * 1024)
            in_maps.append({"x": shards[2 * b + r], "wqT": w[wq_key], "wkT": w[wk_key],
                            "wvT": w[wv_key], "cosd": np.ascontiguousarray(w[cos_key][:, sl]),
                            "sind": np.ascontiguousarray(w[sin_key][:, sl])})
    resA = _run(ncA, in_maps)
    qF = [np.asarray(r["qF"]) for r in resA]
    kF = [np.asarray(r["kF"]) for r in resA]
    vT = [np.asarray(r["vT"]) for r in resA]
    in_maps = []
    for b in range(NB):
        qb = np.concatenate([qF[2 * b], qF[2 * b + 1]], axis=1)
        kb = np.concatenate([kF[2 * b], kF[2 * b + 1]], axis=1)
        vb = np.concatenate([vT[2 * b], vT[2 * b + 1]], axis=0)
        for s_ in range(2):
            rows = slice(2048 * s_, 2048 * (s_ + 1))
            in_maps.append({"q": np.ascontiguousarray(qb[rows]),
                            "k": np.ascontiguousarray(kb[rows]),
                            "vT": np.ascontiguousarray(vb[:, rows]),})
    resB = _run(progB, in_maps)
    ao = [np.asarray(r["ao"]) for r in resB]
    in_maps = []
    for b in range(NB):
        aob = np.concatenate([ao[2 * b], ao[2 * b + 1]], axis=0)
        for r in range(2):
            in_maps.append({"x": shards[2 * b + r], "woT": w[wo_key],
                            "ao": np.ascontiguousarray(aob[:, r * 1024:(r + 1) * 1024])})
    resC = _run(progC, in_maps)
    return _unshard(resC, "xo")


def l1_ffn_host(xF, w):
    nc = _get_prog("l1ffn", build_l1_ffn)
    base = {"a1T": w["a1T"], "b1T": w["b1T"], "wrT": w["wr1T"]}
    in_maps = [dict(base, x=s) for s in _shard_tokens(xF)]
    return _unshard(_run(nc, in_maps), "xo")


def l1_attn_host(xF, w):
    ncA1 = _get_prog("l1atA1", build_l1_attn_A1)
    ncA2 = _get_prog("l1atA2", build_l1_attn_A2)
    ncB = _get_prog("l1atB", lambda: build_attn_B(8, 64, 4, tag="l1"))
    ncC = _get_prog("l1atC", lambda: build_attn_C(8, tag="l1"))
    shards = _shard_tokens(xF)
    resA1 = _run(ncA1, [{"x": s, "wqkvT": w["l1_wqkvT"]} for s in shards])
    qkv = [np.asarray(r["qkv"]) for r in resA1]
    zh1024 = np.zeros((1024, 4), BF)
    zh256 = np.zeros((256, 4), BF)
    zh128 = np.zeros((128, 1), BF)
    base2 = {k[3:]: w[k] for k in ("l1_qc1blk", "l1_kc1blk", "l1_qc2T", "l1_kc2T",
                                   "l1_qc1b", "l1_kc1b", "l1_qc2b", "l1_kc2b",
                                   "l1_mselk", "l1_mselq", "l1_hselq", "l1_hbcq",
                                   "l1_hselk", "l1_hbck", "l1_kt")}
    base2 = {"qc1blk": w["l1_qc1blk"], "kc1blk": w["l1_kc1blk"], "qc2T": w["l1_qc2T"],
             "kc2T": w["l1_kc2T"], "qc1b": w["l1_qc1b"], "kc1b": w["l1_kc1b"],
             "qc2b": w["l1_qc2b"], "kc2b": w["l1_kc2b"], "mselk": w["l1_mselk"],
             "mselq": w["l1_mselq"], "hselq": w["l1_hselq"], "hbcq": w["l1_hbcq"],
             "hselk": w["l1_hselk"], "hbck": w["l1_hbck"], "kt": w["l1_kt"]}
    in_maps = []
    for b in range(NB):
        qkv_b = np.concatenate([qkv[2 * b], qkv[2 * b + 1]], axis=1)
        for r in range(2):
            sl = slice(r * 1024, (r + 1) * 1024)
            m = dict(base2,
                     qp=np.ascontiguousarray(qkv_b[0:1024, sl]),
                     kp=np.ascontiguousarray(qkv_b[1024:1280, sl]),
                     vc=np.ascontiguousarray(qkv_b[1280:1408, sl]),
                     vp=np.ascontiguousarray(qkv_b[1408:1536, sl]),
                     qph=(np.ascontiguousarray(qkv_b[0:1024, 1020:1024]) if r else zh1024),
                     kph=(np.ascontiguousarray(qkv_b[1024:1280, 1020:1024]) if r else zh256),
                     vph=(np.ascontiguousarray(qkv_b[1408:1536, 1023:1024]) if r else zh128),
                     cosd=np.ascontiguousarray(w["l1_cos"][:, sl]),
                     sind=np.ascontiguousarray(w["l1_sin"][:, sl]))
            in_maps.append(m)
    resA2 = _run(ncA2, in_maps)
    qh = [np.asarray(r["qhat"]) for r in resA2]
    kh = [np.asarray(r["khat"]) for r in resA2]
    vt = [np.asarray(r["vT"]) for r in resA2]
    in_maps = []
    for b in range(NB):
        qb = np.concatenate([qh[2 * b], qh[2 * b + 1]], axis=1)
        kb = np.concatenate([kh[2 * b], kh[2 * b + 1]], axis=1)
        vb = np.concatenate([vt[2 * b], vt[2 * b + 1]], axis=0)
        for s_ in range(2):
            in_maps.append({"q": np.ascontiguousarray(qb[512 * s_:512 * (s_ + 1)]),
                            "k": np.ascontiguousarray(kb[128 * s_:128 * (s_ + 1)]),
                            "vT": np.ascontiguousarray(vb[:, 128 * s_:128 * (s_ + 1)])})
    resB = _run(ncB, in_maps)
    ao = [np.asarray(r["ao"]) for r in resB]
    shards = _shard_tokens(xF)
    in_maps = []
    for b in range(NB):
        aob = np.concatenate([ao[2 * b], ao[2 * b + 1]], axis=0)
        for r in range(2):
            in_maps.append({"x": shards[2 * b + r], "woT": w["l1_woT"],
                            "ao": np.ascontiguousarray(aob[:, r * 1024:(r + 1) * 1024])})
    return _unshard(_run(ncC, in_maps), "xo")


def l2_ffn_host(xF, w):
    nc = _get_prog("l2ffn", build_l2_ffn)
    base = {"a2T": w["a2T"], "b2T": w["b2T"], "rtT": w["rt2T"]}
    in_maps = [dict(base, x=s) for s in _shard_tokens(xF)]
    return _unshard(_run(nc, in_maps), "xo")


def l2_attn_host(xF, w):
    ncA = _get_prog("l2atA", build_l2_attn_A)
    ncB = _get_prog("l2atB", lambda: build_attn_B(16, 128, 4, tag="l2"))
    ncC = _get_prog("l2atC", lambda: build_attn_C(NCH, tag="l2"))
    shards = _shard_tokens(xF)
    in_maps = []
    for b in range(NB):
        for r in range(2):
            sl = slice(r * 1024, (r + 1) * 1024)
            in_maps.append({"x": shards[2 * b + r], "wqkvT": w["l2_wqkvT"],
                            "cosd": np.ascontiguousarray(w["l2_cos"][:, sl]),
                            "sind": np.ascontiguousarray(w["l2_sin"][:, sl])})
    resA = _run(ncA, in_maps)
    qF = [np.asarray(r["qF"]) for r in resA]
    kF = [np.asarray(r["kF"]) for r in resA]
    vT = [np.asarray(r["vT"]) for r in resA]
    in_maps = []
    for b in range(NB):
        qb = np.concatenate([qF[2 * b], qF[2 * b + 1]], axis=1)
        kb = np.concatenate([kF[2 * b], kF[2 * b + 1]], axis=1)
        vb = np.concatenate([vT[2 * b], vT[2 * b + 1]], axis=0)
        for s_ in range(2):
            in_maps.append({"q": np.ascontiguousarray(qb[2048 * s_:2048 * (s_ + 1)]),
                            "k": np.ascontiguousarray(kb[512 * s_:512 * (s_ + 1)]),
                            "vT": np.ascontiguousarray(vb[:, 512 * s_:512 * (s_ + 1)])})
    resB = _run(ncB, in_maps)
    ao = [np.asarray(r["ao"]) for r in resB]
    in_maps = []
    for b in range(NB):
        aob = np.concatenate([ao[2 * b], ao[2 * b + 1]], axis=0)
        for r in range(2):
            in_maps.append({"x": shards[2 * b + r], "woT": w["l2_woT"],
                            "ao": np.ascontiguousarray(aob[:, r * 1024:(r + 1) * 1024])})
    return _unshard(_run(ncC, in_maps), "xo")


def l3_dequant_host(i):
    nc = _get_prog("l3dq", build_l3_dequant)
    qw = i["L3_ffn_qweight"]
    in_maps = []
    for c in range(NCORE):
        sl = slice(512 * c, 512 * (c + 1))
        in_maps.append({"qw": np.ascontiguousarray(qw[:, sl, :].reshape(8 * 512, 128)),
                        "lut4": np.ascontiguousarray(i["L3_ffn_lut4"][sl]),
                        "lut8": np.ascontiguousarray(i["L3_ffn_lut8"][sl])})
    res = _run(nc, in_maps)
    wlT = np.concatenate([np.asarray(r["wlT"]) for r in res], axis=1)
    whT = np.concatenate([np.asarray(r["whT"]) for r in res], axis=1)
    return tile_ws(wlT), tile_ws(whT)


def l3_ffn_host(xF, w, wlT, whT):
    nc = _get_prog("l3ffn", build_l3_ffn)
    base = {"wlT": wlT, "whT": whT, "bias2": w["l3_bias2"]}
    in_maps = [dict(base, x=s) for s in _shard_tokens(xF)]
    return _unshard(_run(nc, in_maps), "xo")


def l3_attn_host(xF, w):
    ncA = _get_prog("l3atA", build_l3_attn_A)
    ncB = _get_prog("l3atB", lambda: build_attn_B(16, 128, 16, tag="l3"))
    ncC = _get_prog("l3atC", lambda: build_attn_C(NCH, tag="l3"))
    shards = _shard_tokens(xF)
    in_maps = [{"x": s, "wqT": w["l3_wqT"], "wkvT": w["l3_wkvT"]} for s in shards]
    resA = _run(ncA, in_maps)
    qF = [np.asarray(r["qF"]) for r in resA]
    kF = [np.asarray(r["kF"]) for r in resA]
    vT = [np.asarray(r["vT"]) for r in resA]
    in_maps = []
    for b in range(NB):
        qb = np.concatenate([qF[2 * b], qF[2 * b + 1]], axis=1)
        kb = np.ascontiguousarray(np.concatenate([kF[2 * b], kF[2 * b + 1]], axis=1))
        vb = np.ascontiguousarray(np.concatenate([vT[2 * b], vT[2 * b + 1]], axis=0))
        for s_ in range(2):
            in_maps.append({"q": np.ascontiguousarray(qb[2048 * s_:2048 * (s_ + 1)]),
                            "k": kb, "vT": vb})
    resB = _run(ncB, in_maps)
    ao = [np.asarray(r["ao"]) for r in resB]
    in_maps = []
    for b in range(NB):
        aob = np.concatenate([ao[2 * b], ao[2 * b + 1]], axis=0)
        for r in range(2):
            in_maps.append({"x": shards[2 * b + r], "woT": w["l3_woT"],
                            "ao": np.ascontiguousarray(aob[:, r * 1024:(r + 1) * 1024])})
    return _unshard(_run(ncC, in_maps), "xo")


N_SUB = 12



def emit_copy_out(p, src, dst, nk=NCH):
    sC = p.sem("cp_d")
    stg = p.sbuf_rel("cp_stg", [128, src.shape[1]], src.dtype if hasattr(src, "dtype") else F32)
    for c in range(nk):
        w = p.dma("sync", stg[:, :], src[c * 128:(c + 1) * 128, :], sem=sC)
        p.wait("scalar", sC, w)
        w2 = p.dma("scalar", dst[c * 128:(c + 1) * 128, :], stg[:, :], sem=sC)
        p.wait("sync", sC, w2)
    p.barrier()


def build_mega(n_units=12):
    p = Prog("mega")
    S_ = S
    bwT = p.din("bwT", [D, D], F32)
    bb2 = p.din("bb2", [128, NCH], F32)
    a0T = p.din("a0T", [D, 64], F32)
    b0T = p.din("b0T", [D, 1024], F32)
    r0T = p.din("r0T", [D, 16], F32)
    wq0T = p.din("wq0T", [D, D], F32)
    wk0T = p.din("wk0T", [D, D], F32)
    wv0T = p.din("wv0T", [D, D], F32)
    wo0T = p.din("wo0T", [D, D], F32)
    cos0 = p.din("cos0", [64, S_], F32)
    sin0 = p.din("sin0", [64, S_], F32)
    a1T = p.din("a1T", [1024, D], F32)
    b1T = p.din("b1T", [D, 1024], F32)
    wr1T = p.din("wr1T", [D, 64], F32)
    wqkv1T = p.din("wqkv1T", [1536, D], F32)
    wo1T = p.din("wo1T", [D, 1024], F32)
    qc1blk = p.din("qc1blk", [24 * 128, 128], F32)
    kc1blk = p.din("kc1blk", [6 * 128, 128], F32)
    qc2T = p.din("qc2T", [1024, 3 * 1024], F32)
    kc2T = p.din("kc2T", [256, 3 * 256], F32)
    qc1b = p.din("qc1b", [128, 8], F32)
    kc1b = p.din("kc1b", [128, 2], F32)
    qc2b = p.din("qc2b", [128, 8], F32)
    kc2b = p.din("kc2b", [128, 2], F32)
    mselk = p.din("mselk", [256, 128], F32)
    mselq = p.din("mselq", [1024, 128], F32)
    hselq = p.din("hselq", [1024, 16], F32)
    hbcq = p.din("hbcq", [16, 1024], F32)
    hselk = p.din("hselk", [256, 4], F32)
    hbck = p.din("hbck", [4, 256], F32)
    kt = p.din("kt", [1, 1], F32)
    cos1 = p.din("cos1", [128, S_], F32)
    sin1 = p.din("sin1", [128, S_], F32)
    a2T = p.din("a2T", [D, 128], F32)
    b2T = p.din("b2T", [D, 128], F32)
    rt2T = p.din("rt2T", [D, 8], F32)
    wqkv2T = p.din("wqkv2T", [6144, D], F32)
    wo2T = p.din("wo2T", [D, D], F32)
    cos2 = p.din("cos2", [64, S_], F32)
    sin2 = p.din("sin2", [64, S_], F32)
    wq3T = p.din("wq3T", [D, D], F32)
    wkv3T = p.din("wkv3T", [256, D], F32)
    wo3T = p.din("wo3T", [D, D], F32)
    wlT = p.din("wlT", [D, D], BF16)
    whT = p.din("whT", [D, D], BF16)
    bias2 = p.din("bias2", [128, NCH], F32)
    zq = p.din("zq", [1536, 4], BF16)
    x_in = p.din("x", [D, S_], F32)
    xo_h = [p.dout(f"xo{h}", [D, 1024], F32) for h in range(2)]
    cm = prewarm(p)

    def stage(name):
        return [p.dtmp(f"{name}_{h}", [D, 1024], F32) for h in range(2)]

    xs = [[x_in[:, 0:1024], x_in[:, 1024:2048]]] + [stage(f"x{i}") for i in range(1, 12)]
    xs.append(xo_h)
    NT = 1024

    def SL(emitfn, *args, **kw):
        mark = p.scope_begin()
        emitfn(p, cm, *args, **kw)
        p.sublayer_end(mark)

    si = [0]
    done_u = [0]

    def bump():
        si[0] += 1
        done_u[0] += 1
        if done_u[0] == n_units and n_units < 12:
            for h in range(2):
                mark = p.scope_begin()
                emit_copy_out(p, xs[si[0]][h], xo_h[h])
                p.sublayer_end(mark)
            return True
        return False

    def nx(h):
        return xs[si[0] + 1][h]

    def cur(h):
        return xs[si[0]][h]

    def hoff(h):
        return h * NT

    for h in range(2):
        SL(emit_l0_ffn, cur(h), bwT, bb2, a0T, b0T, r0T, nx(h), NT)
    if bump():
        return p
    qF = [p.dtmp(f"l0qF{h}", [D, NT], BF16) for h in range(2)]
    kF = [p.dtmp(f"l0kF{h}", [D, NT], BF16) for h in range(2)]
    vT = [p.dtmp(f"l0vT{h}", [NT, D], BF16) for h in range(2)]
    ao0 = p.dtmp("ao0", [D, S_], BF16)
    for h in range(2):
        SL(emit_l0_attn_A, cur(h), wq0T, wk0T, wv0T,
           cos0[:, hoff(h):hoff(h) + NT], sin0[:, hoff(h):hoff(h) + NT],
           qF[h], kF[h], vT[h], NT)
    SL(emit_attn_B, [(qF[0], 0), (qF[1], NT)], [(kF[0], 0), (kF[1], NT)],
       [(vT[0], 0), (vT[1], NT)], ao0, 32, 128, 1, S_)
    for h in range(2):
        SL(emit_attn_C, cur(h), ao0[:, hoff(h):hoff(h) + NT], wo0T, nx(h), NCH, NT)
    if bump():
        return p
    for h in range(2):
        SL(emit_l0_ffn, cur(h), bwT, bb2, a0T, b0T, r0T, nx(h), NT)
    if bump():
        return p

    for h in range(2):
        SL(emit_l1_ffn, cur(h), a1T, b1T, wr1T, nx(h), NT)
    if bump():
        return p
    qkv = [p.dtmp(f"l1qkv{h}", [1536, NT], BF16) for h in range(2)]
    for h in range(2):
        SL(emit_l1_attn_A1, cur(h), wqkv1T, qkv[h], NT)
    qhat = [p.dtmp(f"l1qh{h}", [1024, NT], BF16) for h in range(2)]
    khat = [p.dtmp(f"l1kh{h}", [256, NT], BF16) for h in range(2)]
    v1T = [p.dtmp(f"l1vT{h}", [NT, 256], BF16) for h in range(2)]
    ao1 = p.dtmp("ao1", [1024, S_], BF16)
    for h in range(2):
        SL(emit_l1_attn_A2, dict(
            qp=qkv[h][0:1024, :], kp=qkv[h][1024:1280, :],
            vc=qkv[h][1280:1408, :], vp=qkv[h][1408:1536, :],
            qph=(zq[0:1024, :] if h == 0 else qkv[0][0:1024, 1020:1024]),
            kph=(zq[1024:1280, :] if h == 0 else qkv[0][1024:1280, 1020:1024]),
            vph=(zq[1408:1536, 0:1] if h == 0 else qkv[0][1408:1536, 1023:1024]),
            qc1blk=qc1blk, kc1blk=kc1blk, qc2T=qc2T, kc2T=kc2T,
            qc1b=qc1b, kc1b=kc1b, qc2b=qc2b, kc2b=kc2b,
            mselk=mselk, mselq=mselq, hselq=hselq, hbcq=hbcq,
            hselk=hselk, hbck=hbck, kt=kt,
            cosd=cos1[:, hoff(h):hoff(h) + NT], sind=sin1[:, hoff(h):hoff(h) + NT],
            qhat=qhat[h], khat=khat[h], vT=v1T[h]), NT)
    SL(emit_attn_B, [(qhat[0], 0), (qhat[1], NT)], [(khat[0], 0), (khat[1], NT)],
       [(v1T[0], 0), (v1T[1], NT)], ao1, 16, 64, 4, S_)
    for h in range(2):
        SL(emit_attn_C, cur(h), ao1[:, hoff(h):hoff(h) + NT], wo1T, nx(h), 8, NT)
    if bump():
        return p
    for h in range(2):
        SL(emit_l1_ffn, cur(h), a1T, b1T, wr1T, nx(h), NT)
    if bump():
        return p

    for h in range(2):
        SL(emit_l2_ffn, cur(h), a2T, b2T, rt2T, nx(h), NT)
    if bump():
        return p
    q2F = [p.dtmp(f"l2qF{h}", [D, NT], BF16) for h in range(2)]
    k2F = [p.dtmp(f"l2kF{h}", [1024, NT], BF16) for h in range(2)]
    v2T = [p.dtmp(f"l2vT{h}", [NT, 1024], BF16) for h in range(2)]
    ao2 = p.dtmp("ao2", [D, S_], BF16)
    for h in range(2):
        SL(emit_l2_attn_A, cur(h), wqkv2T,
           cos2[:, hoff(h):hoff(h) + NT], sin2[:, hoff(h):hoff(h) + NT],
           q2F[h], k2F[h], v2T[h], NT)
    SL(emit_attn_B, [(q2F[0], 0), (q2F[1], NT)], [(k2F[0], 0), (k2F[1], NT)],
       [(v2T[0], 0), (v2T[1], NT)], ao2, 32, 128, 4, S_)
    for h in range(2):
        SL(emit_attn_C, cur(h), ao2[:, hoff(h):hoff(h) + NT], wo2T, nx(h), NCH, NT)
    if bump():
        return p
    for h in range(2):
        SL(emit_l2_ffn, cur(h), a2T, b2T, rt2T, nx(h), NT)
    if bump():
        return p

    for h in range(2):
        SL(emit_l3_ffn, cur(h), wlT, whT, bias2, nx(h), NT)
    if bump():
        return p
    q3F = [p.dtmp(f"l3qF{h}", [D, NT], BF16) for h in range(2)]
    k3F = [p.dtmp(f"l3kF{h}", [128, NT], BF16) for h in range(2)]
    v3T = [p.dtmp(f"l3vT{h}", [NT, 128], BF16) for h in range(2)]
    ao3 = p.dtmp("ao3", [D, S_], BF16)
    for h in range(2):
        SL(emit_l3_attn_A, cur(h), wq3T, wkv3T, q3F[h], k3F[h], v3T[h], NT)
    SL(emit_attn_B, [(q3F[0], 0), (q3F[1], NT)], [(k3F[0], 0), (k3F[1], NT)],
       [(v3T[0], 0), (v3T[1], NT)], ao3, 32, 128, 32, S_)
    for h in range(2):
        SL(emit_attn_C, cur(h), ao3[:, hoff(h):hoff(h) + NT], wo3T, nx(h), NCH, NT)
    if bump():
        return p
    for h in range(2):
        SL(emit_l3_ffn, cur(h), wlT, whT, bias2, nx(h), NT)
    return p


def mega_base(w, wlT, whT):
    return {
        "bwT": w["bwT"], "bb2": w["bb2"], "a0T": w["a0T"], "b0T": w["b0T"],
        "r0T": w["r0T"], "wq0T": w["l0_wqT"], "wk0T": w["l0_wkT"],
        "wv0T": w["l0_wvT"], "wo0T": w["l0_woT"], "cos0": w["l0_cos"],
        "sin0": w["l0_sin"], "a1T": w["a1T"], "b1T": w["b1T"], "wr1T": w["wr1T"],
        "wqkv1T": w["l1_wqkvT"], "wo1T": w["l1_woT"], "qc1blk": w["l1_qc1blk"],
        "kc1blk": w["l1_kc1blk"], "qc2T": w["l1_qc2T"], "kc2T": w["l1_kc2T"],
        "qc1b": w["l1_qc1b"], "kc1b": w["l1_kc1b"], "qc2b": w["l1_qc2b"],
        "kc2b": w["l1_kc2b"], "mselk": w["l1_mselk"], "mselq": w["l1_mselq"],
        "hselq": w["l1_hselq"], "hbcq": w["l1_hbcq"], "hselk": w["l1_hselk"],
        "hbck": w["l1_hbck"], "kt": w["l1_kt"], "cos1": w["l1_cos"],
        "sin1": w["l1_sin"], "a2T": w["a2T"], "b2T": w["b2T"], "rt2T": w["rt2T"],
        "wqkv2T": w["l2_wqkvT"], "wo2T": w["l2_woT"], "cos2": w["l2_cos"],
        "sin2": w["l2_sin"], "wq3T": w["l3_wqT"], "wkv3T": w["l3_wkvT"],
        "wo3T": w["l3_woT"], "wlT": wlT, "whT": whT, "bias2": w["l3_bias2"],
        "zq": np.zeros((1536, 4), BF),
    }


def kernel_mega(inputs):
    w = prep_weights(inputs)
    wlT, whT = l3_dequant_host(inputs)
    nc = _get_prog("mega", build_mega)
    base = mega_base(w, wlT, whT)
    x = inputs["L0_attn_x"]
    xb = [np.ascontiguousarray(x[b].reshape(S, D).T) for b in range(NB)]
    in_maps = [dict(base, x=xb[c // 2]) for c in range(NCORE)]
    res = _run(nc, in_maps)
    out = np.empty((NB, S, D), np.float32)
    for b in range(NB):
        r = res[2 * b]
        xf = np.concatenate([np.asarray(r["xo0"], np.float32),
                             np.asarray(r["xo1"], np.float32)], axis=1)
        out[b] = xf.T
    return out



def build_G0(nt=1024):
    p = Prog("G0")
    x_d = p.din("x", [D, nt], F32)
    bwT = p.din("bwT", [D, D], F32)
    bb2 = p.din("bb2", [128, NCH], F32)
    a0T = p.din("a0T", [D, 64], F32)
    b0T = p.din("b0T", [D, 1024], F32)
    r0T = p.din("r0T", [D, 16], F32)
    wq0T = p.din("wq0T", [D, D], F32)
    wk0T = p.din("wk0T", [D, D], F32)
    wv0T = p.din("wv0T", [D, D], F32)
    cos0 = p.din("cosd", [64, nt], F32)
    sin0 = p.din("sind", [64, nt], F32)
    x1 = p.dout("x1", [D, nt], F32)
    qF = p.dout("qF", [D, nt], BF16)
    kF = p.dout("kF", [D, nt], BF16)
    vT = p.dout("vT", [nt, D], BF16)
    cm = prewarm(p)
    mark = p.scope_begin()
    emit_l0_ffn(p, cm, x_d, bwT, bb2, a0T, b0T, r0T, x1, nt)
    p.sublayer_end(mark)
    mark = p.scope_begin()
    emit_l0_attn_A(p, cm, x1, wq0T, wk0T, wv0T, cos0, sin0, qF, kF, vT, nt)
    p.sublayer_end(mark)
    return p


def build_G1(nt=1024):
    p = Prog("G1")
    x1 = p.din("x1", [D, nt], F32)
    ao = p.din("ao", [D, nt], BF16)
    wo0T = p.din("wo0T", [D, D], F32)
    bwT = p.din("bwT", [D, D], F32)
    bb2 = p.din("bb2", [128, NCH], F32)
    a0T = p.din("a0T", [D, 64], F32)
    b0T = p.din("b0T", [D, 1024], F32)
    r0T = p.din("r0T", [D, 16], F32)
    a1T = p.din("a1T", [1024, D], F32)
    b1T = p.din("b1T", [D, 1024], F32)
    wr1T = p.din("wr1T", [D, 64], F32)
    wqkv1T = p.din("wqkv1T", [1536, D], F32)
    x4 = p.dout("x4", [D, nt], F32)
    qkv = p.dout("qkv", [1536, nt], BF16)
    x2 = p.dtmp("x2", [D, nt], F32)
    x3 = p.dtmp("x3", [D, nt], F32)
    cm = prewarm(p)
    for fn, args in ((emit_attn_C, (x1, ao, wo0T, x2, NCH, nt)),
                     (emit_l0_ffn, (x2, bwT, bb2, a0T, b0T, r0T, x3, nt)),
                     (emit_l1_ffn, (x3, a1T, b1T, wr1T, x4, nt)),
                     (emit_l1_attn_A1, (x4, wqkv1T, qkv, nt))):
        mark = p.scope_begin()
        fn(p, cm, *args)
        p.sublayer_end(mark)
    return p


def build_G3(nt=1024):
    p = Prog("G3")
    x4 = p.din("x4", [D, nt], F32)
    ao = p.din("ao", [1024, nt], BF16)
    wo1T = p.din("wo1T", [D, 1024], F32)
    a1T = p.din("a1T", [1024, D], F32)
    b1T = p.din("b1T", [D, 1024], F32)
    wr1T = p.din("wr1T", [D, 64], F32)
    a2T = p.din("a2T", [D, 128], F32)
    b2T = p.din("b2T", [D, 128], F32)
    rt2T = p.din("rt2T", [D, 8], F32)
    wqkv2T = p.din("wqkv2T", [6144, D], F32)
    cos2 = p.din("cosd", [64, nt], F32)
    sin2 = p.din("sind", [64, nt], F32)
    x7 = p.dout("x7", [D, nt], F32)
    qF = p.dout("qF", [D, nt], BF16)
    kF = p.dout("kF", [1024, nt], BF16)
    vT = p.dout("vT", [nt, 1024], BF16)
    x5 = p.dtmp("x5", [D, nt], F32)
    x6 = p.dtmp("x6", [D, nt], F32)
    cm = prewarm(p)
    for fn, args in ((emit_attn_C, (x4, ao, wo1T, x5, 8, nt)),
                     (emit_l1_ffn, (x5, a1T, b1T, wr1T, x6, nt)),
                     (emit_l2_ffn, (x6, a2T, b2T, rt2T, x7, nt)),
                     (emit_l2_attn_A, (x7, wqkv2T, cos2, sin2, qF, kF, vT, nt))):
        mark = p.scope_begin()
        fn(p, cm, *args)
        p.sublayer_end(mark)
    return p


def build_G4(nt=1024):
    p = Prog("G4")
    x7 = p.din("x7", [D, nt], F32)
    ao = p.din("ao", [D, nt], BF16)
    wo2T = p.din("wo2T", [D, D], F32)
    a2T = p.din("a2T", [D, 128], F32)
    b2T = p.din("b2T", [D, 128], F32)
    rt2T = p.din("rt2T", [D, 8], F32)
    wlT = p.din("wlT", [D, D], BF16)
    whT = p.din("whT", [D, D], BF16)
    bias2 = p.din("bias2", [128, NCH], F32)
    wq3T = p.din("wq3T", [D, D], F32)
    wkv3T = p.din("wkv3T", [256, D], F32)
    x10 = p.dout("x10", [D, nt], F32)
    qF = p.dout("qF", [D, nt], BF16)
    kF = p.dout("kF", [128, nt], BF16)
    vT = p.dout("vT", [nt, 128], BF16)
    x8 = p.dtmp("x8", [D, nt], F32)
    x9 = p.dtmp("x9", [D, nt], F32)
    cm = prewarm(p)
    for fn, args in ((emit_attn_C, (x7, ao, wo2T, x8, NCH, nt)),
                     (emit_l2_ffn, (x8, a2T, b2T, rt2T, x9, nt)),
                     (emit_l3_ffn, (x9, wlT, whT, bias2, x10, nt)),
                     (emit_l3_attn_A, (x10, wq3T, wkv3T, qF, kF, vT, nt))):
        mark = p.scope_begin()
        fn(p, cm, *args)
        p.sublayer_end(mark)
    return p


def build_G5(nt=1024):
    p = Prog("G5")
    x10 = p.din("x10", [D, nt], F32)
    ao = p.din("ao", [D, nt], BF16)
    wo3T = p.din("wo3T", [D, D], F32)
    wlT = p.din("wlT", [D, D], BF16)
    whT = p.din("whT", [D, D], BF16)
    bias2 = p.din("bias2", [128, NCH], F32)
    xo = p.dout("xo", [D, nt], F32)
    x11 = p.dtmp("x11", [D, nt], F32)
    cm = prewarm(p)
    for fn, args in ((emit_attn_C, (x10, ao, wo3T, x11, NCH, nt)),
                     (emit_l3_ffn, (x11, wlT, whT, bias2, xo, nt))):
        mark = p.scope_begin()
        fn(p, cm, *args)
        p.sublayer_end(mark)
    return p


def kernel_grouped(inputs):
    w = prep_weights(inputs)
    wlT, whT = l3_dequant_host(inputs)
    x = inputs["L0_attn_x"]
    xF = np.ascontiguousarray(x.reshape(TOK, D).T)
    shards = _shard_tokens(xF)

    nc = _get_prog("G0", build_G0)
    base = {"bwT": w["bwT"], "bb2": w["bb2"], "a0T": w["a0T"], "b0T": w["b0T"],
            "r0T": w["r0T"], "wq0T": w["l0_wqT"], "wk0T": w["l0_wkT"],
            "wv0T": w["l0_wvT"]}
    in_maps = []
    for b in range(NB):
        for r in range(2):
            sl = slice(r * 1024, (r + 1) * 1024)
            in_maps.append(dict(base, x=shards[2 * b + r],
                                cosd=np.ascontiguousarray(w["l0_cos"][:, sl]),
                                sind=np.ascontiguousarray(w["l0_sin"][:, sl])))
    res = _run(nc, in_maps)
    x1s = [np.asarray(r_["x1"]) for r_ in res]
    qF = [np.asarray(r_["qF"]) for r_ in res]
    kF = [np.asarray(r_["kF"]) for r_ in res]
    vT = [np.asarray(r_["vT"]) for r_ in res]

    ncB = _get_prog("l0atB", lambda: build_attn_B(16, 128, 1, tag="l0"))
    in_maps = []
    for b in range(NB):
        qb = np.concatenate([qF[2 * b], qF[2 * b + 1]], axis=1)
        kb = np.concatenate([kF[2 * b], kF[2 * b + 1]], axis=1)
        vb = np.concatenate([vT[2 * b], vT[2 * b + 1]], axis=0)
        for s_ in range(2):
            rows = slice(2048 * s_, 2048 * (s_ + 1))
            in_maps.append({"q": np.ascontiguousarray(qb[rows]),
                            "k": np.ascontiguousarray(kb[rows]),
                            "vT": np.ascontiguousarray(vb[:, rows])})
    ao = [np.asarray(r_["ao"]) for r_ in _run(ncB, in_maps)]

    nc = _get_prog("G1", build_G1)
    base = {"wo0T": w["l0_woT"], "bwT": w["bwT"], "bb2": w["bb2"], "a0T": w["a0T"],
            "b0T": w["b0T"], "r0T": w["r0T"], "a1T": w["a1T"], "b1T": w["b1T"],
            "wr1T": w["wr1T"], "wqkv1T": w["l1_wqkvT"]}
    in_maps = []
    for b in range(NB):
        aob = np.concatenate([ao[2 * b], ao[2 * b + 1]], axis=0)
        for r in range(2):
            in_maps.append(dict(base, x1=x1s[2 * b + r],
                                ao=np.ascontiguousarray(aob[:, r * 1024:(r + 1) * 1024])))
    res = _run(nc, in_maps)
    x4s = [np.asarray(r_["x4"]) for r_ in res]
    qkv = [np.asarray(r_["qkv"]) for r_ in res]

    ncA2 = _get_prog("l1atA2", build_l1_attn_A2)
    zh1024 = np.zeros((1024, 4), BF)
    zh256 = np.zeros((256, 4), BF)
    zh128 = np.zeros((128, 1), BF)
    base2 = {"qc1blk": w["l1_qc1blk"], "kc1blk": w["l1_kc1blk"], "qc2T": w["l1_qc2T"],
             "kc2T": w["l1_kc2T"], "qc1b": w["l1_qc1b"], "kc1b": w["l1_kc1b"],
             "qc2b": w["l1_qc2b"], "kc2b": w["l1_kc2b"], "mselk": w["l1_mselk"],
             "mselq": w["l1_mselq"], "hselq": w["l1_hselq"], "hbcq": w["l1_hbcq"],
             "hselk": w["l1_hselk"], "hbck": w["l1_hbck"], "kt": w["l1_kt"]}
    in_maps = []
    for b in range(NB):
        for r in range(2):
            sl = slice(r * 1024, (r + 1) * 1024)
            qkv_m = qkv[2 * b + r]
            qkv_p = qkv[2 * b]
            m = dict(base2,
                     qp=np.ascontiguousarray(qkv_m[0:1024]),
                     kp=np.ascontiguousarray(qkv_m[1024:1280]),
                     vc=np.ascontiguousarray(qkv_m[1280:1408]),
                     vp=np.ascontiguousarray(qkv_m[1408:1536]),
                     qph=(np.ascontiguousarray(qkv_p[0:1024, 1020:1024]) if r else zh1024),
                     kph=(np.ascontiguousarray(qkv_p[1024:1280, 1020:1024]) if r else zh256),
                     vph=(np.ascontiguousarray(qkv_p[1408:1536, 1023:1024]) if r else zh128),
                     cosd=np.ascontiguousarray(w["l1_cos"][:, sl]),
                     sind=np.ascontiguousarray(w["l1_sin"][:, sl]))
            in_maps.append(m)
    resA2 = _run(ncA2, in_maps)
    qh = [np.asarray(r_["qhat"]) for r_ in resA2]
    kh = [np.asarray(r_["khat"]) for r_ in resA2]
    vt1 = [np.asarray(r_["vT"]) for r_ in resA2]

    ncB1 = _get_prog("l1atB", lambda: build_attn_B(8, 64, 4, tag="l1"))
    in_maps = []
    for b in range(NB):
        qb = np.concatenate([qh[2 * b], qh[2 * b + 1]], axis=1)
        kb = np.concatenate([kh[2 * b], kh[2 * b + 1]], axis=1)
        vb = np.concatenate([vt1[2 * b], vt1[2 * b + 1]], axis=0)
        for s_ in range(2):
            in_maps.append({"q": np.ascontiguousarray(qb[512 * s_:512 * (s_ + 1)]),
                            "k": np.ascontiguousarray(kb[128 * s_:128 * (s_ + 1)]),
                            "vT": np.ascontiguousarray(vb[:, 128 * s_:128 * (s_ + 1)])})
    ao1 = [np.asarray(r_["ao"]) for r_ in _run(ncB1, in_maps)]

    nc = _get_prog("G3", build_G3)
    base = {"wo1T": w["l1_woT"], "a1T": w["a1T"], "b1T": w["b1T"], "wr1T": w["wr1T"],
            "a2T": w["a2T"], "b2T": w["b2T"], "rt2T": w["rt2T"],
            "wqkv2T": w["l2_wqkvT"]}
    in_maps = []
    for b in range(NB):
        aob = np.concatenate([ao1[2 * b], ao1[2 * b + 1]], axis=0)
        for r in range(2):
            sl = slice(r * 1024, (r + 1) * 1024)
            in_maps.append(dict(base, x4=x4s[2 * b + r],
                                ao=np.ascontiguousarray(aob[:, r * 1024:(r + 1) * 1024]),
                                cosd=np.ascontiguousarray(w["l2_cos"][:, sl]),
                                sind=np.ascontiguousarray(w["l2_sin"][:, sl])))
    res = _run(nc, in_maps)
    x7s = [np.asarray(r_["x7"]) for r_ in res]
    qF = [np.asarray(r_["qF"]) for r_ in res]
    kF = [np.asarray(r_["kF"]) for r_ in res]
    vT = [np.asarray(r_["vT"]) for r_ in res]

    ncB2 = _get_prog("l2atB", lambda: build_attn_B(16, 128, 4, tag="l2"))
    in_maps = []
    for b in range(NB):
        qb = np.concatenate([qF[2 * b], qF[2 * b + 1]], axis=1)
        kb = np.concatenate([kF[2 * b], kF[2 * b + 1]], axis=1)
        vb = np.concatenate([vT[2 * b], vT[2 * b + 1]], axis=0)
        for s_ in range(2):
            in_maps.append({"q": np.ascontiguousarray(qb[2048 * s_:2048 * (s_ + 1)]),
                            "k": np.ascontiguousarray(kb[512 * s_:512 * (s_ + 1)]),
                            "vT": np.ascontiguousarray(vb[:, 512 * s_:512 * (s_ + 1)])})
    ao2 = [np.asarray(r_["ao"]) for r_ in _run(ncB2, in_maps)]

    nc = _get_prog("G4", build_G4)
    base = {"wo2T": w["l2_woT"], "a2T": w["a2T"], "b2T": w["b2T"], "rt2T": w["rt2T"],
            "wlT": wlT, "whT": whT, "bias2": w["l3_bias2"],
            "wq3T": w["l3_wqT"], "wkv3T": w["l3_wkvT"]}
    in_maps = []
    for b in range(NB):
        aob = np.concatenate([ao2[2 * b], ao2[2 * b + 1]], axis=0)
        for r in range(2):
            in_maps.append(dict(base, x7=x7s[2 * b + r],
                                ao=np.ascontiguousarray(aob[:, r * 1024:(r + 1) * 1024])))
    res = _run(nc, in_maps)
    x10s = [np.asarray(r_["x10"]) for r_ in res]
    qF = [np.asarray(r_["qF"]) for r_ in res]
    kF = [np.asarray(r_["kF"]) for r_ in res]
    vT = [np.asarray(r_["vT"]) for r_ in res]

    ncB3 = _get_prog("l3atB", lambda: build_attn_B(16, 128, 16, tag="l3"))
    in_maps = []
    for b in range(NB):
        qb = np.concatenate([qF[2 * b], qF[2 * b + 1]], axis=1)
        kb = np.ascontiguousarray(np.concatenate([kF[2 * b], kF[2 * b + 1]], axis=1))
        vb = np.ascontiguousarray(np.concatenate([vT[2 * b], vT[2 * b + 1]], axis=0))
        for s_ in range(2):
            in_maps.append({"q": np.ascontiguousarray(qb[2048 * s_:2048 * (s_ + 1)]),
                            "k": kb, "vT": vb})
    ao3 = [np.asarray(r_["ao"]) for r_ in _run(ncB3, in_maps)]

    nc = _get_prog("G5", build_G5)
    base = {"wo3T": w["l3_woT"], "wlT": wlT, "whT": whT, "bias2": w["l3_bias2"]}
    in_maps = []
    for b in range(NB):
        aob = np.concatenate([ao3[2 * b], ao3[2 * b + 1]], axis=0)
        for r in range(2):
            in_maps.append(dict(base, x10=x10s[2 * b + r],
                                ao=np.ascontiguousarray(aob[:, r * 1024:(r + 1) * 1024])))
    res = _run(nc, in_maps)
    xo = np.concatenate([np.asarray(r_["xo"], np.float32) for r_ in res], axis=1)
    return np.ascontiguousarray(xo.T).reshape(NB, S, D)


MODE = "grouped"


def kernel(**inputs):
    if MODE == "mega":
        return kernel_mega(inputs)
    if MODE == "grouped":
        return kernel_grouped(inputs)
    return kernel_unfused(**inputs)


def kernel_unfused(**inputs):
    w = prep_weights(inputs)
    x = inputs["L0_attn_x"]
    xF = np.ascontiguousarray(x.reshape(TOK, D).T)
    xF = l0_ffn_host(xF, w)
    if N_SUB >= 2:
        xF = attn_host(xF, w, "l0_wqT", "l0_wkT", "l0_wvT", "l0_woT",
                       "l0_cos", "l0_sin",
                       _get_prog("l0atA", build_l0_attn_A),
                       _get_prog("l0atB", lambda: build_attn_B(16, 128, 1, tag="l0")),
                       _get_prog("l0atC", lambda: build_attn_C(NCH, tag="l0")))
    if N_SUB >= 3:
        xF = l0_ffn_host(xF, w)
    if N_SUB >= 4:
        xF = l1_ffn_host(xF, w)
    if N_SUB >= 5:
        xF = l1_attn_host(xF, w)
    if N_SUB >= 6:
        xF = l1_ffn_host(xF, w)
    if N_SUB >= 7:
        xF = l2_ffn_host(xF, w)
    if N_SUB >= 8:
        xF = l2_attn_host(xF, w)
    if N_SUB >= 9:
        xF = l2_ffn_host(xF, w)
    if N_SUB >= 10:
        wlT, whT = l3_dequant_host(inputs)
        xF = l3_ffn_host(xF, w, wlT, whT)
    if N_SUB >= 11:
        xF = l3_attn_host(xF, w)
    if N_SUB >= 12:
        xF = l3_ffn_host(xF, w, wlT, whT)
    return np.ascontiguousarray(xF.T).reshape(NB, S, D)



def emit_load_act(p, act_dram, nk, nt, dst_sb, sem_name="la"):
    sL = p.sem(f"{sem_name}_d")
    w = None
    for c in range(nk):
        w = p.dma("sync", dst_sb[:, c * nt:(c + 1) * nt],
                  act_dram[c * 128:(c + 1) * 128, :], sem=sL)
    return sL, w



def emit_proj_resid(p, cm, wT_dram, act_sb, nk, nt, x_d, xo_d, scale,
                    tag, bias_sb=None, bias_scale=1.0):
    ntile = nt // TT
    ws = WStream(p, f"pr_{tag}", wT_dram, nk)
    ps_y = [p.psum(f"pr_ps_{tag}", [128, TT]) for _ in range(2)]
    lastE = [0, 0]
    tl = [p.sbuf(f"pr_tl_{tag}", [128, TT], F32) for _ in range(2)]
    xstage = [p.sbuf(f"pr_xs_{tag}", [128, TT], F32) for _ in range(2)]
    sDi = [p.sem(f"pr_di_{tag}{i}") for i in range(2)]
    sDo = p.sem(f"pr_do_{tag}")
    sV2 = p.sem(f"pr_v_{tag}")
    sT2 = p.sem(f"pr_t_{tag}")
    O = wT_dram.shape[0]
    noc = O // 128
    loads = {0: ws.load(0)}
    if noc > 1:
        loads[1] = ws.load(1)
    it_count = [0]
    outw = {}
    for oc in range(noc):
        b1, g1 = loads[oc]
        if ws.src_bf16:
            p.wait("tensor", g1[0], g1[1])
        else:
            p.wait("tensor", ws.sG, g1)
        for it in range(ntile):
            t0 = it * TT
            i2 = it_count[0] % 2
            it_count[0] += 1
            ps = ps_y[i2]
            if it_count[0] - 1 >= 2:
                p.wait("sync", sDo, outw.get(it_count[0] - 3, 0))
            w = p.dma("sync", xstage[i2][:, :], x_d[oc * 128:(oc + 1) * 128, t0:t0 + TT],
                      sem=sDi[i2])
            p.wait("tensor", sV2, lastE[i2])
            for c in range(nk):
                p.emit("tensor", "matmul", ps[:, :], ws.wb[b1][:, c * 128:(c + 1) * 128],
                       act_sb[:, c * nt + t0:c * nt + t0 + TT], start=(c == 0),
                       stop=(c == nk - 1), sem=(sT2 if c == nk - 1 else None))
            p.wait("vector", sT2, sT2.v)
            p.wait("vector", sDi[i2], sDi[i2].v)
            if bias_sb is not None:
                p.emit("vector", "tensor_scalar", tl[i2][:, :], ps[:, :], scale,
                       bias_sb[:, oc:oc + 1], ALU.mult, ALU.add)
                v = p.emit("vector", "tensor_add", xstage[i2][:, :], xstage[i2][:, :],
                           tl[i2][:, :], sem=sV2)
            else:
                v = p.emit("vector", "scalar_tensor_tensor", out=xstage[i2][:, :],
                           in0=ps[:, :], scalar=scale, in1=xstage[i2][:, :],
                           op0=ALU.mult, op1=ALU.add, sem=sV2)
            lastE[i2] = v
            p.wait("scalar", sV2, v)
            outw[it_count[0] - 1] = p.dma("scalar", xo_d[oc * 128:(oc + 1) * 128, t0:t0 + TT],
                                          xstage[i2][:, :], sem=sDo)
        ws.free((sT2.h, sT2.v))
        if oc + 2 < noc:
            loads[oc + 2] = ws.load(oc + 2)
    p.wait("scalar", sDo, sDo.v)
    p.barrier()



def build_l0_attn_A(nt=1024):
    p = Prog("l0atA")
    x_d = p.din("x", [D, nt], F32)
    wqT = p.din("wqT", [D, D], F32)
    wkT = p.din("wkT", [D, D], F32)
    wvT = p.din("wvT", [D, D], F32)
    cos_d = p.din("cosd", [64, nt], F32)
    sin_d = p.din("sind", [64, nt], F32)
    qF_d = p.dout("qF", [D, nt], BF16)
    kF_d = p.dout("kF", [D, nt], BF16)
    vT_d = p.dout("vT", [nt, D], BF16)
    cm = prewarm(p)
    emit_l0_attn_A(p, cm, x_d, wqT, wkT, wvT, cos_d, sin_d, qF_d, kF_d, vT_d, nt)
    return p


def emit_l0_attn_A(p, cm, x_d, wqT, wkT, wvT, cos_d, sin_d, qF_d, kF_d, vT_d,
                   nt=1024):
    ntile = nt // TT
    xhat = p.sbuf("xhat", [128, NCH * nt], BF16)
    stats = p.sbuf("stats", [1, 4 * nt], F32)
    emit_ln(p, cm, x_d, nt, xhat, stats)

    cosb = p.sbuf("cosb", [64, nt], F32)
    sinb = p.sbuf("sinb", [64, nt], F32)
    sC = p.sem("a0_c")
    p.dma("sync", cosb[:, :], cos_d, sem=sC)
    w = p.dma("sync", sinb[:, :], sin_d, sem=sC)
    p.wait("vector", sC, w)
    p.wait("scalar", sC, w)

    sT = p.sem("a0_t")
    sV = p.sem("a0_v")
    sA = p.sem("a0_a")
    sDo = p.sem("a0_do")
    t1 = p.sbuf("ro_t1", [64, TT], F32)
    t2 = p.sbuf("ro_t2", [64, TT], F32)
    stg = p.sbuf("ro_stg", [128, TT], BF16)
    vstg = p.sbuf("v_stg", [128, TT], BF16)
    vtt = p.sbuf("v_tt", [128, 128], BF16)
    ps_y = [p.psum("a0_ps", [128, TT]) for _ in range(2)]
    ps_tr = p.psum("a0_ptr", [128, 128], BF16)
    lastE = [0, 0]
    lastVV = [0]

    def proj_pass(wT, out_is_rope, out_dram):
        ws = WStream(p, f"a0w{p.uid()}", wT, NCH)
        lds = {0: ws.load(0), 1: ws.load(1)}
        for oc in range(NCH):
            b1, g1 = lds[oc]
            p.wait("tensor", ws.sG, g1)
            for it in range(ntile):
                t0 = it * TT
                pi = (oc * ntile + it) % 2
                ps = ps_y[pi]
                p.wait("tensor", sV, lastE[pi])
                for c in range(NCH):
                    p.emit("tensor", "matmul", ps[:, :], ws.wb[b1][:, c * 128:(c + 1) * 128],
                           xhat[:, c * nt + t0:c * nt + t0 + TT], start=(c == 0),
                           stop=(c == NCH - 1), sem=(sT if c == NCH - 1 else None))
                p.wait("vector", sT, sT.v)
                if out_is_rope:
                    cs = cosb[:, t0:t0 + TT]
                    sn = sinb[:, t0:t0 + TT]
                    pse = ps[0:64, :]
                    pso = ps[64:128, :]
                    p.emit("vector", "tensor_mul", t1[:, :], pse, cs)
                    p.emit("vector", "tensor_mul", t2[:, :], pso, sn)
                    p.emit("vector", "tensor_sub", stg[0:64, :], t1[:, :], t2[:, :])
                    p.emit("vector", "tensor_mul", t1[:, :], pse, sn)
                    p.emit("vector", "tensor_mul", t2[:, :], pso, cs)
                    v = p.emit("vector", "tensor_add", stg[64:128, :], t1[:, :], t2[:, :], sem=sV)
                    lastE[pi] = v
                    p.wait("scalar", sV, v)
                    ow = p.dma("scalar", out_dram[oc * 128:(oc + 1) * 128, t0:t0 + TT],
                               stg[:, :], sem=sDo)
                    p.wait("vector", sDo, ow)
                else:
                    v = p.emit("vector", "tensor_copy", vstg[:, :], ps[:, :], sem=sV)
                    lastE[pi] = v
                    for g in range(TT // 128):
                        p.wait("tensor", sV, v)
                        p.wait("tensor", sV, lastVV[0])
                        tv = p.emit("tensor", "transpose", ps_tr[:, :],
                                    vstg[:, g * 128:(g + 1) * 128], cm.ident_bf[:, :], sem=sT)
                        p.wait("vector", sT, tv)
                        vv = p.emit("vector", "tensor_copy", vtt[:, :], ps_tr[:, :], sem=sV)
                        lastVV[0] = vv
                        p.wait("scalar", sV, vv)
                        ow = p.dma("scalar", vT_d[t0 + g * 128:t0 + (g + 1) * 128,
                                                  oc * 128:(oc + 1) * 128], vtt[:, :], sem=sDo)
                        p.wait("vector", sDo, ow)
            ws.free((sT.h, sT.v))
            if oc + 2 < NCH:
                lds[oc + 2] = ws.load(oc + 2)
        ws.done()

    proj_pass(wqT, True, qF_d)
    proj_pass(wkT, True, kF_d)
    proj_pass(wvT, False, None)
    p.wait("scalar", sDo, sDo.v)
    p.barrier()


def build_attn_C(nk=NCH, nt=1024, tag="c"):
    p = Prog(f"attC_{tag}")
    x_d = p.din("x", [D, nt], F32)
    ao_d = p.din("ao", [nk * 128, nt], BF16)
    woT = p.din("woT", [D, nk * 128], F32)
    xo_d = p.dout("xo", [D, nt], F32)
    cm = prewarm(p)
    emit_attn_C(p, cm, x_d, ao_d, woT, xo_d, nk, nt)
    return p


def emit_attn_C(p, cm, x_d, ao_d, woT, xo_d, nk=NCH, nt=1024):
    act = p.sbuf("act", [128, nk * nt], BF16)
    sL, w = emit_load_act(p, ao_d, nk, nt, act)
    p.wait("tensor", sL, w)
    p.wait("vector", sL, w)
    emit_proj_resid(p, cm, woT, act, nk, nt, x_d, xo_d, 1.0, "c")



def build_l1_ffn(nt=1024):
    p = Prog("l1ffn")
    x_d = p.din("x", [D, nt], F32)
    a1T = p.din("a1T", [1024, D], F32)
    b1T = p.din("b1T", [D, 1024], F32)
    wrT = p.din("wrT", [D, 64], F32)
    xo_d = p.dout("xo", [D, nt], F32)
    cm = prewarm(p)
    emit_l1_ffn(p, cm, x_d, a1T, b1T, wrT, xo_d, nt)
    return p


def emit_l1_ffn(p, cm, x_d, a1T, b1T, wrT, xo_d, nt=1024):
    ntile = nt // TT

    xhat = p.sbuf("xhat", [128, NCH * nt], BF16)
    stats = p.sbuf("stats", [1, 4 * nt], F32)
    emit_ln(p, cm, x_d, nt, xhat, stats)

    sW = p.sem("f1_w")
    casts = []
    wr_f, wr_b = load_const_weight(p, "wr1", wrT, NCH, 64, sW, casts)
    sG = p.sem("f1_g")
    p.wait("gpsimd", sW, sW.v)
    for f, bq, wv_ in casts:
        p.emit("gpsimd", "tensor_copy", bq[:, :], f[:, :], sem=sG)
    p.barrier()

    gf_bf = p.sbuf("gf_bf", [64, nt], BF16)
    hb8 = p.sbuf("hb8", [128, 8 * nt], BF16)
    ghb = p.sbuf("ghb", [128, 8 * nt], BF16)
    selb1 = p.sbuf("selb1", [64, 1024], BF16)
    sI = p.sem("f1_i")
    v = p.emit("vector", "memset", selb1[:, :], 1.0, sem=sI)
    p.wait("gpsimd", sI, v)
    p.emit("gpsimd", "affine_select", selb1[:, :], selb1[:, :],
           [[8, 8], [1, 8], [0, 16]], ALU.is_equal, 0.0, base=0, channel_multiplier=-1,
           sem=sI)
    p.barrier()

    ps_r = p.psum("f1_pr", [64, TT])
    ps_h = p.psum("f1_ph", [128, TT])
    ps_tr = p.psum("f1_ptr", [128, 64])
    ps_g = p.psum("f1_pg", [64, 128])
    ps_ge = p.psum("f1_pge", [128, TT])
    lsb = p.sbuf("lsb", [64, TT], F32)
    ltm = p.sbuf("ltm", [128, 64], F32)
    ex = p.sbuf("ex", [128, 64], F32)
    lx = p.sbuf("lx", [128, 64], F32)
    gtm = p.sbuf("gtm", [128, 64], F32)
    top8a = p.sbuf("top8a", [128, 8], F32)
    top8b = p.sbuf("top8b", [128, 8], F32)
    ssum = p.sbuf("ssum", [128, 1], F32)

    wsa = WStream(p, "a1", a1T, NCH)
    sT = p.sem("f1_t")
    sV = p.sem("f1_v")
    sA = p.sem("f1_a")
    lds = {0: wsa.load(0), 1: wsa.load(1)}
    for oc in range(8):
        b1_, g1 = lds[oc]
        p.wait("tensor", wsa.sG, g1)
        for it in range(ntile):
            t0 = it * TT
            for c in range(NCH):
                p.emit("tensor", "matmul", ps_h[:, :], wsa.wb[b1_][:, c * 128:(c + 1) * 128],
                       xhat[:, c * nt + t0:c * nt + t0 + TT], start=(c == 0),
                       stop=(c == NCH - 1), sem=(sT if c == NCH - 1 else None))
            p.wait("vector", sT, sT.v)
            v = p.emit("vector", "tensor_copy", hb8[:, oc * nt + t0:oc * nt + t0 + TT],
                       ps_h[:, :], sem=sV)
            p.wait("tensor", sV, v)
        wsa.free((sT.h, sT.v))
        if oc + 2 < 8:
            lds[oc + 2] = wsa.load(oc + 2)
    wsa.done()

    for it in range(ntile):
        t0 = it * TT
        for c in range(NCH):
            p.emit("tensor", "matmul", ps_r[:, :], wr_b[:, c * 64:(c + 1) * 64],
                   xhat[:, c * nt + t0:c * nt + t0 + TT], start=(c == 0), stop=(c == NCH - 1),
                   sem=(sT if c == NCH - 1 else None))
        p.wait("vector", sT, sT.v)
        v = p.emit("vector", "tensor_copy", lsb[:, :], ps_r[:, :], sem=sV)
        p.barrier()
        for g in range(TT // 128):
            gt = t0 + g * 128
            v = p.emit("tensor", "transpose", ps_tr[:, :], lsb[0:64, g * 128:(g + 1) * 128],
                       cm.ident_f[0:64, 0:64], sem=sT)
            p.wait("vector", sT, v)
            v = p.emit("vector", "tensor_copy", ltm[:, :], ps_tr[:, 0:64], sem=sV)
            p.emit("vector", "max", out=top8a[:, :], in_=ltm[:, :])
            p.emit("vector", "match_replace", out=lx[:, :], in_to_replace=top8a[:, :],
                   in_values=ltm[:, :], imm_value=-1e30)
            v = p.emit("vector", "max", out=top8b[:, :], in_=lx[:, :], sem=sV)
            p.wait("scalar", sV, v)
            a = p.emit("scalar", "activation", ex[:, :], ltm[:, :], AF.Exp,
                       bias=p.constf(-20.0), scale=1.0, sem=sA)
            p.wait("vector", sA, a)
            p.emit("vector", "tensor_scalar", lx[:, :], ltm[:, :], top8b[:, 7:8], None,
                   ALU.is_ge, ALU.bypass)
            p.emit("vector", "tensor_mul", ex[:, :], ex[:, :], lx[:, :])
            p.emit("vector", "reduce_sum", ssum[:, :], ex[:, :], axis=AX.X)
            p.emit("vector", "reciprocal", ssum[:, :], ssum[:, :])
            v = p.emit("vector", "tensor_scalar", gtm[:, :], ex[:, :], ssum[:, :], None,
                       ALU.mult, ALU.bypass, sem=sV)
            p.wait("tensor", sV, v)
            v = p.emit("tensor", "transpose", ps_g[:, :], gtm[:, :], cm.ident_f[:, :], sem=sT)
            p.wait("vector", sT, v)
            v = p.emit("vector", "tensor_copy", gf_bf[:, gt:gt + 128], ps_g[0:64, :], sem=sV)
            p.wait("tensor", sV, v)
        p.barrier()
        for c in range(8):
            v = p.emit("tensor", "matmul", ps_ge[:, :], selb1[:, c * 128:(c + 1) * 128],
                       gf_bf[:, t0:t0 + TT], start=True, stop=True, sem=sT)
            p.wait("vector", sT, v)
            v = p.emit("vector", "tensor_mul", ghb[:, c * nt + t0:c * nt + t0 + TT],
                       hb8[:, c * nt + t0:c * nt + t0 + TT], ps_ge[:, :], sem=sV)
            p.wait("tensor", sV, v)
        p.barrier()
    p.release(ps_r, ps_h, ps_tr, ps_g, ps_ge, lsb, ltm, ex, lx, gtm, top8a, top8b, ssum)

    emit_proj_resid(p, cm, b1T, ghb, 8, nt, x_d, xo_d, 1.0, "f1")



def build_attn_B(nh, hd, rep, Sq=2048, hpl=None, tag="b"):
    p = Prog(f"attB_{tag}")
    nkv = nh // rep
    q_d = p.din("q", [nh * hd, Sq], BF16)
    k_d = p.din("k", [nkv * hd, Sq], BF16)
    vT_d = p.din("vT", [Sq, nkv * hd], BF16)
    ao_d = p.dout("ao", [nh * hd, Sq], BF16)
    cm = prewarm(p)
    emit_attn_B(p, cm, [(q_d, 0)], [(k_d, 0)], [(vT_d, 0)], ao_d,
                nh, hd, rep, Sq, hpl)
    return p


def emit_attn_B(p, cm, qs, ks, vs, ao_d, nh, hd, rep, Sq=2048, hpl=None):
    nkv = nh // rep
    scale = 1.0 / math.sqrt(hd)
    sc_ap = p.constf(scale)
    if hpl is None:
        hpl = 1 if hd == 128 else nh
    band = p.sbuf("band", [128, 1024], BF16)
    sB = p.sem("b_band")
    v = p.emit("vector", "memset", band[:, :], 1.0, sem=sB)
    p.wait("gpsimd", sB, v)
    p.emit("gpsimd", "affine_select", band[:, :], band[:, :], [[1, 1024]],
           ALU.is_ge, 0.0, base=-512, channel_multiplier=-1, sem=sB)
    p.barrier()

    sD = p.sem("b_d")
    sT = p.sem("b_t")
    sV = p.sem("b_v")
    sA = p.sem("b_a")
    sDo = p.sem("b_do")

    stg1 = [p.sbuf("b_s1", [128, Sq], BF16) for _ in range(2)]
    sq2 = p.sbuf("b_sq2", [128, Sq], BF16)
    red = p.sbuf("b_red", [1, 16], F32)
    negc = p.sbuf("b_negc", [128, 1], F32)
    crow = p.sbuf("b_crow", [1, 1], BF16)
    ps_n = p.psum("b_pn", [1, TT])
    ps_b = p.psum("b_pb", [128, 1])
    nqt = Sq // TT
    rv = 0
    sD1 = [p.sem("b_d1a"), p.sem("b_d1b")]
    chunks = [(qs, c, 0) for c in range(nh * hd // 128)] + \
             [(ks, c, 8) for c in range(nkv * hd // 128)]
    seen = set()
    for gi, (segs, c, ri) in enumerate(chunks):
        b = gi % 2
        if gi >= 2:
            p.wait("sync", sA, sA.v)
        for (seg, off) in segs:
            w = p.dma("sync", stg1[b][:, off:off + seg.shape[1]],
                      seg[c * 128:(c + 1) * 128, :], sem=sD1[b])
        p.wait("scalar", sD1[b], w)
        a = p.emit("scalar", "activation", sq2[:, :], stg1[b][:, :], AF.Square, sem=sA)
        p.wait("tensor", sA, a)
        for it in range(nqt):
            p.wait("tensor", sV, rv)
            p.emit("tensor", "matmul", ps_n[0:1, :], cm.ones128_bf[:, :],
                   sq2[:, it * TT:(it + 1) * TT], start=True, stop=True, sem=sT)
            p.wait("vector", sT, sT.v)
            if ri not in seen:
                seen.add(ri)
                rv = p.emit("vector", "reduce_max", red[0:1, ri:ri + 1], ps_n[0:1, :],
                            axis=AX.X, sem=sV)
            else:
                p.emit("vector", "reduce_max", red[0:1, ri + 1:ri + 2], ps_n[0:1, :],
                       axis=AX.X)
                rv = p.emit("vector", "tensor_max", red[0:1, ri:ri + 1],
                            red[0:1, ri:ri + 1], red[0:1, ri + 1:ri + 2], sem=sV)
        p.wait("scalar", sT, sT.v)
    p.wait("scalar", sV, rv)
    p.emit("scalar", "activation", red[0:1, 2:3], red[0:1, 0:1], AF.Sqrt)
    a = p.emit("scalar", "activation", red[0:1, 3:4], red[0:1, 8:9], AF.Sqrt, sem=sA)
    p.wait("vector", sA, a)
    p.emit("vector", "tensor_mul", red[0:1, 2:3], red[0:1, 2:3], red[0:1, 3:4])
    p.emit("vector", "tensor_scalar_mul", red[0:1, 2:3], red[0:1, 2:3], -scale)
    v = p.emit("vector", "tensor_copy", crow[0:1, 0:1], red[0:1, 2:3], sem=sV)
    p.wait("tensor", sV, v)
    v = p.emit("tensor", "matmul", ps_b[:, :], cm.ones1_bf[0:1, :], crow[0:1, 0:1],
               start=True, stop=True, sem=sT)
    p.wait("vector", sT, v)
    v = p.emit("vector", "tensor_copy", negc[:, :], ps_b[:, :], sem=sV)
    p.wait("scalar", sV, v)
    p.barrier()
    p.release(ps_n, ps_b)

    NSL = 4
    ps_sc = [p.psum("b_psc", [128, TT]) for _ in range(NSL)]
    ps_o = [p.psum("b_po", [128, TT]) for _ in range(2)]
    ps_sr = [p.psum("b_psr", [128, TT]) for _ in range(2)]
    nqch = max(1, (hpl * hd) // 128)
    kvset_sz = max(1, hpl // rep)
    ntk = Sq // 128
    qg = p.sbuf("qg", [128, nqch * Sq], BF16)
    kg = p.sbuf("kg", [128, kvset_sz * Sq], BF16)
    vg = p.sbuf("vg", [128, ntk * kvset_sz * hd], BF16)
    probs = [p.sbuf("b_pr", [128, TT], BF16) for _ in range(NSL)]
    nrm = [p.sbuf("b_nrm", [1, TT], F32) for _ in range(2)]
    nrmb = [p.sbuf("b_nrmb", [1, TT], BF16) for _ in range(2)]
    rb = [p.sbuf("b_rb", [128, TT], F32) for _ in range(2)]
    aostg = [p.sbuf("b_ao", [128, TT], BF16) for _ in range(2)]
    sE = p.sem("b_e")
    sM = p.sem("b_m")
    sPV = p.sem("b_pv")
    sNm = p.sem("b_nm")
    exp_tok = {}
    pv_tok = {}
    last_evac = [0, 0]
    last_rbps = [0, 0]

    n_lg = nh // hpl
    gkc = [0]
    for lg in range(n_lg):
        p.wait("sync", sDo, sDo.v)
        h0 = lg * hpl
        kv0 = h0 // rep
        for c in range(nqch):
            for (seg, off) in qs:
                p.dma("sync", qg[:, c * Sq + off:c * Sq + off + seg.shape[1]],
                      seg[(h0 * hd) + c * 128:(h0 * hd) + (c + 1) * 128, :], sem=sD)
        if hd == 128:
            for c in range(kvset_sz):
                for (seg, off) in ks:
                    p.dma("sync", kg[:, c * Sq + off:c * Sq + off + seg.shape[1]],
                          seg[(kv0 + c) * hd:(kv0 + c + 1) * hd, :], sem=sD)
        else:
            for c in range(kvset_sz):
                for (seg, off) in ks:
                    p.dma("sync", kg[0:64, c * Sq + off:c * Sq + off + seg.shape[1]],
                          seg[(kv0 + c) * hd:(kv0 + c + 1) * hd, :], sem=sD)
                    p.dma("sync", kg[64:128, c * Sq + off:c * Sq + off + seg.shape[1]],
                          seg[(kv0 + c) * hd:(kv0 + c + 1) * hd, :], sem=sD)
        for (vseg, r0) in vs:
            ck0 = (r0 // 128) * kvset_sz * hd
            w = p.dma("sync",
                      vg[:, ck0:ck0 + (vseg.shape[0] // 128) * kvset_sz * hd]
                      .rearrange("p (kc c) -> p kc c", c=kvset_sz * hd),
                      vseg.rearrange("(kc p) c -> p kc c", p=128)
                      [:, :, kv0 * hd:(kv0 + kvset_sz) * hd], sem=sD)
        p.wait("scalar", sD, sD.v)
        p.wait("tensor", sD, sD.v)
        p.wait("vector", sD, sD.v)

        for hl in range(hpl):
            h = h0 + hl
            qch, qbase = (hl * hd) // 128, (hl * hd) % 128
            kvl = (h // rep) - kv0
            for qt in range(nqt):
                q0 = qt * TT
                nkc = (q0 + TT) // 128
                oslot = (h * nqt + qt) % 2

                def emit_qk(kc):
                    g = qk_g[kc]
                    sl = g % NSL
                    if g >= NSL:
                        p.wait("tensor", exp_tok[g - NSL][0], exp_tok[g - NSL][1])
                    tok_qk[g] = p.emit("tensor", "matmul", ps_sc[sl][0:128, :],
                           kg[qbase:qbase + hd, kvl * Sq + kc * 128:kvl * Sq + (kc + 1) * 128],
                           qg[qbase:qbase + hd, qch * Sq + q0:qch * Sq + q0 + TT],
                           start=True, stop=True, sem=sT)

                qk_g = {}
                tok_qk = {}
                for kc in range(nkc):
                    qk_g[kc] = gkc[0]
                    gkc[0] += 1
                emit_qk(0)
                for kc in range(nkc):
                    g = qk_g[kc]
                    sl = g % NSL
                    p.wait("scalar", sT, tok_qk[g])
                    if g >= NSL:
                        p.wait("scalar", sPV, pv_tok[g - NSL])
                    j = kc - 4 * qt
                    a = p.emit("scalar", "activation", probs[sl][:, :], ps_sc[sl][:, :],
                               AF.Exp, bias=negc[:, :], scale=sc_ap, sem=sE)
                    tok = (sE, a)
                    if j >= 0:
                        off = 512 - 128 * j
                        p.wait("vector", sE, a)
                        a = p.emit("vector", "tensor_mul", probs[sl][:, :], probs[sl][:, :],
                                   band[:, off:off + TT], sem=sM)
                        tok = (sM, a)
                    exp_tok[g] = tok
                    if kc + 1 < nkc:
                        emit_qk(kc + 1)
                    p.wait("tensor", tok[0], tok[1])
                    if kc == 0:
                        p.wait("tensor", sNm, last_evac[oslot])
                    p.emit("tensor", "matmul", ps_sr[oslot][0:1, :], cm.ones128_bf[:, :],
                           probs[sl][:, :], start=(kc == 0), stop=(kc == nkc - 1))
                    pv_tok[g] = p.emit("tensor", "matmul", ps_o[oslot][0:hd, :],
                           vg[:, kc * kvset_sz * hd + kvl * hd:kc * kvset_sz * hd + (kvl + 1) * hd],
                           probs[sl][:, :], start=(kc == 0), stop=(kc == nkc - 1), sem=sPV)
                p.wait("vector", sPV, sPV.v)
                p.wait("vector", sNm, last_rbps[oslot])
                p.emit("vector", "tensor_copy", nrm[oslot][0:1, :], ps_sr[oslot][0:1, :])
                p.emit("vector", "reciprocal", nrm[oslot][0:1, :], nrm[oslot][0:1, :])
                v = p.emit("vector", "tensor_copy", nrmb[oslot][0:1, :], nrm[oslot][0:1, :], sem=sV)
                p.wait("tensor", sV, v)
                v = p.emit("tensor", "matmul", ps_sr[oslot][:, :], cm.ones1_bf[:, :],
                           nrmb[oslot][0:1, :], start=True, stop=True, sem=sT)
                p.wait("vector", sT, v)
                p.emit("vector", "tensor_copy", rb[oslot][:, :], ps_sr[oslot][:, :])
                v = p.emit("vector", "tensor_mul", aostg[oslot][0:hd, :], ps_o[oslot][0:hd, :],
                           rb[oslot][0:hd, :], sem=sNm)
                last_evac[oslot] = v
                p.wait("scalar", sNm, v)
                ow = p.dma("scalar", ao_d[h * hd:(h + 1) * hd, q0:q0 + TT],
                           aostg[oslot][0:hd, :], sem=sDo)
                last_rbps[oslot] = sNm.inc(0) if False else last_evac[oslot]
                p.wait("vector", sDo, ow)
    p.wait("scalar", sDo, sDo.v)
    p.barrier()



def build_l1_attn_A1(nt=1024):
    p = Prog("l1atA1")
    x_d = p.din("x", [D, nt], F32)
    wqkvT = p.din("wqkvT", [1536, D], F32)
    qkv_d = p.dout("qkv", [1536, nt], BF16)
    cm = prewarm(p)
    emit_l1_attn_A1(p, cm, x_d, wqkvT, qkv_d, nt)
    return p


def emit_l1_attn_A1(p, cm, x_d, wqkvT, qkv_d, nt=1024):
    ntile = nt // TT
    xhat = p.sbuf("xhat", [128, NCH * nt], BF16)
    stats = p.sbuf("stats", [1, 4 * nt], F32)
    emit_ln(p, cm, x_d, nt, xhat, stats)
    ws = WStream(p, "qkv1", wqkvT, NCH)
    ps_y = [p.psum("a1_ps", [128, TT]) for _ in range(2)]
    stg = p.sbuf("a1_stg", [128, TT], BF16)
    lastE = [0, 0]
    sT = p.sem("a1_t")
    sV = p.sem("a1_v")
    sDo = p.sem("a1_do")
    lds = {0: ws.load(0), 1: ws.load(1)}
    for oc in range(12):
        b1, g1 = lds[oc]
        p.wait("tensor", ws.sG, g1)
        for it in range(ntile):
            t0 = it * TT
            pi = (oc * ntile + it) % 2
            ps = ps_y[pi]
            p.wait("tensor", sV, lastE[pi])
            for c in range(NCH):
                p.emit("tensor", "matmul", ps[:, :], ws.wb[b1][:, c * 128:(c + 1) * 128],
                       xhat[:, c * nt + t0:c * nt + t0 + TT], start=(c == 0),
                       stop=(c == NCH - 1), sem=(sT if c == NCH - 1 else None))
            p.wait("vector", sT, sT.v)
            v = p.emit("vector", "tensor_copy", stg[:, :], ps[:, :], sem=sV)
            lastE[pi] = v
            p.wait("scalar", sV, v)
            ow = p.dma("scalar", qkv_d[oc * 128:(oc + 1) * 128, t0:t0 + TT], stg[:, :], sem=sDo)
            p.wait("vector", sDo, ow)
        ws.free((sT.h, sT.v))
        if oc + 2 < 12:
            lds[oc + 2] = ws.load(oc + 2)
    p.wait("scalar", sDo, sDo.v)
    p.barrier()



def build_l1_attn_A2(nt=1024):
    p = Prog("l1atA2")
    qp_d = p.din("qp", [1024, nt], BF16)
    qph_d = p.din("qph", [1024, 4], BF16)
    kp_d = p.din("kp", [256, nt], BF16)
    kph_d = p.din("kph", [256, 4], BF16)
    vc_d = p.din("vc", [128, nt], BF16)
    vp_d = p.din("vp", [128, nt], BF16)
    vph_d = p.din("vph", [128, 1], BF16)
    qc1blk_d = p.din("qc1blk", [24 * 128, 128], F32)
    kc1blk_d = p.din("kc1blk", [6 * 128, 128], F32)
    qc2T_d = p.din("qc2T", [1024, 3 * 1024], F32)
    kc2T_d = p.din("kc2T", [256, 3 * 256], F32)
    qc1b_d = p.din("qc1b", [128, 8], F32)
    kc1b_d = p.din("kc1b", [128, 2], F32)
    qc2b_d = p.din("qc2b", [128, 8], F32)
    kc2b_d = p.din("kc2b", [128, 2], F32)
    mselk_d = p.din("mselk", [256, 128], F32)
    mselq_d = p.din("mselq", [1024, 128], F32)
    hselq_d = p.din("hselq", [1024, 16], F32)
    hbcq_d = p.din("hbcq", [16, 1024], F32)
    hselk_d = p.din("hselk", [256, 4], F32)
    hbck_d = p.din("hbck", [4, 256], F32)
    kt_d = p.din("kt", [1, 1], F32)
    cos_d = p.din("cosd", [128, nt], F32)
    sin_d = p.din("sind", [128, nt], F32)
    qhat_d = p.dout("qhat", [1024, nt], BF16)
    khat_d = p.dout("khat", [256, nt], BF16)
    vT_d = p.dout("vT", [nt, 256], BF16)
    cm = prewarm(p)
    emit_l1_attn_A2(p, cm, dict(
        qp=qp_d, qph=qph_d, kp=kp_d, kph=kph_d, vc=vc_d, vp=vp_d, vph=vph_d,
        qc1blk=qc1blk_d, kc1blk=kc1blk_d, qc2T=qc2T_d, kc2T=kc2T_d,
        qc1b=qc1b_d, kc1b=kc1b_d, qc2b=qc2b_d, kc2b=kc2b_d,
        mselk=mselk_d, mselq=mselq_d, hselq=hselq_d, hbcq=hbcq_d,
        hselk=hselk_d, hbck=hbck_d, kt=kt_d, cosd=cos_d, sind=sin_d,
        qhat=qhat_d, khat=khat_d, vT=vT_d), nt)
    return p


def emit_l1_attn_A2(p, cm, t, nt=1024):
    (qp_d, qph_d, kp_d, kph_d, vc_d, vp_d, vph_d, qc1blk_d, kc1blk_d, qc2T_d,
     kc2T_d, qc1b_d, kc1b_d, qc2b_d, kc2b_d, mselk_d, mselq_d, hselq_d,
     hbcq_d, hselk_d, hbck_d, kt_d, cos_d, sin_d, qhat_d, khat_d, vT_d) = (
        t["qp"], t["qph"], t["kp"], t["kph"], t["vc"], t["vp"], t["vph"],
        t["qc1blk"], t["kc1blk"], t["qc2T"], t["kc2T"], t["qc1b"], t["kc1b"],
        t["qc2b"], t["kc2b"], t["mselk"], t["mselq"], t["hselq"], t["hbcq"],
        t["hselk"], t["hbck"], t["kt"], t["cosd"], t["sind"],
        t["qhat"], t["khat"], t["vT"])
    ntile = nt // TT
    EXT = 1032
    EY = 1028

    sW = p.sem("a2_w")
    casts = []
    qc1_f, qc1_b = load_const_weight(p, "qc1", qc1blk_d, 24, 128, sW, casts)
    kc1_f, kc1_b = load_const_weight(p, "kc1", kc1blk_d, 6, 128, sW, casts)
    mselk_f, mselk_b = load_const_weight(p, "mselk", mselk_d, 2, 128, sW, casts)
    mselq_f, mselq_b = load_const_weight(p, "mselq", mselq_d, 8, 128, sW, casts)
    hselq_f, hselq_b = load_const_weight(p, "hselq", hselq_d, 8, 16, sW, casts)
    hselk_f, hselk_b = load_const_weight(p, "hselk", hselk_d, 2, 4, sW, casts)
    hbcq_f = p.sbuf("hbcq_f", [16, 1024], F32)
    hbcq_b = p.sbuf("hbcq_b", [16, 1024], BF16)
    hbck_f = p.sbuf("hbck_f", [4, 256], F32)
    hbck_b = p.sbuf("hbck_b", [4, 256], BF16)
    biases = p.sbuf("biases", [128, 20], F32)
    ktv = p.sbuf("ktv", [1, 1], F32)
    cosb = p.sbuf("cosb", [128, nt], F32)
    sinb = p.sbuf("sinb", [128, nt], F32)
    p.dma("sync", hbcq_f[:, :], hbcq_d, sem=sW)
    p.dma("sync", hbck_f[:, :], hbck_d, sem=sW)
    p.dma("sync", biases[:, 0:8], qc1b_d, sem=sW)
    p.dma("sync", biases[:, 8:10], kc1b_d, sem=sW)
    p.dma("sync", biases[:, 10:18], qc2b_d, sem=sW)
    p.dma("sync", biases[:, 18:20], kc2b_d, sem=sW)
    p.dma("sync", ktv[:, :], kt_d, sem=sW)
    p.dma("sync", cosb[:, :], cos_d, sem=sW)
    p.dma("sync", sinb[:, :], sin_d, sem=sW)
    sG = p.sem("a2_g")
    p.wait("gpsimd", sW, sW.v)
    for f, bq, wv_ in casts:
        p.emit("gpsimd", "tensor_copy", bq[:, :], f[:, :], sem=sG)
    p.emit("gpsimd", "tensor_copy", hbcq_b[:, :], hbcq_f[:, :])
    p.emit("gpsimd", "tensor_copy", hbck_b[:, :], hbck_f[:, :], sem=sG)
    p.barrier()

    qpe = p.sbuf("qpe", [128, 8 * EXT], BF16)
    kpe = p.sbuf("kpe", [128, 2 * EXT], BF16)
    sD = p.sem("a2_d")
    for c in range(8):
        p.dma("sync", qpe[:, c * EXT:c * EXT + 4], qph_d[c * 128:(c + 1) * 128, :], sem=sD)
        p.dma("sync", qpe[:, c * EXT + 4:c * EXT + 4 + nt], qp_d[c * 128:(c + 1) * 128, :], sem=sD)
    for c in range(2):
        p.dma("sync", kpe[:, c * EXT:c * EXT + 4], kph_d[c * 128:(c + 1) * 128, :], sem=sD)
        w = p.dma("sync", kpe[:, c * EXT + 4:c * EXT + 4 + nt], kp_d[c * 128:(c + 1) * 128, :], sem=sD)
    p.wait("tensor", sD, sD.v)
    p.wait("vector", sD, sD.v)
    p.barrier()

    ps_m = p.psum("a2_pm", [128, TT])
    km = p.sbuf("km", [128, nt], F32)
    qm = p.sbuf("qm", [128, nt], F32)
    sT = p.sem("a2_t")
    sV = p.sem("a2_v")
    sA = p.sem("a2_a")
    for it in range(ntile):
        t0 = it * TT
        for c in range(2):
            p.emit("tensor", "matmul", ps_m[:, :], mselk_b[:, c * 128:(c + 1) * 128],
                   kpe[:, c * EXT + 4 + t0:c * EXT + 4 + t0 + TT], start=(c == 0),
                   stop=(c == 1), sem=(sT if c == 1 else None))
        p.wait("vector", sT, sT.v)
        v = p.emit("vector", "tensor_copy", km[:, t0:t0 + TT], ps_m[:, :], sem=sV)
        p.wait("tensor", sV, v)
        for c in range(8):
            p.emit("tensor", "matmul", ps_m[:, :], mselq_b[:, c * 128:(c + 1) * 128],
                   qpe[:, c * EXT + 4 + t0:c * EXT + 4 + t0 + TT], start=(c == 0),
                   stop=(c == 7), sem=(sT if c == 7 else None))
        p.wait("vector", sT, sT.v)
        v = p.emit("vector", "tensor_copy", qm[:, t0:t0 + TT], ps_m[:, :], sem=sV)
        p.wait("tensor", sV, v)
    p.barrier()

    y1q = p.sbuf("y1q", [128, 8 * EY], BF16)
    y1k = p.sbuf("y1k", [128, 2 * EY], BF16)
    ps_c = p.psum("a2_pc", [128, TT])
    ps_c2 = p.psum("a2_pc2", [128, 4])
    for (nch, xe, y1, blk, boff) in ((8, qpe, y1q, qc1_b, 0), (2, kpe, y1k, kc1_b, 8)):
        for c in range(nch):
            for d in range(3):
                p.emit("tensor", "matmul", ps_c2[:, 0:2],
                       blk[:, (d * nch + c) * 128:(d * nch + c + 1) * 128],
                       xe[:, c * EXT + d:c * EXT + d + 2], start=(d == 0), stop=(d == 2),
                       sem=(sT if d == 2 else None))
            p.wait("vector", sT, sT.v)
            v = p.emit("vector", "tensor_scalar", y1[:, c * EY:c * EY + 2], ps_c2[:, 0:2],
                       biases[:, boff + c:boff + c + 1], None, ALU.add, ALU.bypass, sem=sV)
            p.wait("tensor", sV, v)
            for it in range(ntile):
                t0 = it * TT
                for d in range(3):
                    p.emit("tensor", "matmul", ps_c[:, :],
                           blk[:, (d * nch + c) * 128:(d * nch + c + 1) * 128],
                           xe[:, c * EXT + t0 + d + 2:c * EXT + t0 + d + 2 + TT],
                           start=(d == 0), stop=(d == 2), sem=(sT if d == 2 else None))
                p.wait("vector", sT, sT.v)
                v = p.emit("vector", "tensor_scalar", y1[:, c * EY + 2 + t0:c * EY + 2 + t0 + TT],
                           ps_c[:, :], biases[:, boff + c:boff + c + 1], None,
                           ALU.add, ALU.bypass, sem=sV)
                p.wait("tensor", sV, v)
    p.barrier()

    qeff = p.sbuf("qeff", [128, 8 * nt], F32)
    keff = p.sbuf("keff", [128, 2 * nt], F32)
    tl2 = p.sbuf("a2_tl", [128, TT], F32)
    wsq2 = WStream(p, "qc2", qc2T_d, 24)
    lds = {0: wsq2.load(0), 1: wsq2.load(1)}
    for oc in range(8):
        b1, g1 = lds[oc]
        p.wait("tensor", wsq2.sG, g1)
        for it in range(ntile):
            t0 = it * TT
            for d in range(3):
                for kc in range(8):
                    i = d * 8 + kc
                    p.emit("tensor", "matmul", ps_c[:, :],
                           wsq2.wb[b1][:, i * 128:(i + 1) * 128],
                           y1q[:, kc * EY + t0 + d:kc * EY + t0 + d + TT],
                           start=(i == 0), stop=(i == 23), sem=(sT if i == 23 else None))
            p.wait("vector", sT, sT.v)
            p.emit("vector", "tensor_scalar", tl2[:, :], ps_c[:, :],
                   biases[:, 10 + oc:11 + oc], None, ALU.add, ALU.bypass)
            v = p.emit("vector", "tensor_add", tl2[:, :], tl2[:, :], km[:, t0:t0 + TT])
            v = p.emit("vector", "tensor_copy", qeff[:, oc * nt + t0:oc * nt + t0 + TT],
                       tl2[:, :], sem=sV)
            p.wait("tensor", sV, v)
        wsq2.free((sT.h, sT.v))
        if oc + 2 < 8:
            lds[oc + 2] = wsq2.load(oc + 2)
    wsq2.done()
    wsk2 = WStream(p, "kc2", kc2T_d, 6)
    lds = {0: wsk2.load(0), 1: wsk2.load(1)}
    for oc in range(2):
        b1, g1 = lds[oc]
        p.wait("tensor", wsk2.sG, g1)
        for it in range(ntile):
            t0 = it * TT
            for d in range(3):
                for kc in range(2):
                    i = d * 2 + kc
                    p.emit("tensor", "matmul", ps_c[:, :],
                           wsk2.wb[b1][:, i * 128:(i + 1) * 128],
                           y1k[:, kc * EY + t0 + d:kc * EY + t0 + d + TT],
                           start=(i == 0), stop=(i == 5), sem=(sT if i == 5 else None))
            p.wait("vector", sT, sT.v)
            p.emit("vector", "tensor_scalar", tl2[:, :], ps_c[:, :],
                   biases[:, 18 + oc:19 + oc], None, ALU.add, ALU.bypass)
            p.emit("vector", "tensor_add", tl2[:, :], tl2[:, :], qm[:, t0:t0 + TT])
            v = p.emit("vector", "tensor_copy", keff[:, oc * nt + t0:oc * nt + t0 + TT],
                       tl2[:, :], sem=sV)
            p.wait("tensor", sV, v)
        wsk2.free((sT.h, sT.v))
    wsk2.done()
    p.barrier()

    sq2 = p.sbuf("a2_sq", [128, TT], BF16)
    ps_n = p.psum("a2_pn", [16, TT])
    ps_e = p.psum("a2_pe", [128, TT])
    rq = p.sbuf("rq", [16, nt], F32)
    rqb = p.sbuf("rqb", [16, nt], BF16)
    rk = p.sbuf("rk", [4, nt], F32)
    rkb = p.sbuf("rkb", [4, nt], BF16)
    ktc = p.sbuf("ktc", [4, 1], F32)
    ktb = p.sbuf("ktb", [1, 1], BF16)
    ps_kt = p.psum("a2_pkt", [4, 1])
    t1 = p.sbuf("a2_t1", [64, TT], F32)
    t2 = p.sbuf("a2_t2", [64, TT], F32)
    stg = p.sbuf("a2_stg", [128, TT], BF16)
    sDo = p.sem("a2_do")
    eps12 = p.constf(1e-12)
    v = p.emit("vector", "tensor_copy", ktb[:, :], ktv[:, :], sem=sV)
    p.wait("tensor", sV, v)
    v = p.emit("tensor", "matmul", ps_kt[:, :], cm.ones1_bf[0:1, 0:4], ktb[0:1, 0:1],
               start=True, stop=True, sem=sT)
    p.wait("vector", sT, v)
    v = p.emit("vector", "tensor_copy", ktc[:, :], ps_kt[:, :], sem=sV)

    for (nch, eff, rr, rrb, hsel, hbc, nhh, outd) in (
            (8, qeff, rq, rqb, hselq_b, hbcq_b, 16, qhat_d),
            (2, keff, rk, rkb, hselk_b, hbck_b, 4, khat_d)):
        for it in range(ntile):
            t0 = it * TT
            for c in range(nch):
                if c > 0:
                    p.wait("scalar", sT, sT.v)
                a = p.emit("scalar", "activation", sq2[:, :],
                           eff[:, c * nt + t0:c * nt + t0 + TT], AF.Square, sem=sA)
                p.wait("tensor", sA, a)
                p.emit("tensor", "matmul", ps_n[0:nhh, :],
                       hsel[:, c * nhh:(c + 1) * nhh], sq2[:, :],
                       start=(c == 0), stop=(c == nch - 1), sem=sT)
            p.wait("vector", sT, sT.v)
            p.emit("vector", "tensor_copy", rr[0:nhh, t0:t0 + TT], ps_n[0:nhh, :], sem=sV)
            p.wait("scalar", sV, sV.v)
            a = p.emit("scalar", "activation", rr[0:nhh, t0:t0 + TT], rr[0:nhh, t0:t0 + TT],
                       AF.Sqrt, sem=sA)
            p.wait("vector", sA, a)
            p.emit("vector", "tensor_scalar", rr[0:nhh, t0:t0 + TT], rr[0:nhh, t0:t0 + TT],
                   eps12[0:nhh, :], None, ALU.max, ALU.bypass)
            p.emit("vector", "reciprocal", rr[0:nhh, t0:t0 + TT], rr[0:nhh, t0:t0 + TT])
            if rr is rk:
                p.emit("vector", "tensor_scalar", rr[0:nhh, t0:t0 + TT], rr[0:nhh, t0:t0 + TT],
                       ktc[0:4, :], None, ALU.mult, ALU.bypass)
            v = p.emit("vector", "tensor_copy", rrb[0:nhh, t0:t0 + TT], rr[0:nhh, t0:t0 + TT], sem=sV)
            p.wait("tensor", sV, v)
            for c in range(nch):
                v = p.emit("tensor", "matmul", ps_e[:, :], hbc[0:nhh, c * 128:(c + 1) * 128],
                           rrb[0:nhh, t0:t0 + TT], start=True, stop=True, sem=sT)
                p.wait("vector", sT, v)
                v = p.emit("vector", "tensor_mul", stg[:, :],
                           eff[:, c * nt + t0:c * nt + t0 + TT], ps_e[:, :], sem=sV)
                cse = cosb[0:64, t0:t0 + TT]
                cso = cosb[64:128, t0:t0 + TT]
                sne = sinb[0:64, t0:t0 + TT]
                sno = sinb[64:128, t0:t0 + TT]
                p.emit("vector", "tensor_mul", t1[:, :], stg[0:64, :], cse)
                p.emit("vector", "tensor_mul", t2[:, :], stg[64:128, :], sno)
                p.emit("vector", "tensor_sub", t1[:, :], t1[:, :], t2[:, :])
                p.emit("vector", "tensor_mul", t2[:, :], stg[0:64, :], sne)
                v2 = p.emit("vector", "tensor_copy", stg[0:64, :], t1[:, :], sem=sV)
                p.emit("vector", "tensor_mul", t1[:, :], stg[64:128, :], cso)
                p.emit("vector", "tensor_add", t1[:, :], t1[:, :], t2[:, :])
                v2 = p.emit("vector", "tensor_copy", stg[64:128, :], t1[:, :], sem=sV)
                p.wait("scalar", sV, v2)
                ow = p.dma("scalar", outd[c * 128:(c + 1) * 128, t0:t0 + TT], stg[:, :], sem=sDo)
                p.wait("vector", sDo, ow)
                p.wait("tensor", sDo, ow)
    p.barrier()

    vv = p.sbuf("a2_vv", [128, nt], BF16)
    vcs = p.sbuf("a2_vc", [128, nt], BF16)
    vtt = p.sbuf("a2_vtt", [128, 128], BF16)
    ps_tr = p.psum("a2_ptr", [128, 128], BF16)
    sD2 = p.sem("a2_d2")
    p.dma("sync", vcs[:, :], vc_d, sem=sD2)
    p.dma("sync", vv[:, 0:1], vph_d, sem=sD2, allow_slow_non_contiguous=True)
    w = p.dma("sync", vv[:, 1:nt], vp_d[:, 0:nt - 1], sem=sD2)
    p.wait("tensor", sD2, w)
    lastVV = [0]
    for ci, vsrc in ((0, vcs), (1, vv)):
        for g in range(nt // 128):
            p.wait("tensor", sV, lastVV[0])
            tv = p.emit("tensor", "transpose", ps_tr[:, :], vsrc[:, g * 128:(g + 1) * 128],
                        cm.ident_bf[:, :], sem=sT)
            p.wait("vector", sT, tv)
            v = p.emit("vector", "tensor_copy", vtt[:, :], ps_tr[:, :], sem=sV)
            lastVV[0] = v
            p.wait("scalar", sV, v)
            ow = p.dma("scalar", vT_d[g * 128:(g + 1) * 128, ci * 128:(ci + 1) * 128],
                       vtt[:, :], sem=sDo)
            p.wait("vector", sDo, ow)
    p.wait("scalar", sDo, sDo.v)
    p.barrier()



def build_l2_ffn(nt=1024):
    p = Prog("l2ffn")
    x_d = p.din("x", [D, nt], F32)
    a2T = p.din("a2T", [D, 128], F32)
    b2T = p.din("b2T", [D, 128], F32)
    rtT = p.din("rtT", [D, 8], F32)
    xo_d = p.dout("xo", [D, nt], F32)
    cm = prewarm(p)
    emit_l2_ffn(p, cm, x_d, a2T, b2T, rtT, xo_d, nt)
    return p


def emit_l2_ffn(p, cm, x_d, a2T, b2T, rtT, xo_d, nt=1024):
    ntile = nt // TT
    xhat = p.sbuf("xhat", [128, NCH * nt], BF16)
    stats = p.sbuf("stats", [1, 4 * nt], F32)
    emit_ln(p, cm, x_d, nt, xhat, stats)

    sW = p.sem("f2_w")
    casts = []
    rt_f, rt_b = load_const_weight(p, "rt2", rtT, NCH, 8, sW, casts)
    a2_f, a2_b = load_const_weight(p, "a2w", a2T, NCH, 128, sW, casts)
    sG = p.sem("f2_g")
    p.wait("gpsimd", sW, sW.v)
    for f, bq, wv_ in casts:
        p.emit("gpsimd", "tensor_copy", bq[:, :], f[:, :], sem=sG)
    p.barrier()

    whb = p.sbuf("whb", [128, nt], BF16)
    selb2 = p.sbuf("selb2", [8, 128], BF16)
    rwf = p.sbuf("rwf", [8, nt], BF16)
    sI = p.sem("f2_i")
    v = p.emit("vector", "memset", selb2[:, :], 1.0, sem=sI)
    p.wait("gpsimd", sI, v)
    p.emit("gpsimd", "affine_select", selb2[:, :], selb2[:, :], [[1, 8], [0, 16]],
           ALU.is_equal, 0.0, base=0, channel_multiplier=-1, sem=sI)
    p.barrier()

    ps_r = p.psum("f2_pr", [8, TT])
    ps_h = p.psum("f2_ph", [128, TT])
    ps_tr = p.psum("f2_ptr", [128, 8])
    ps_g = p.psum("f2_pg", [8, 128])
    ps_ge = p.psum("f2_pge", [128, TT])
    lsb = p.sbuf("lsb2", [8, TT], F32)
    ltm = p.sbuf("ltm2", [128, 8], F32)
    rwtm = p.sbuf("rwtm", [128, 8], F32)
    mx = p.sbuf("mx2", [128, 1], F32)
    hb = p.sbuf("hb2", [128, nt], BF16)
    sT = p.sem("f2_t")
    sV = p.sem("f2_v")
    for it in range(ntile):
        t0 = it * TT
        for c in range(NCH):
            p.emit("tensor", "matmul", ps_r[:, :], rt_b[:, c * 8:(c + 1) * 8],
                   xhat[:, c * nt + t0:c * nt + t0 + TT], start=(c == 0), stop=(c == NCH - 1))
        for c in range(NCH):
            p.emit("tensor", "matmul", ps_h[:, :], a2_b[:, c * 128:(c + 1) * 128],
                   xhat[:, c * nt + t0:c * nt + t0 + TT], start=(c == 0), stop=(c == NCH - 1),
                   sem=sT)
        p.wait("vector", sT, sT.v)
        p.emit("vector", "tensor_copy", lsb[:, :], ps_r[:, :])
        v = p.emit("vector", "tensor_copy", hb[:, t0:t0 + TT], ps_h[:, :], sem=sV)
        p.wait("tensor", sV, v)
        p.barrier()
        for g in range(TT // 128):
            gt = t0 + g * 128
            v = p.emit("tensor", "transpose", ps_tr[:, :], lsb[0:8, g * 128:(g + 1) * 128],
                       cm.ident_f[0:8, 0:8], sem=sT)
            p.wait("vector", sT, v)
            v = p.emit("vector", "tensor_copy", ltm[:, :], ps_tr[:, 0:8], sem=sV)
            p.wait("vector", sV, v)
            p.emit("vector", "reduce_max", mx[:, :], ltm[:, :], axis=AX.X)
            v = p.emit("vector", "tensor_scalar", rwtm[:, :], ltm[:, :], mx[:, :], mx[:, :],
                       ALU.is_ge, ALU.mult, sem=sV)
            p.wait("tensor", sV, v)
            v = p.emit("tensor", "transpose", ps_g[:, :], rwtm[:, :], cm.ident_f[:, :], sem=sT)
            p.wait("vector", sT, v)
            v = p.emit("vector", "tensor_copy", rwf[:, gt:gt + 128], ps_g[0:8, :], sem=sV)
            p.wait("tensor", sV, v)
        p.barrier()
        v = p.emit("tensor", "matmul", ps_ge[:, :], selb2[:, :], rwf[:, t0:t0 + TT],
                   start=True, stop=True, sem=sT)
        p.wait("vector", sT, v)
        v = p.emit("vector", "tensor_mul", whb[:, t0:t0 + TT], hb[:, t0:t0 + TT],
                   ps_ge[:, :], sem=sV)
        p.wait("tensor", sV, v)
    p.barrier()
    emit_proj_resid(p, cm, b2T, whb, 1, nt, x_d, xo_d, 0.5, "f2")



def build_l2_attn_A(nt=1024):
    p = Prog("l2atA")
    x_d = p.din("x", [D, nt], F32)
    wqkvT = p.din("wqkvT", [6144, D], F32)
    cos_d = p.din("cosd", [64, nt], F32)
    sin_d = p.din("sind", [64, nt], F32)
    qF_d = p.dout("qF", [D, nt], BF16)
    kF_d = p.dout("kF", [1024, nt], BF16)
    vT_d = p.dout("vT", [nt, 1024], BF16)
    cm = prewarm(p)
    emit_l2_attn_A(p, cm, x_d, wqkvT, cos_d, sin_d, qF_d, kF_d, vT_d, nt)
    return p


def emit_l2_attn_A(p, cm, x_d, wqkvT, cos_d, sin_d, qF_d, kF_d, vT_d, nt=1024):
    ntile = nt // TT
    xhat = p.sbuf("xhat", [128, NCH * nt], BF16)
    stats = p.sbuf("stats", [1, 4 * nt], F32)
    emit_ln(p, cm, x_d, nt, xhat, stats)

    cosb = p.sbuf("cosb", [64, nt], F32)
    sinb = p.sbuf("sinb", [64, nt], F32)
    sC = p.sem("a2l_c")
    p.dma("sync", cosb[:, :], cos_d, sem=sC)
    w = p.dma("sync", sinb[:, :], sin_d, sem=sC)
    p.wait("vector", sC, w)
    p.wait("scalar", sC, w)

    sT = p.sem("a2l_t")
    sV = p.sem("a2l_v")
    sDo = p.sem("a2l_do")
    t1 = p.sbuf("ro2_t1", [64, TT], F32)
    t2 = p.sbuf("ro2_t2", [64, TT], F32)
    stg = p.sbuf("ro2_stg", [128, TT], BF16)
    vstg = p.sbuf("v2_stg", [128, TT], BF16)
    vtt = p.sbuf("v2_tt", [128, 128], BF16)
    ps_y = [p.psum("a2l_ps", [128, TT]) for _ in range(2)]
    ps_tr = p.psum("a2l_ptr", [128, 128], BF16)
    lastE = [0, 0]
    lastVV = [0]

    ws = WStream(p, "qkv2", wqkvT, NCH)
    lds = {0: ws.load(0), 1: ws.load(1)}
    for oc in range(48):
        b1, g1 = lds[oc]
        p.wait("tensor", ws.sG, g1)
        for it in range(ntile):
            t0 = it * TT
            pi = (oc * ntile + it) % 2
            ps = ps_y[pi]
            p.wait("tensor", sV, lastE[pi])
            for c in range(NCH):
                p.emit("tensor", "matmul", ps[:, :], ws.wb[b1][:, c * 128:(c + 1) * 128],
                       xhat[:, c * nt + t0:c * nt + t0 + TT], start=(c == 0),
                       stop=(c == NCH - 1), sem=(sT if c == NCH - 1 else None))
            p.wait("vector", sT, sT.v)
            if oc < 40:
                cs = cosb[:, t0:t0 + TT]
                sn = sinb[:, t0:t0 + TT]
                pse = ps[0:64, :]
                pso = ps[64:128, :]
                p.emit("vector", "tensor_mul", t1[:, :], pse, cs)
                p.emit("vector", "tensor_mul", t2[:, :], pso, sn)
                p.emit("vector", "tensor_sub", stg[0:64, :], t1[:, :], t2[:, :])
                p.emit("vector", "tensor_mul", t1[:, :], pse, sn)
                p.emit("vector", "tensor_mul", t2[:, :], pso, cs)
                v = p.emit("vector", "tensor_add", stg[64:128, :], t1[:, :], t2[:, :], sem=sV)
                lastE[pi] = v
                p.wait("scalar", sV, v)
                if oc < 32:
                    ow = p.dma("scalar", qF_d[oc * 128:(oc + 1) * 128, t0:t0 + TT],
                               stg[:, :], sem=sDo)
                else:
                    ow = p.dma("scalar", kF_d[(oc - 32) * 128:(oc - 31) * 128, t0:t0 + TT],
                               stg[:, :], sem=sDo)
                p.wait("vector", sDo, ow)
            else:
                v = p.emit("vector", "tensor_copy", vstg[:, :], ps[:, :], sem=sV)
                lastE[pi] = v
                for g in range(TT // 128):
                    p.wait("tensor", sV, v)
                    p.wait("tensor", sV, lastVV[0])
                    tv = p.emit("tensor", "transpose", ps_tr[:, :],
                                vstg[:, g * 128:(g + 1) * 128], cm.ident_bf[:, :], sem=sT)
                    p.wait("vector", sT, tv)
                    vv = p.emit("vector", "tensor_copy", vtt[:, :], ps_tr[:, :], sem=sV)
                    lastVV[0] = vv
                    p.wait("scalar", sV, vv)
                    ow = p.dma("scalar", vT_d[t0 + g * 128:t0 + (g + 1) * 128,
                                              (oc - 40) * 128:(oc - 39) * 128], vtt[:, :], sem=sDo)
                    p.wait("vector", sDo, ow)
        ws.free((sT.h, sT.v))
        if oc + 2 < 48:
            lds[oc + 2] = ws.load(oc + 2)
    p.wait("scalar", sDo, sDo.v)
    p.barrier()



def build_l3_attn_A(nt=1024):
    p = Prog("l3atA")
    x_d = p.din("x", [D, nt], F32)
    wqT = p.din("wqT", [D, D], F32)
    wkvT = p.din("wkvT", [256, D], F32)
    qF_d = p.dout("qF", [D, nt], BF16)
    kF_d = p.dout("kF", [128, nt], BF16)
    vT_d = p.dout("vT", [nt, 128], BF16)
    cm = prewarm(p)
    emit_l3_attn_A(p, cm, x_d, wqT, wkvT, qF_d, kF_d, vT_d, nt)
    return p


def emit_l3_attn_A(p, cm, x_d, wqT, wkvT, qF_d, kF_d, vT_d, nt=1024):
    ntile = nt // TT
    xhat = p.sbuf("xhat", [128, NCH * nt], BF16)
    stats = p.sbuf("stats", [1, 4 * nt], F32)
    emit_ln(p, cm, x_d, nt, xhat, stats)

    sT = p.sem("a3_t")
    sV = p.sem("a3_v")
    sDo = p.sem("a3_do")
    stg = p.sbuf("a3_stg", [128, TT], BF16)
    vtt = p.sbuf("a3_vtt", [128, 128], BF16)
    ps_y = [p.psum("a3_ps", [128, TT]) for _ in range(2)]
    ps_tr = p.psum("a3_ptr", [128, 128], BF16)
    lastE = [0, 0]
    lastVV = [0]

    ws = WStream(p, "wq3", wqT, NCH)
    wskv = WStream(p, "wkv3", wkvT, NCH)

    def _ld(oc):
        return ws.load(oc) if oc < 32 else wskv.load(oc - 32)
    lds = {0: _ld(0), 1: _ld(1)}
    for oc in range(34):
        st = ws if oc < 32 else wskv
        b1, g1 = lds[oc]
        p.wait("tensor", st.sG, g1)
        wb = st.wb[b1]
        for it in range(ntile):
            t0 = it * TT
            pi = (oc * ntile + it) % 2
            ps = ps_y[pi]
            p.wait("tensor", sV, lastE[pi])
            for c in range(NCH):
                p.emit("tensor", "matmul", ps[:, :], wb[:, c * 128:(c + 1) * 128],
                       xhat[:, c * nt + t0:c * nt + t0 + TT], start=(c == 0),
                       stop=(c == NCH - 1), sem=(sT if c == NCH - 1 else None))
            p.wait("vector", sT, sT.v)
            if oc < 33:
                v = p.emit("vector", "tensor_copy", stg[:, :], ps[:, :], sem=sV)
                lastE[pi] = v
                p.wait("scalar", sV, v)
                dst = qF_d[oc * 128:(oc + 1) * 128, t0:t0 + TT] if oc < 32 else kF_d[:, t0:t0 + TT]
                ow = p.dma("scalar", dst, stg[:, :], sem=sDo)
                p.wait("vector", sDo, ow)
            else:
                v = p.emit("vector", "tensor_copy", stg[:, :], ps[:, :], sem=sV)
                lastE[pi] = v
                for g in range(TT // 128):
                    p.wait("tensor", sV, v)
                    p.wait("tensor", sV, lastVV[0])
                    tv = p.emit("tensor", "transpose", ps_tr[:, :],
                                stg[:, g * 128:(g + 1) * 128], cm.ident_bf[:, :], sem=sT)
                    p.wait("vector", sT, tv)
                    vv = p.emit("vector", "tensor_copy", vtt[:, :], ps_tr[:, :], sem=sV)
                    lastVV[0] = vv
                    p.wait("scalar", sV, vv)
                    ow = p.dma("scalar", vT_d[t0 + g * 128:t0 + (g + 1) * 128, :],
                               vtt[:, :], sem=sDo)
                    p.wait("vector", sDo, ow)
        st.free((sT.h, sT.v))
        if oc + 2 < 34:
            lds[oc + 2] = _ld(oc + 2)
    p.wait("scalar", sDo, sDo.v)
    p.barrier()



def build_l3_dequant(nrow=512):
    p = Prog("l3dq")
    qw_d = p.din("qw", [8 * nrow, 128], I32)
    lut4_d = p.din("lut4", [nrow, 16], F32)
    lut8_d = p.din("lut8", [nrow, 256], F32)
    wlT_d = p.dout("wlT", [D, nrow], BF16)
    whT_d = p.dout("whT", [D, nrow], BF16)
    cm = prewarm(p)
    emit_l3_dequant(p, cm, qw_d, lut4_d, lut8_d, wlT_d, whT_d, nrow)
    return p


def emit_l3_dequant(p, cm, qw_d, lut4_d, lut8_d, wlT_d, whT_d, nrow=512):
    nchk = nrow // 128
    qwp = p.sbuf("qwp", [128, 8 * 128], I32)
    idx = p.sbuf("idx", [128, D], I32)
    tmpb = p.sbuf("dq_tmpb", [128, 32 * 128], I32)
    idxb = p.sbuf("idxb", [128, D], BF16)
    acc = p.sbuf("dq_acc", [128, D], BF16)
    cands = [p.sbuf("dq_c", [128, D], BF16) for _ in range(6)]
    lut8s = p.sbuf("lut8s", [128, 256], F32)
    lut4s = p.sbuf("lut4s", [128, 16], F32)
    trr = p.sbuf("dq_tr", [128, 128], BF16)
    sD = p.sem("dq_d")
    sT = p.sem("dq_t")
    sV = p.sem("dq_v")
    sG = p.sem("dq_g")
    sA = p.sem("dq_a")
    sDo = p.sem("dq_do")
    sVX = p.sem("dq_vx")
    sGX = p.sem("dq_gx")

    for ck in range(nchk):
        p.barrier()

        w = None
        for pl in range(8):
            w = p.dma("sync", qwp[:, pl * 128:(pl + 1) * 128],
                      qw_d[pl * nrow + ck * 128:pl * nrow + (ck + 1) * 128, :], sem=sD)
        p.dma("sync", lut8s[:, :], lut8_d[ck * 128:(ck + 1) * 128, :], sem=sD)
        w = p.dma("sync", lut4s[:, :], lut4_d[ck * 128:(ck + 1) * 128, :], sem=sD)
        p.wait("vector", sD, sD.v)
        sR = p.sem("dq_r")
        for pl in range(8):
            for s in range(32):
                sh = s - pl
                op0 = ALU.logical_shift_right if sh >= 0 else ALU.logical_shift_left
                p.emit("vector", "tensor_scalar", tmpb[:, s * 128:(s + 1) * 128],
                       qwp[:, pl * 128:(pl + 1) * 128], abs(sh), 1 << pl, op0,
                       ALU.bitwise_and, nochain=True,
                       sem=(sR if s == 31 else None))
            p.wait("vector", sR, sR.v)
            for s in range(32):
                if pl == 0:
                    p.emit("vector", "tensor_copy", idx[:, s::32],
                           tmpb[:, s * 128:(s + 1) * 128], nochain=True,
                           sem=(sR if s == 31 else None))
                else:
                    p.emit("vector", "tensor_tensor", idx[:, s::32], idx[:, s::32],
                           tmpb[:, s * 128:(s + 1) * 128], ALU.bitwise_or, nochain=True,
                           sem=(sR if s == 31 else None))
            p.wait("vector", sR, sR.v)
        v = p.emit("vector", "tensor_copy", idxb[:, :], idx[:, :], sem=sV)
        for ti, (ncode, luts, outd) in enumerate(((256, lut8s, whT_d), (16, lut4s, wlT_d))):
            if ti == 1:
                p.emit("vector", "tensor_single_scalar", idx[:, :], idx[:, :], 15,
                       ALU.bitwise_and)
                v = p.emit("vector", "tensor_copy", idxb[:, :], idx[:, :], sem=sV)
            p.wait("vector", sV, v)
            ps_acc = p.psum(f"dq_pa", [128, D])
            sP = p.sem("dq_p")
            sPE = p.sem("dq_pe")
            baseP = sP.v
            basePE = sPE.v
            ncd = len(cands)

            def ts_op(c):
                if c >= ncd:
                    p.wait("vector", sPE, basePE + (c - ncd) + 1)
                p.emit("vector", "tensor_scalar", cands[c % ncd][:, :], idxb[:, :],
                       float(c), luts[:, c:c + 1], ALU.is_equal, ALU.mult,
                       nochain=True, sem=sP)

            for c in range(ncode):
                ts_op(c)
                p.wait("tensor", sP, baseP + c + 1)
                for t in range(8):
                    p.emit("tensor", "matmul", ps_acc[:, t * 512:(t + 1) * 512],
                           cm.ident_bf[:, :], cands[c % ncd][:, t * 512:(t + 1) * 512],
                           start=(c == 0), stop=(c == ncode - 1),
                           sem=(sPE if t == 7 else None))
            p.wait("vector", sPE, basePE + ncode)
            for t in range(8):
                p.emit("vector", "tensor_copy", acc[:, t * 512:(t + 1) * 512],
                       ps_acc[:, t * 512:(t + 1) * 512],
                       sem=(sV if t == 7 else None))
            v = sV.v
            p.release(ps_acc)
            ps_tr = p.psum("dq_ptr", [128, 128], BF16)
            p.wait("tensor", sV, v)
            lastVV = 0
            for g in range(D // 128):
                p.wait("tensor", sVX, lastVV)
                tv = p.emit("tensor", "transpose", ps_tr[:, :], acc[:, g * 128:(g + 1) * 128],
                            cm.ident_bf[:, :], sem=sT)
                p.wait("vector", sT, tv)
                vv = p.emit("vector", "tensor_copy", trr[:, :], ps_tr[:, :], sem=sVX)
                lastVV = vv
                p.wait("scalar", sVX, vv)
                ow = p.dma("scalar", outd[g * 128:(g + 1) * 128, ck * 128:(ck + 1) * 128],
                           trr[:, :], sem=sDo)
                p.wait("vector", sDo, ow)
            p.wait("tensor", sVX, lastVV)
            p.release(ps_tr)
    p.wait("scalar", sDo, sDo.v)
    p.barrier()



def build_l3_ffn(nt=1024):
    p = Prog("l3ffn")
    x_d = p.din("x", [D, nt], F32)
    wlT_d = p.din("wlT", [D, D], BF16)
    whT_d = p.din("whT", [D, D], BF16)
    bias_d = p.din("bias2", [128, NCH], F32)
    xo_d = p.dout("xo", [D, nt], F32)
    cm = prewarm(p)
    emit_l3_ffn(p, cm, x_d, wlT_d, whT_d, bias_d, xo_d, nt)
    return p


def emit_l3_ffn(p, cm, x_d, wlT_d, whT_d, bias_d, xo_d, nt=1024):
    ntile = nt // TT
    xhat = p.sbuf("xhat", [128, NCH * nt], BF16)
    stats = p.sbuf("stats", [1, 4 * nt], F32)
    emit_ln(p, cm, x_d, nt, xhat, stats)

    mb = p.sbuf("mb", [128, nt], BF16)
    m01 = p.sbuf("m01", [1, nt], BF16)
    bb_f = p.sbuf("bb3", [128, NCH], F32)
    ps_b = p.psum("f3_pb", [128, TT])
    sW = p.sem("f3_w")
    sT = p.sem("f3_t")
    sV = p.sem("f3_v")
    w = p.dma("sync", bb_f[:, :], bias_d, sem=sW)
    p.wait("vector", sW, w)
    p.emit("vector", "tensor_scalar_mul", bb_f[:, :], bb_f[:, :], 0.5)
    for it in range(ntile):
        t0 = it * TT
        e_sl = stats[0:1, 2 * nt + t0:2 * nt + t0 + TT]
        v = p.emit("vector", "tensor_scalar", m01[0:1, t0:t0 + TT], e_sl, 64.0, None,
                   ALU.is_gt, ALU.bypass, sem=sV)
        p.wait("tensor", sV, v)
        v = p.emit("tensor", "matmul", ps_b[:, :], cm.ones1_bf[:, :], m01[0:1, t0:t0 + TT],
                   start=True, stop=True, sem=sT)
        p.wait("vector", sT, v)
        v = p.emit("vector", "tensor_copy", mb[:, t0:t0 + TT], ps_b[:, :], sem=sV)
    p.barrier()

    wsl = WStream(p, "wl3", wlT_d, NCH, src_bf16=True)
    wsh = WStream(p, "wh3", whT_d, NCH, src_bf16=True)
    ps_l = [p.psum("f3_pl", [128, TT]) for _ in range(2)]
    ps_h = [p.psum("f3_ph", [128, TT]) for _ in range(2)]
    tl = p.sbuf("f3_tl", [128, TT], F32)
    th = p.sbuf("f3_th", [128, TT], F32)
    sel = p.sbuf("f3_sel", [128, TT], F32)
    xstage = p.sbuf("f3_xs", [128, TT], F32)
    sDi = p.sem("f3_di")
    sDo = p.sem("f3_do")
    sV2 = p.sem("f3_v2")
    sT2 = p.sem("f3_t2")
    lastE = [0, 0]
    lds = {0: (wsl.load(0), wsh.load(0)), 1: (wsl.load(1), wsh.load(1))}
    for oc in range(NCH):
        (bl, gl), (bh, gh_) = lds[oc]
        p.wait("tensor", gl[0], gl[1])
        p.wait("tensor", gh_[0], gh_[1])
        for it in range(ntile):
            t0 = it * TT
            pi = (oc * ntile + it) % 2
            p.wait("tensor", sV2, lastE[pi])
            for c in range(NCH):
                p.emit("tensor", "matmul", ps_l[pi][:, :], wsl.wb[bl][:, c * 128:(c + 1) * 128],
                       xhat[:, c * nt + t0:c * nt + t0 + TT], start=(c == 0),
                       stop=(c == NCH - 1))
            for c in range(NCH):
                p.emit("tensor", "matmul", ps_h[pi][:, :], wsh.wb[bh][:, c * 128:(c + 1) * 128],
                       xhat[:, c * nt + t0:c * nt + t0 + TT], start=(c == 0),
                       stop=(c == NCH - 1), sem=(sT2 if c == NCH - 1 else None))
            p.wait("vector", sT2, sT2.v)
            w = p.dma("sync", xstage[:, :], x_d[oc * 128:(oc + 1) * 128, t0:t0 + TT], sem=sDi)
            p.emit("vector", "tensor_scalar", tl[:, :], ps_l[pi][:, :], 0.5,
                   bb_f[:, oc:oc + 1], ALU.mult, ALU.add)
            p.emit("vector", "tensor_scalar", th[:, :], ps_h[pi][:, :], 0.5,
                   bb_f[:, oc:oc + 1], ALU.mult, ALU.add)
            p.emit("vector", "tensor_sub", th[:, :], th[:, :], tl[:, :])
            p.emit("vector", "tensor_mul", th[:, :], th[:, :], mb[:, t0:t0 + TT])
            p.emit("vector", "tensor_add", sel[:, :], tl[:, :], th[:, :])
            p.wait("vector", sDi, w)
            v = p.emit("vector", "tensor_add", xstage[:, :], xstage[:, :], sel[:, :], sem=sV2)
            lastE[pi] = v
            p.wait("scalar", sV2, v)
            ow = p.dma("scalar", xo_d[oc * 128:(oc + 1) * 128, t0:t0 + TT], xstage[:, :], sem=sDo)
            p.wait("sync", sDo, ow)
        wsl.free((sT2.h, sT2.v))
        wsh.free((sT2.h, sT2.v))
        if oc + 2 < NCH:
            lds[oc + 2] = (wsl.load(oc + 2), wsh.load(oc + 2))
    p.wait("scalar", sDo, sDo.v)
    p.barrier()


# version 12
# speedup vs baseline: 1.0506x; 1.0506x over previous
import math
from contextlib import ExitStack

import numpy as np
import ml_dtypes

import concourse.bass as bass
import concourse.mybir as mybir
from concourse.bass_utils import run_bass_kernel_spmd

F32 = mybir.dt.float32
BF16 = mybir.dt.bfloat16
I32 = mybir.dt.int32
AF = mybir.ActivationFunctionType
ALU = mybir.AluOpType
AX = mybir.AxisListType
BF = ml_dtypes.bfloat16

D = 4096
NB = 4
S = 2048
TOK = NB * S
TT = 512
NCH = D // 128
NCORE = 8

ENGINES = ("sync", "vector", "scalar", "tensor", "gpsimd")
COMPUTE = ("vector", "scalar", "tensor", "gpsimd")


class Sem:
    __slots__ = ("h", "v")

    def __init__(self, handle):
        self.h = handle
        self.v = 0

    def inc(self, n=1):
        self.v += n
        return self.v


class Prog:
    def __init__(self, name):
        self.name = name
        self.nc = bass.Bass("TRN2", target_bir_lowering=False)
        self.es = ExitStack()
        self.es.__enter__()
        self.ops = {e: [] for e in ENGINES}
        self._uid = 0
        self.sems = {}
        self._open_cms = {}
        self._alloc_seq = {}
        self._sem_free = []
        self._sem_count = 0
        self._consts_frozen = False

    def uid(self):
        self._uid += 1
        return self._uid

    def din(self, name, shape, dtype):
        return self.nc.dram_tensor(name, list(shape), dtype, kind="ExternalInput").ap()

    def dout(self, name, shape, dtype):
        return self.nc.dram_tensor(name, list(shape), dtype, kind="ExternalOutput").ap()

    def dtmp(self, name, shape, dtype):
        return self.nc.dram_tensor(name, list(shape), dtype, kind="Internal").ap()

    def sbuf(self, name, shape, dtype):
        return self.sbuf_rel(name, shape, dtype)

    def psum(self, name, shape, dtype=F32):
        cm = self.nc.psum_tensor(f"{name}_{self.uid()}", list(shape), dtype)
        h = cm.__enter__()
        self._open_cms[id(h)] = cm
        self._alloc_seq[id(h)] = self._uid
        return h

    def sbuf_rel(self, name, shape, dtype):
        cm = self.nc.sbuf_tensor(f"{name}_{self.uid()}", list(shape), dtype)
        h = cm.__enter__()
        self._open_cms[id(h)] = cm
        self._alloc_seq[id(h)] = self._uid
        return h

    def release(self, *handles):
        hs = sorted(handles, key=lambda h: self._alloc_seq[id(h)], reverse=True)
        for h in hs:
            cm = self._open_cms.pop(id(h))
            self._alloc_seq.pop(id(h))
            cm.__exit__(None, None, None)

    def sem(self, name):
        if name not in self.sems:
            if self._sem_free:
                self.sems[name] = self._sem_free.pop()
            else:
                self._sem_count += 1
                self.sems[name] = Sem(self.es.enter_context(
                    self.nc.semaphore(f"sm{self._sem_count}")))
        return self.sems[name]

    def reset_sems(self, keep=()):
        for n in list(self.sems):
            if n.startswith("_ch_") or n.startswith("_bar_") or n in keep:
                continue
            self._sem_free.append(self.sems.pop(n))

    def scope_begin(self):
        return self._uid

    def scope_end(self, mark):
        hids = [hid for hid, sq in self._alloc_seq.items() if sq > mark]
        hids.sort(key=lambda hid: self._alloc_seq[hid], reverse=True)
        for hid in hids:
            cm = self._open_cms.pop(hid)
            self._alloc_seq.pop(hid)
            cm.__exit__(None, None, None)

    def sublayer_end(self, mark):
        self.barrier()
        self.reset_sems()
        self.scope_end(mark)

    def constf(self, value, parts=128):
        key = ("constf", float(value), parts)
        if not hasattr(self, "_consts"):
            self._consts = {}
        if key not in self._consts:
            assert not self._consts_frozen, f"constf {key} not prewarmed"
            t = self.sbuf_rel("cst", [parts, 1], F32)
            self.emit("vector", "memset", t[:, :], float(value))
            self.barrier()
            self._consts[key] = t[:, :]
        return self._consts[key]

    def wait(self, eng, sem, val):
        if val > 0:
            self.ops[eng].append(("wait", sem.h, val))

    CHAINED = ("vector", "scalar", "gpsimd")

    def emit(self, eng, method, *args, sem=None, inc=1, nochain=False, **kwargs):
        if not hasattr(self, "_chain"):
            self._chain = {}
        chain = (not nochain) and eng in self.CHAINED and method != "dma_start"
        if chain:
            prev = self._chain.get(eng)
            if prev is not None:
                self.ops[eng].append(("wait", prev[0], prev[1]))
            if sem is None:
                sem = self.sem(f"_ch_{eng}")
        tgt = sem.inc(inc) if sem is not None else None
        rec = {"m": method, "a": args, "k": kwargs,
               "sems": ([(sem.h, inc)] if sem is not None else [])}
        if sem is not None:
            rec["tok"] = (sem.h, tgt)
        self.ops[eng].append(rec)
        if chain:
            self._chain[eng] = rec["tok"]
        if not hasattr(self, "_last_op"):
            self._last_op = {}
        self._last_op[eng] = rec
        return tgt

    def dma(self, eng, out, in_, sem=None, **kw):
        return self.emit(eng, "dma_start", sem=sem, inc=16, out=out, in_=in_, **kw)

    def barrier(self, engines=ENGINES):
        if not hasattr(self, "_last_op"):
            self._last_op = {}
        toks = {}
        for eng in engines:
            rec = self._last_op.get(eng)
            if rec is not None and "tok" in rec:
                toks[eng] = rec["tok"]
                continue
            b = self.sem(f"_bar_{eng}")
            if rec is None or rec.get("_bar"):
                rec = {"m": "nop", "a": (), "k": {}, "sems": []}
                self.ops[eng].append(rec)
                self._last_op[eng] = rec
            rec["_bar"] = True
            step = 16 if rec["m"] == "dma_start" else 1
            rec["sems"].append((b.h, step))
            b.inc(step)
            rec["tok"] = (b.h, b.v)
            toks[eng] = rec["tok"]
        for eng in engines:
            for e2, (h, v) in toks.items():
                if e2 != eng:
                    self.ops[eng].append(("wait", h, v))
        return None

    def finalize(self):
        def replay(e, recs):
            for r in recs:
                if isinstance(r, tuple):
                    e.wait_ge(r[1], r[2])
                else:
                    ins = getattr(e, r["m"])(*r["a"], **r["k"])
                    for (h, i) in r["sems"]:
                        ins.then_inc(h, i)

        with self.nc.Block() as block:
            for eng in ENGINES:
                recs = self.ops[eng]
                if recs:
                    getattr(block, eng)(lambda e, recs=recs: replay(e, recs))
        for hid in sorted(self._open_cms, key=lambda k: self._alloc_seq[k], reverse=True):
            self._open_cms[hid].__exit__(None, None, None)
        self._open_cms.clear()
        self.es.__exit__(None, None, None)
        return self.nc


class Common:
    def __init__(self, p: Prog):
        self.p = p
        self.ones128_bf = p.sbuf("c_on128", [128, 1], BF16)
        self.ones1_bf = p.sbuf("c_on1", [1, 128], BF16)
        self.ident_bf = p.sbuf("c_idb", [128, 128], BF16)
        self.ident_f = p.sbuf("c_idf", [128, 128], F32)
        s = p.sem("_cinit")
        p.emit("vector", "memset", self.ones128_bf[:], 1.0)
        p.emit("vector", "memset", self.ones1_bf[:], 1.0)
        p.emit("vector", "memset", self.ident_bf[:], 1.0)
        v = p.emit("vector", "memset", self.ident_f[:], 1.0, sem=s)
        p.wait("gpsimd", s, v)
        p.emit("gpsimd", "affine_select", self.ident_bf[:], self.ident_bf[:],
               [[-1, 128]], ALU.is_equal, 0.0, base=0, channel_multiplier=1)
        p.emit("gpsimd", "affine_select", self.ident_f[:], self.ident_f[:],
               [[-1, 128]], ALU.is_equal, 0.0, base=0, channel_multiplier=1, sem=s)
        p.barrier()



def emit_ln(p, cm, x_dram, nt, xhat, stats):
    ntile = nt // TT
    sD = [p.sem("ln_d0"), p.sem("ln_d1")]
    sV = p.sem("ln_v")
    sA = p.sem("ln_a")
    sT = p.sem("ln_t")
    xs = [p.sbuf_rel("ln_xs", [128, nt], F32) for _ in range(2)]
    xb = [p.sbuf_rel("ln_xb", [128, nt], BF16) for _ in range(2)]
    xq = [p.sbuf_rel("ln_xq", [128, nt], BF16) for _ in range(2)]
    ps_su = [p.psum("ln_psu", [1, TT]) for _ in range(ntile)]
    ps_sq = [p.psum("ln_psq", [1, TT]) for _ in range(ntile)]
    m_row = p.sbuf_rel("ln_mrow", [1, nt], BF16)
    r_row = p.sbuf_rel("ln_rrow", [1, nt], BF16)
    scr = p.sbuf_rel("ln_scr", [1, nt], F32)

    sT0 = sT.v
    for c in range(NCH):
        b = c % 2
        if c >= 2:
            p.wait("sync", sT, sT0 + c - 1)
        w = p.dma("sync", xs[b][:, :], x_dram[c * 128:(c + 1) * 128, :], sem=sD[b])
        p.wait("vector", sD[b], w)
        v = p.emit("vector", "tensor_copy", xb[b][:, :], xs[b][:, :], sem=sV)
        p.wait("scalar", sV, v)
        a = p.emit("scalar", "activation", xq[b][:, :], xb[b][:, :], AF.Square, sem=sA)
        p.wait("tensor", sA, a)
        for it in range(ntile):
            sl = slice(it * TT, it * TT + TT)
            p.emit("tensor", "matmul", ps_su[it][0:1, :], cm.ones128_bf[:], xb[b][:, sl],
                   start=(c == 0), stop=(c == NCH - 1))
            p.emit("tensor", "matmul", ps_sq[it][0:1, :], cm.ones128_bf[:], xq[b][:, sl],
                   start=(c == 0), stop=(c == NCH - 1),
                   sem=(sT if it == ntile - 1 else None))
    p.barrier()

    def sl4(row, it):
        t0 = it * TT
        return stats[0:1, row * nt + t0:row * nt + t0 + TT]

    for it in range(ntile):
        sc = scr[0:1, it * TT:(it + 1) * TT]
        p.emit("vector", "tensor_scalar_mul", sl4(0, it), ps_su[it][0:1, :], 1.0 / D)
        p.emit("vector", "tensor_scalar_mul", sl4(3, it), ps_sq[it][0:1, :], 1.0 / D)
        p.emit("vector", "tensor_mul", sc, sl4(0, it), sl4(0, it))
        p.emit("vector", "tensor_sub", sl4(3, it), sl4(3, it), sc)
    p.barrier()
    for it in range(ntile):
        p.emit("scalar", "activation", sl4(1, it), sl4(3, it), AF.Sqrt, bias=p.constf(1e-5, 1), scale=1.0)
    p.barrier()
    for it in range(ntile):
        p.emit("vector", "reciprocal", sl4(1, it), sl4(1, it))
    p.barrier()
    for it in range(ntile):
        p.emit("scalar", "activation", scr[0:1, it * TT:(it + 1) * TT], sl4(3, it), AF.Sqrt)
    p.barrier()
    for it in range(ntile):
        sc = scr[0:1, it * TT:(it + 1) * TT]
        p.emit("vector", "tensor_mul", sl4(2, it), sc, sl4(1, it))
        p.emit("vector", "tensor_scalar_mul", sl4(2, it), sl4(2, it), 64.0)
        p.emit("vector", "tensor_copy", m_row[0:1, it * TT:(it + 1) * TT], sl4(0, it))
        p.emit("vector", "tensor_copy", r_row[0:1, it * TT:(it + 1) * TT], sl4(1, it))
    p.barrier()
    ps_m = [p.psum("ln_pm", [128, TT]) for _ in range(ntile)]
    ps_r = [p.psum("ln_pr", [128, TT]) for _ in range(ntile)]
    sT2 = p.sem("ln_t2")
    for it in range(ntile):
        t0 = it * TT
        p.emit("tensor", "matmul", ps_m[it][:, :], cm.ones1_bf[:], m_row[0:1, t0:t0 + TT],
               start=True, stop=True)
        p.emit("tensor", "matmul", ps_r[it][:, :], cm.ones1_bf[:], r_row[0:1, t0:t0 + TT],
               start=True, stop=True, sem=sT2)
    sD2 = [p.sem("ln_d2a"), p.sem("ln_d2b")]
    sV5 = p.sem("ln_v5")
    p.wait("vector", sT2, sT2.v)
    sV50 = sV5.v
    for c in range(NCH):
        b = c % 2
        if c >= 2:
            p.wait("scalar", sV5, sV50 + ntile * (c - 1))
        w = p.dma("scalar", xs[b][:, :], x_dram[c * 128:(c + 1) * 128, :], sem=sD2[b])
        p.wait("vector", sD2[b], w)
        for it in range(ntile):
            t0 = it * TT
            p.emit("vector", "tensor_sub", xs[b][:, t0:t0 + TT], xs[b][:, t0:t0 + TT], ps_m[it][:, :])
            p.emit("vector", "tensor_mul", xhat[:, c * nt + t0:c * nt + t0 + TT],
                   xs[b][:, t0:t0 + TT], ps_r[it][:, :], sem=sV5)
    p.barrier()
    p.release(*xs, *xb, *xq, *ps_su, *ps_sq, *ps_m, *ps_r, m_row, r_row, scr)



class WStream:

    def __init__(self, p, name, w_dram, nk, cast_eng="scalar", src_bf16=False):
        self.p = p
        self.w = w_dram
        self.nk = nk
        self.O = w_dram.shape[0]
        assert w_dram.shape[1] == nk * 128, (w_dram.shape, nk)
        self.cast_eng = cast_eng
        self.src_bf16 = src_bf16
        if not src_bf16:
            self.wf = [p.sbuf_rel(f"wsf_{name}", [128, 128 * nk], F32) for _ in range(2)]
        self.wb = [p.sbuf_rel(f"wsb_{name}", [128, 128 * nk], BF16) for _ in range(2)]
        self.sW = [p.sem(f"wsw_{name}0"), p.sem(f"wsw_{name}1")]
        self.sG = p.sem(f"wsg_{name}")
        self.free_toks = []
        self.oi = 0

    def load(self, oc):
        p = self.p
        b = self.oi % 2
        tok = None
        if self.oi >= 2:
            assert len(self.free_toks) >= self.oi - 1, "free() not called"
            tok = self.free_toks[self.oi - 2]
            p.ops["sync"].append(("wait", tok[0], tok[1]))
        dst = self.wb if self.src_bf16 else self.wf
        if self.src_bf16 and tok is not None:
            p.ops["sync"].append(("wait", tok[0], tok[1]))
        w = p.dma("sync", dst[b][:, :], self.w[oc * 128:(oc + 1) * 128, :],
                  sem=self.sW[b])
        if self.src_bf16:
            g = self.sG.inc(0) if False else None
            self.oi += 1
            return b, (self.sW[b], w)
        p.wait(self.cast_eng, self.sW[b], w)
        if tok is not None:
            p.ops[self.cast_eng].append(("wait", tok[0], tok[1]))
        if self.cast_eng == "scalar":
            g = p.emit("scalar", "activation", self.wb[b][:, :], self.wf[b][:, :],
                       AF.Copy, sem=self.sG)
        else:
            g = p.emit(self.cast_eng, "tensor_copy", self.wb[b][:, :],
                       self.wf[b][:, :], sem=self.sG)
        self.oi += 1
        return b, g

    def load_tok(self, t):
        return t

    def free(self, tok):
        self.free_toks.append(tok)

    def done(self):
        self.p.barrier()
        self.p.release(*self.wf, *self.wb)


def tile_ws(wT):
    K_, O_ = wT.shape
    nk, noc = K_ // 128, O_ // 128
    return np.ascontiguousarray(
        wT.reshape(nk, 128, noc, 128).transpose(2, 1, 0, 3).reshape(noc * 128, nk * 128))


def load_const_weight(p, name, w_dram, nk, ocols, sW, casts):
    f = p.sbuf_rel(f"cw_f_{name}", [128, nk * ocols], F32)
    bq = p.sbuf(f"cw_b_{name}", [128, nk * ocols], BF16)
    w = p.dma("sync", f[:, :].rearrange("p (c o) -> p c o", o=ocols),
              w_dram.rearrange("(c p) o -> p c o", p=128), sem=sW)
    casts.append((f, bq, w))
    return f, bq



def prewarm(p):
    cm = Common(p)
    for v, parts in ((1e-5, 1), (-20.0, 128), (0.5, 128), (1e-12, 128),
                     (1.0 / math.sqrt(128), 128), (1.0 / math.sqrt(64), 128)):
        p.constf(v, parts)
    p._consts_frozen = True
    return cm


def build_l0_ffn(nt=1024):
    p = Prog("l0ffn")
    x_d = p.din("x", [D, nt], F32)
    bwT = p.din("bwT", [D, D], F32)
    bb2 = p.din("bb2", [128, NCH], F32)
    a0T = p.din("a0T", [D, 64], F32)
    b0T = p.din("b0T", [D, 1024], F32)
    r0T = p.din("r0T", [D, 16], F32)
    xo_d = p.dout("xo", [D, nt], F32)
    cm = prewarm(p)
    emit_l0_ffn(p, cm, x_d, bwT, bb2, a0T, b0T, r0T, xo_d, nt)
    return p


def emit_l0_ffn(p, cm, x_d, bwT, bb2, a0T, b0T, r0T, xo_d, nt=1024):
    ntile = nt // TT

    xhat = p.sbuf("xhat", [128, NCH * nt], BF16)
    stats = p.sbuf("stats", [1, 4 * nt], F32)
    emit_ln(p, cm, x_d, nt, xhat, stats)

    sW = p.sem("f0_w")
    casts = []
    rwt_f, rwt_b = load_const_weight(p, "rw", r0T, NCH, 16, sW, casts)
    a0_f, a0_b = load_const_weight(p, "a0", a0T, NCH, 64, sW, casts)
    bb_f = p.sbuf("bb_f", [128, NCH], F32)
    wv = p.dma("sync", bb_f[:, :], bb2, sem=sW)
    sG = p.sem("f0_g")
    p.wait("gpsimd", sW, sW.v)
    for f, bq, w in casts:
        p.emit("gpsimd", "tensor_copy", bq[:, :], f[:, :], sem=sG)
    p.wait("vector", sW, sW.v)
    p.emit("vector", "tensor_scalar_mul", bb_f[:, :], bb_f[:, :], 0.5)
    p.barrier()

    cb_bf = p.sbuf("cb_bf", [16, nt], BF16)
    selb = p.sbuf("selb", [16, 16 * 64], BF16)
    sinit = p.sem("f0_sel")
    v = p.emit("vector", "memset", selb[:, :], 1.0, sem=sinit)
    p.wait("gpsimd", sinit, v)
    p.emit("gpsimd", "affine_select", selb[:, :], selb[:, :],
           [[1, 16], [0, 64]], ALU.is_equal, 0.0, base=0, channel_multiplier=-1, sem=sinit)
    hb = p.sbuf("hb", [64, nt], BF16)
    cher = p.sbuf("cher", [128, 8 * nt], BF16)
    ps_r = p.psum("ps_r", [16, TT])
    ps_h = p.psum("ps_h", [64, TT])
    ps_tr = p.psum("ps_tr", [128, 16])
    ps_cb = p.psum("ps_cb", [16, 128])
    ps_bc = p.psum("ps_bc", [64, TT])
    lsb = p.sbuf_rel("lsb", [16, TT], F32)
    ltm = p.sbuf_rel("ltm", [128, 16], F32)
    ex = p.sbuf_rel("ex", [128, 16], F32)
    msk = p.sbuf_rel("msk", [128, 16], F32)
    cbtm = p.sbuf_rel("cbtm", [128, 16], F32)
    top8 = p.sbuf_rel("top8", [128, 8], F32)
    ssum = p.sbuf_rel("ssum", [128, 1], F32)
    den = p.sbuf_rel("den", [128, 1], F32)

    sT = p.sem("f0_t")
    sV = p.sem("f0_v")
    sA = p.sem("f0_a")
    for it in range(ntile):
        t0 = it * TT
        for c in range(NCH):
            p.emit("tensor", "matmul", ps_r[:, :], rwt_b[:, c * 16:(c + 1) * 16],
                   xhat[:, c * nt + t0:c * nt + t0 + TT], start=(c == 0), stop=(c == NCH - 1))
        for c in range(NCH):
            p.emit("tensor", "matmul", ps_h[:, :], a0_b[:, c * 64:(c + 1) * 64],
                   xhat[:, c * nt + t0:c * nt + t0 + TT], start=(c == 0), stop=(c == NCH - 1),
                   sem=sT)
        p.wait("vector", sT, sT.v)
        p.emit("vector", "tensor_copy", lsb[:, :], ps_r[:, :])
        p.emit("vector", "tensor_copy", hb[:, t0:t0 + TT], ps_h[:, :], sem=sV)
        p.barrier()
        for g in range(TT // 128):
            gt = t0 + g * 128
            v = p.emit("tensor", "transpose", ps_tr[:, :], lsb[0:16, g * 128:(g + 1) * 128],
                       cm.ident_f[0:16, 0:16], sem=sT)
            p.wait("vector", sT, v)
            v = p.emit("vector", "tensor_copy", ltm[:, :], ps_tr[:, 0:16], sem=sV)
            p.wait("scalar", sV, v)
            a = p.emit("scalar", "activation", ex[:, :], ltm[:, :], AF.Exp, bias=p.constf(-20.0),
                       scale=1.0, accum_out=ssum[:, :], sem=sA)
            p.wait("vector", sA, a)
            v = p.emit("vector", "max", out=top8[:, :], in_=ex[:, :], sem=sV)
            p.wait("vector", sV, v)
            p.emit("vector", "tensor_add", den[:, :], top8[:, 0:1], top8[:, 1:2])
            p.emit("vector", "tensor_scalar_mul", ssum[:, :], ssum[:, :], 1e-6)
            v = p.emit("vector", "tensor_add", den[:, :], den[:, :], ssum[:, :], sem=sV)
            p.wait("vector", sV, v)
            v = p.emit("vector", "reciprocal", den[:, :], den[:, :], sem=sV)
            p.wait("vector", sV, v)
            p.emit("vector", "tensor_scalar_mul", den[:, :], den[:, :], 0.5)
            p.emit("vector", "tensor_scalar", msk[:, :], ex[:, :], top8[:, 1:2], None,
                   ALU.is_ge, ALU.bypass)
            v = p.emit("vector", "scalar_tensor_tensor", out=cbtm[:, :], in0=msk[:, :],
                       scalar=den[:, :], in1=ex[:, :], op0=ALU.mult, op1=ALU.mult, sem=sV)
            p.wait("tensor", sV, v)
            v = p.emit("tensor", "transpose", ps_cb[:, :], cbtm[:, :], cm.ident_f[:, :], sem=sT)
            p.wait("vector", sT, v)
            p.emit("vector", "tensor_copy", cb_bf[:, gt:gt + 128], ps_cb[0:16, 0:128], sem=sV)
        p.barrier()
        for c in range(8):
            for half in range(2):
                e_idx = 2 * c + half
                v = p.emit("tensor", "matmul", ps_bc[:, :], selb[:, e_idx * 64:(e_idx + 1) * 64],
                           cb_bf[0:16, t0:t0 + TT], start=True, stop=True, sem=sT)
                p.wait("vector", sT, v)
                v = p.emit("vector", "tensor_mul",
                           cher[half * 64:(half + 1) * 64, c * nt + t0:c * nt + t0 + TT],
                           hb[:, t0:t0 + TT], ps_bc[:, :], sem=sV)
                p.wait("tensor", sV, v)
        p.barrier()
    p.release(ps_r, ps_h, ps_tr, ps_cb, ps_bc, lsb, ltm, ex, msk, cbtm, top8, ssum, den)

    ws = WStream(p, "bw", bwT, NCH)
    wsb = WStream(p, "b0", b0T, 8)
    ps_y = [p.psum("ps_y", [128, TT]) for _ in range(2)]
    lastA = [0, 0]
    tl = p.sbuf("tl", [128, TT], F32)
    xstage = p.sbuf("xst", [128, TT], F32)
    sDi = p.sem("f0_di")
    sDo = p.sem("f0_do")
    sV2 = p.sem("f0_v2")
    sA2 = p.sem("f0_a2")
    sT2 = p.sem("f0_t2")
    lds = {0: (ws.load(0), wsb.load(0)), 1: (ws.load(1), wsb.load(1))}
    for oc in range(NCH):
        (b1, g1), (b2, g2) = lds[oc]
        p.wait("tensor", ws.sG, g1)
        p.wait("tensor", wsb.sG, g2)
        for it in range(ntile):
            t0 = it * TT
            pi = (oc * ntile + it) % 2
            ps = ps_y[pi]
            p.wait("tensor", sA2, lastA[pi])
            for c in range(NCH):
                p.emit("tensor", "matmul", ps[:, :], ws.wb[b1][:, c * 128:(c + 1) * 128],
                       xhat[:, c * nt + t0:c * nt + t0 + TT], start=(c == 0), stop=False)
            for c in range(8):
                p.emit("tensor", "matmul", ps[:, :], wsb.wb[b2][:, c * 128:(c + 1) * 128],
                       cher[:, c * nt + t0:c * nt + t0 + TT], start=False, stop=(c == 7),
                       sem=(sT2 if c == 7 else None))
            p.wait("vector", sT2, sT2.v)
            p.wait("vector", sV2, sV2.v)
            a = p.emit("vector", "tensor_scalar", tl[:, :], ps[:, :], 0.5,
                       bb_f[:, oc:oc + 1], ALU.mult, ALU.add, sem=sA2)
            lastA[pi] = a
            w = p.dma("sync", xstage[:, :], x_d[oc * 128:(oc + 1) * 128, t0:t0 + TT], sem=sDi)
            p.wait("vector", sA2, a)
            p.wait("vector", sDi, w)
            v = p.emit("vector", "tensor_add", xstage[:, :], xstage[:, :], tl[:, :], sem=sV2)
            p.wait("scalar", sV2, v)
            ow = p.dma("scalar", xo_d[oc * 128:(oc + 1) * 128, t0:t0 + TT], xstage[:, :], sem=sDo)
            p.wait("sync", sDo, ow)
        ws.free((sT2.h, sT2.v))
        wsb.free((sT2.h, sT2.v))
        if oc + 2 < NCH:
            lds[oc + 2] = (ws.load(oc + 2), wsb.load(oc + 2))
    p.wait("scalar", sDo, sDo.v)
    p.barrier(("scalar",))



_PROG_CACHE = {}


def _get_prog(key, builder, *args):
    if key not in _PROG_CACHE:
        _PROG_CACHE[key] = builder(*args).finalize()
    return _PROG_CACHE[key]


def _run(nc, in_maps):
    return run_bass_kernel_spmd(nc, in_maps, core_ids=list(range(NCORE))).results


def _shard_tokens(xF):
    return [np.ascontiguousarray(xF[:, c * 1024:(c + 1) * 1024]) for c in range(NCORE)]


def _unshard(results, key):
    return np.concatenate([np.asarray(r[key], np.float32) for r in results], axis=1)


def l0_ffn_host(xF, w):
    nc = _get_prog("l0ffn", build_l0_ffn)
    base = {"bwT": w["bwT"], "bb2": w["bb2"], "a0T": w["a0T"], "b0T": w["b0T"], "r0T": w["r0T"]}
    in_maps = [dict(base, x=s) for s in _shard_tokens(xF)]
    return _unshard(_run(nc, in_maps), "xo")


def _rope_perm(n_heads, hd):
    per = np.concatenate([np.arange(0, hd, 2), np.arange(1, hd, 2)])
    return np.concatenate([h * hd + per for h in range(n_heads)])


def prep_weights(i):
    w = {}
    w["bwT"] = tile_ws(i["L0_ffn_base_w"].T)
    w["bb2"] = np.ascontiguousarray(i["L0_ffn_base_b"].reshape(NCH, 128).T)
    w["a0T"] = np.ascontiguousarray(i["L0_ffn_A"].T)
    w["b0T"] = tile_ws(np.transpose(i["L0_ffn_B"], (0, 2, 1)).reshape(1024, D))
    w["r0T"] = np.ascontiguousarray(i["L0_ffn_router_w"].T)
    perm0 = _rope_perm(32, 128)
    w["l0_wqT"] = tile_ws(i["L0_attn_wq"].T[:, perm0])
    w["l0_wkT"] = tile_ws(i["L0_attn_wk"].T[:, perm0])
    w["l0_wvT"] = tile_ws(i["L0_attn_wv"].T)
    w["l0_woT"] = tile_ws(i["L0_attn_wo"].T)
    th = np.arange(S, dtype=np.float64)
    w["l0_cos"] = np.ascontiguousarray(np.broadcast_to(np.cos(th)[None, :], (64, S))).astype(np.float32)
    w["l0_sin"] = np.ascontiguousarray(np.broadcast_to(np.sin(th)[None, :], (64, S))).astype(np.float32)

    w["a1T"] = tile_ws(np.transpose(i["L1_ffn_A"], (2, 0, 1)).reshape(D, 1024))
    w["b1T"] = tile_ws(np.transpose(i["L1_ffn_B"], (0, 2, 1)).reshape(1024, D))
    w["wr1T"] = np.ascontiguousarray(i["L1_ffn_Wr"].T)

    def l1_perm(n_rows):
        nchunk = n_rows // 128
        pm = np.zeros(n_rows, np.int64)
        for c in range(nchunk):
            h0, h1 = 2 * c, 2 * c + 1
            pm[c * 128 + 0: c * 128 + 32] = h0 * 64 + np.arange(0, 64, 2)
            pm[c * 128 + 32: c * 128 + 64] = h1 * 64 + np.arange(0, 64, 2)
            pm[c * 128 + 64: c * 128 + 96] = h0 * 64 + np.arange(1, 64, 2)
            pm[c * 128 + 96: c * 128 + 128] = h1 * 64 + np.arange(1, 64, 2)
        return pm
    qpm = l1_perm(1024)
    kpm = l1_perm(256)
    w["l1_qpm"] = qpm
    w["l1_kpm"] = kpm
    w["l1_wqkvT"] = tile_ws(i["L1_attn_w_qkv"].T)
    w["l1_woT"] = tile_ws(i["L1_attn_w_o"].T)
    def blk1(cw, nch):
        out = np.zeros((3, nch, 128, 128), np.float32)
        for d in range(3):
            for c in range(nch):
                for half in range(2):
                    o0 = c * 128 + half * 64
                    out[d, c, half * 64:(half + 1) * 64, half * 64:(half + 1) * 64] =                         cw[o0:o0 + 64, :, d].T
        return out.reshape(3 * nch * 128, 128)
    w["l1_qc1blk"] = blk1(i["L1_attn_qc1_w"], 8)
    w["l1_kc1blk"] = blk1(i["L1_attn_kc1_w"], 2)
    def c2T(cw, pm):
        n = cw.shape[0]
        out = np.zeros((3, n, n), np.float32)
        for d in range(3):
            out[d] = cw[:, :, d].T[:, pm]
        return out.reshape(3 * n, n)
    w["l1_qc2T"] = tile_ws(c2T(i["L1_attn_qc2_w"], qpm))
    w["l1_kc2T"] = tile_ws(c2T(i["L1_attn_kc2_w"], kpm))
    w["l1_qc1b"] = np.ascontiguousarray(i["L1_attn_qc1_b"].reshape(8, 128).T)
    w["l1_kc1b"] = np.ascontiguousarray(i["L1_attn_kc1_b"].reshape(2, 128).T)
    w["l1_qc2b"] = np.ascontiguousarray(i["L1_attn_qc2_b"][qpm].reshape(8, 128).T)
    w["l1_kc2b"] = np.ascontiguousarray(i["L1_attn_kc2_b"][kpm].reshape(2, 128).T)
    def msel(n_in, pm_in, coef):
        tgt = np.zeros(128, np.int64)
        tgt[0:32] = np.arange(0, 64, 2); tgt[32:64] = np.arange(0, 64, 2)
        tgt[64:96] = np.arange(1, 64, 2); tgt[96:128] = np.arange(1, 64, 2)
        out = np.zeros((n_in, 128), np.float32)
        din = pm_in % 64 if False else None
        for r in range(n_in):
            out[r, :] = (tgt == (r % 64)) * coef
        return out
    w["l1_mselk"] = msel(256, None, 1.0 / 8.0)
    w["l1_mselq"] = msel(1024, None, 1.0 / 32.0)
    def headof(n_rows):
        ho = np.zeros(n_rows, np.int64)
        for c in range(n_rows // 128):
            ho[c * 128 + 0:c * 128 + 32] = 2 * c
            ho[c * 128 + 32:c * 128 + 64] = 2 * c + 1
            ho[c * 128 + 64:c * 128 + 96] = 2 * c
            ho[c * 128 + 96:c * 128 + 128] = 2 * c + 1
        return ho
    hoq = headof(1024)
    hok = headof(256)
    w["l1_hselq"] = np.ascontiguousarray((hoq[:, None] == np.arange(16)[None, :]).astype(np.float32))
    w["l1_hbcq"] = np.ascontiguousarray(w["l1_hselq"].T)
    w["l1_hselk"] = np.ascontiguousarray((hok[:, None] == np.arange(4)[None, :]).astype(np.float32))
    w["l1_hbck"] = np.ascontiguousarray(w["l1_hselk"].T)
    w["l1_kt"] = np.ascontiguousarray(i["L1_attn_key_temp"].reshape(1, 1))
    inv = 1.0 / 10000.0 ** (np.arange(0, 64, 2, dtype=np.float64) / 64.0)
    ang = np.arange(S, dtype=np.float64)[:, None] * inv[None, :]
    c1 = np.cos(ang).T.astype(np.float32)
    s1 = np.sin(ang).T.astype(np.float32)
    w["l1_cos"] = np.ascontiguousarray(np.concatenate([c1, c1, c1, c1], axis=0))
    w["l1_sin"] = np.ascontiguousarray(np.concatenate([s1, s1, s1, s1], axis=0))

    w["a2T"] = np.ascontiguousarray(np.transpose(i["L2_ffn_A"], (2, 0, 1)).reshape(D, 128))
    w["b2T"] = tile_ws(np.transpose(i["L2_ffn_Bw"], (0, 2, 1)).reshape(128, D))
    w["rt2T"] = np.ascontiguousarray(i["L2_ffn_route_w"].T)
    wqkv2 = i["L2_attn_w_qkv"]
    permq2 = _rope_perm(32, 128)
    permk2 = _rope_perm(8, 128)
    cols = np.concatenate([permq2, 4096 + permk2, np.arange(5120, 6144)])
    w["l2_wqkvT"] = tile_ws(wqkv2[:, cols])
    w["l2_woT"] = tile_ws(i["L2_attn_w_o"])
    inv2 = 1.0 / 10000.0 ** (np.arange(0, 128, 2, dtype=np.float64) / 128.0)
    ang2 = np.arange(S, dtype=np.float64)[None, :] * inv2[:, None]
    w["l2_cos"] = np.ascontiguousarray(np.cos(ang2)).astype(np.float32)
    w["l2_sin"] = np.ascontiguousarray(np.sin(ang2)).astype(np.float32)

    w["l3_wqT"] = tile_ws(i["L3_attn_wq"])
    w["l3_wkvT"] = tile_ws(np.concatenate([i["L3_attn_wk"], i["L3_attn_wv"]], axis=1))
    w["l3_woT"] = tile_ws(i["L3_attn_wo"])
    w["l3_bias2"] = np.ascontiguousarray(i["L3_ffn_bias"].reshape(NCH, 128).T)
    return w


def attn_host(xF, w, wq_key, wk_key, wv_key, wo_key, cos_key, sin_key,
              progA, progB, progC):
    ncA = progA
    shards = _shard_tokens(xF)
    in_maps = []
    for b in range(NB):
        for r in range(2):
            sl = slice(r * 1024, (r + 1) * 1024)
            in_maps.append({"x": shards[2 * b + r], "wqT": w[wq_key], "wkT": w[wk_key],
                            "wvT": w[wv_key], "cosd": np.ascontiguousarray(w[cos_key][:, sl]),
                            "sind": np.ascontiguousarray(w[sin_key][:, sl])})
    resA = _run(ncA, in_maps)
    qF = [np.asarray(r["qF"]) for r in resA]
    kF = [np.asarray(r["kF"]) for r in resA]
    vT = [np.asarray(r["vT"]) for r in resA]
    in_maps = []
    for b in range(NB):
        qb = np.concatenate([qF[2 * b], qF[2 * b + 1]], axis=1)
        kb = np.concatenate([kF[2 * b], kF[2 * b + 1]], axis=1)
        vb = np.concatenate([vT[2 * b], vT[2 * b + 1]], axis=0)
        for s_ in range(2):
            rows = slice(2048 * s_, 2048 * (s_ + 1))
            in_maps.append({"q": np.ascontiguousarray(qb[rows]),
                            "k": np.ascontiguousarray(kb[rows]),
                            "vT": np.ascontiguousarray(vb[:, rows]),})
    resB = _run(progB, in_maps)
    ao = [np.asarray(r["ao"]) for r in resB]
    in_maps = []
    for b in range(NB):
        aob = np.concatenate([ao[2 * b], ao[2 * b + 1]], axis=0)
        for r in range(2):
            in_maps.append({"x": shards[2 * b + r], "woT": w[wo_key],
                            "ao": np.ascontiguousarray(aob[:, r * 1024:(r + 1) * 1024])})
    resC = _run(progC, in_maps)
    return _unshard(resC, "xo")


def l1_ffn_host(xF, w):
    nc = _get_prog("l1ffn", build_l1_ffn)
    base = {"a1T": w["a1T"], "b1T": w["b1T"], "wrT": w["wr1T"]}
    in_maps = [dict(base, x=s) for s in _shard_tokens(xF)]
    return _unshard(_run(nc, in_maps), "xo")


def l1_attn_host(xF, w):
    ncA1 = _get_prog("l1atA1", build_l1_attn_A1)
    ncA2 = _get_prog("l1atA2", build_l1_attn_A2)
    ncB = _get_prog("l1atB", lambda: build_attn_B(8, 64, 4, tag="l1"))
    ncC = _get_prog("l1atC", lambda: build_attn_C(8, tag="l1"))
    shards = _shard_tokens(xF)
    resA1 = _run(ncA1, [{"x": s, "wqkvT": w["l1_wqkvT"]} for s in shards])
    qkv = [np.asarray(r["qkv"]) for r in resA1]
    zh1024 = np.zeros((1024, 4), BF)
    zh256 = np.zeros((256, 4), BF)
    zh128 = np.zeros((128, 1), BF)
    base2 = {k[3:]: w[k] for k in ("l1_qc1blk", "l1_kc1blk", "l1_qc2T", "l1_kc2T",
                                   "l1_qc1b", "l1_kc1b", "l1_qc2b", "l1_kc2b",
                                   "l1_mselk", "l1_mselq", "l1_hselq", "l1_hbcq",
                                   "l1_hselk", "l1_hbck", "l1_kt")}
    base2 = {"qc1blk": w["l1_qc1blk"], "kc1blk": w["l1_kc1blk"], "qc2T": w["l1_qc2T"],
             "kc2T": w["l1_kc2T"], "qc1b": w["l1_qc1b"], "kc1b": w["l1_kc1b"],
             "qc2b": w["l1_qc2b"], "kc2b": w["l1_kc2b"], "mselk": w["l1_mselk"],
             "mselq": w["l1_mselq"], "hselq": w["l1_hselq"], "hbcq": w["l1_hbcq"],
             "hselk": w["l1_hselk"], "hbck": w["l1_hbck"], "kt": w["l1_kt"]}
    in_maps = []
    for b in range(NB):
        qkv_b = np.concatenate([qkv[2 * b], qkv[2 * b + 1]], axis=1)
        for r in range(2):
            sl = slice(r * 1024, (r + 1) * 1024)
            m = dict(base2,
                     qp=np.ascontiguousarray(qkv_b[0:1024, sl]),
                     kp=np.ascontiguousarray(qkv_b[1024:1280, sl]),
                     vc=np.ascontiguousarray(qkv_b[1280:1408, sl]),
                     vp=np.ascontiguousarray(qkv_b[1408:1536, sl]),
                     qph=(np.ascontiguousarray(qkv_b[0:1024, 1020:1024]) if r else zh1024),
                     kph=(np.ascontiguousarray(qkv_b[1024:1280, 1020:1024]) if r else zh256),
                     vph=(np.ascontiguousarray(qkv_b[1408:1536, 1023:1024]) if r else zh128),
                     cosd=np.ascontiguousarray(w["l1_cos"][:, sl]),
                     sind=np.ascontiguousarray(w["l1_sin"][:, sl]))
            in_maps.append(m)
    resA2 = _run(ncA2, in_maps)
    qh = [np.asarray(r["qhat"]) for r in resA2]
    kh = [np.asarray(r["khat"]) for r in resA2]
    vt = [np.asarray(r["vT"]) for r in resA2]
    in_maps = []
    for b in range(NB):
        qb = np.concatenate([qh[2 * b], qh[2 * b + 1]], axis=1)
        kb = np.concatenate([kh[2 * b], kh[2 * b + 1]], axis=1)
        vb = np.concatenate([vt[2 * b], vt[2 * b + 1]], axis=0)
        for s_ in range(2):
            in_maps.append({"q": np.ascontiguousarray(qb[512 * s_:512 * (s_ + 1)]),
                            "k": np.ascontiguousarray(kb[128 * s_:128 * (s_ + 1)]),
                            "vT": np.ascontiguousarray(vb[:, 128 * s_:128 * (s_ + 1)])})
    resB = _run(ncB, in_maps)
    ao = [np.asarray(r["ao"]) for r in resB]
    shards = _shard_tokens(xF)
    in_maps = []
    for b in range(NB):
        aob = np.concatenate([ao[2 * b], ao[2 * b + 1]], axis=0)
        for r in range(2):
            in_maps.append({"x": shards[2 * b + r], "woT": w["l1_woT"],
                            "ao": np.ascontiguousarray(aob[:, r * 1024:(r + 1) * 1024])})
    return _unshard(_run(ncC, in_maps), "xo")


def l2_ffn_host(xF, w):
    nc = _get_prog("l2ffn", build_l2_ffn)
    base = {"a2T": w["a2T"], "b2T": w["b2T"], "rtT": w["rt2T"]}
    in_maps = [dict(base, x=s) for s in _shard_tokens(xF)]
    return _unshard(_run(nc, in_maps), "xo")


def l2_attn_host(xF, w):
    ncA = _get_prog("l2atA", build_l2_attn_A)
    ncB = _get_prog("l2atB", lambda: build_attn_B(16, 128, 4, tag="l2"))
    ncC = _get_prog("l2atC", lambda: build_attn_C(NCH, tag="l2"))
    shards = _shard_tokens(xF)
    in_maps = []
    for b in range(NB):
        for r in range(2):
            sl = slice(r * 1024, (r + 1) * 1024)
            in_maps.append({"x": shards[2 * b + r], "wqkvT": w["l2_wqkvT"],
                            "cosd": np.ascontiguousarray(w["l2_cos"][:, sl]),
                            "sind": np.ascontiguousarray(w["l2_sin"][:, sl])})
    resA = _run(ncA, in_maps)
    qF = [np.asarray(r["qF"]) for r in resA]
    kF = [np.asarray(r["kF"]) for r in resA]
    vT = [np.asarray(r["vT"]) for r in resA]
    in_maps = []
    for b in range(NB):
        qb = np.concatenate([qF[2 * b], qF[2 * b + 1]], axis=1)
        kb = np.concatenate([kF[2 * b], kF[2 * b + 1]], axis=1)
        vb = np.concatenate([vT[2 * b], vT[2 * b + 1]], axis=0)
        for s_ in range(2):
            in_maps.append({"q": np.ascontiguousarray(qb[2048 * s_:2048 * (s_ + 1)]),
                            "k": np.ascontiguousarray(kb[512 * s_:512 * (s_ + 1)]),
                            "vT": np.ascontiguousarray(vb[:, 512 * s_:512 * (s_ + 1)])})
    resB = _run(ncB, in_maps)
    ao = [np.asarray(r["ao"]) for r in resB]
    in_maps = []
    for b in range(NB):
        aob = np.concatenate([ao[2 * b], ao[2 * b + 1]], axis=0)
        for r in range(2):
            in_maps.append({"x": shards[2 * b + r], "woT": w["l2_woT"],
                            "ao": np.ascontiguousarray(aob[:, r * 1024:(r + 1) * 1024])})
    return _unshard(_run(ncC, in_maps), "xo")


def l3_dequant_host(i):
    nc = _get_prog("l3dq", build_l3_dequant)
    qw = i["L3_ffn_qweight"]
    in_maps = []
    for c in range(NCORE):
        sl = slice(512 * c, 512 * (c + 1))
        in_maps.append({"qw": np.ascontiguousarray(qw[:, sl, :].reshape(8 * 512, 128)),
                        "lut4": np.ascontiguousarray(i["L3_ffn_lut4"][sl]),
                        "lut8": np.ascontiguousarray(i["L3_ffn_lut8"][sl])})
    res = _run(nc, in_maps)
    wlT = np.concatenate([np.asarray(r["wlT"]) for r in res], axis=1)
    whT = np.concatenate([np.asarray(r["whT"]) for r in res], axis=1)
    return tile_ws(wlT), tile_ws(whT)


def l3_ffn_host(xF, w, wlT, whT):
    nc = _get_prog("l3ffn", build_l3_ffn)
    base = {"wlT": wlT, "whT": whT, "bias2": w["l3_bias2"]}
    in_maps = [dict(base, x=s) for s in _shard_tokens(xF)]
    return _unshard(_run(nc, in_maps), "xo")


def l3_attn_host(xF, w):
    ncA = _get_prog("l3atA", build_l3_attn_A)
    ncB = _get_prog("l3atB", lambda: build_attn_B(16, 128, 16, tag="l3"))
    ncC = _get_prog("l3atC", lambda: build_attn_C(NCH, tag="l3"))
    shards = _shard_tokens(xF)
    in_maps = [{"x": s, "wqT": w["l3_wqT"], "wkvT": w["l3_wkvT"]} for s in shards]
    resA = _run(ncA, in_maps)
    qF = [np.asarray(r["qF"]) for r in resA]
    kF = [np.asarray(r["kF"]) for r in resA]
    vT = [np.asarray(r["vT"]) for r in resA]
    in_maps = []
    for b in range(NB):
        qb = np.concatenate([qF[2 * b], qF[2 * b + 1]], axis=1)
        kb = np.ascontiguousarray(np.concatenate([kF[2 * b], kF[2 * b + 1]], axis=1))
        vb = np.ascontiguousarray(np.concatenate([vT[2 * b], vT[2 * b + 1]], axis=0))
        for s_ in range(2):
            in_maps.append({"q": np.ascontiguousarray(qb[2048 * s_:2048 * (s_ + 1)]),
                            "k": kb, "vT": vb})
    resB = _run(ncB, in_maps)
    ao = [np.asarray(r["ao"]) for r in resB]
    in_maps = []
    for b in range(NB):
        aob = np.concatenate([ao[2 * b], ao[2 * b + 1]], axis=0)
        for r in range(2):
            in_maps.append({"x": shards[2 * b + r], "woT": w["l3_woT"],
                            "ao": np.ascontiguousarray(aob[:, r * 1024:(r + 1) * 1024])})
    return _unshard(_run(ncC, in_maps), "xo")


N_SUB = 12



def emit_copy_out(p, src, dst, nk=NCH):
    sC = p.sem("cp_d")
    stg = p.sbuf_rel("cp_stg", [128, src.shape[1]], src.dtype if hasattr(src, "dtype") else F32)
    for c in range(nk):
        w = p.dma("sync", stg[:, :], src[c * 128:(c + 1) * 128, :], sem=sC)
        p.wait("scalar", sC, w)
        w2 = p.dma("scalar", dst[c * 128:(c + 1) * 128, :], stg[:, :], sem=sC)
        p.wait("sync", sC, w2)
    p.barrier()


def build_mega(n_units=12):
    p = Prog("mega")
    S_ = S
    bwT = p.din("bwT", [D, D], F32)
    bb2 = p.din("bb2", [128, NCH], F32)
    a0T = p.din("a0T", [D, 64], F32)
    b0T = p.din("b0T", [D, 1024], F32)
    r0T = p.din("r0T", [D, 16], F32)
    wq0T = p.din("wq0T", [D, D], F32)
    wk0T = p.din("wk0T", [D, D], F32)
    wv0T = p.din("wv0T", [D, D], F32)
    wo0T = p.din("wo0T", [D, D], F32)
    cos0 = p.din("cos0", [64, S_], F32)
    sin0 = p.din("sin0", [64, S_], F32)
    a1T = p.din("a1T", [1024, D], F32)
    b1T = p.din("b1T", [D, 1024], F32)
    wr1T = p.din("wr1T", [D, 64], F32)
    wqkv1T = p.din("wqkv1T", [1536, D], F32)
    wo1T = p.din("wo1T", [D, 1024], F32)
    qc1blk = p.din("qc1blk", [24 * 128, 128], F32)
    kc1blk = p.din("kc1blk", [6 * 128, 128], F32)
    qc2T = p.din("qc2T", [1024, 3 * 1024], F32)
    kc2T = p.din("kc2T", [256, 3 * 256], F32)
    qc1b = p.din("qc1b", [128, 8], F32)
    kc1b = p.din("kc1b", [128, 2], F32)
    qc2b = p.din("qc2b", [128, 8], F32)
    kc2b = p.din("kc2b", [128, 2], F32)
    mselk = p.din("mselk", [256, 128], F32)
    mselq = p.din("mselq", [1024, 128], F32)
    hselq = p.din("hselq", [1024, 16], F32)
    hbcq = p.din("hbcq", [16, 1024], F32)
    hselk = p.din("hselk", [256, 4], F32)
    hbck = p.din("hbck", [4, 256], F32)
    kt = p.din("kt", [1, 1], F32)
    cos1 = p.din("cos1", [128, S_], F32)
    sin1 = p.din("sin1", [128, S_], F32)
    a2T = p.din("a2T", [D, 128], F32)
    b2T = p.din("b2T", [D, 128], F32)
    rt2T = p.din("rt2T", [D, 8], F32)
    wqkv2T = p.din("wqkv2T", [6144, D], F32)
    wo2T = p.din("wo2T", [D, D], F32)
    cos2 = p.din("cos2", [64, S_], F32)
    sin2 = p.din("sin2", [64, S_], F32)
    wq3T = p.din("wq3T", [D, D], F32)
    wkv3T = p.din("wkv3T", [256, D], F32)
    wo3T = p.din("wo3T", [D, D], F32)
    wlT = p.din("wlT", [D, D], BF16)
    whT = p.din("whT", [D, D], BF16)
    bias2 = p.din("bias2", [128, NCH], F32)
    zq = p.din("zq", [1536, 4], BF16)
    x_in = p.din("x", [D, S_], F32)
    xo_h = [p.dout(f"xo{h}", [D, 1024], F32) for h in range(2)]
    cm = prewarm(p)

    def stage(name):
        return [p.dtmp(f"{name}_{h}", [D, 1024], F32) for h in range(2)]

    xs = [[x_in[:, 0:1024], x_in[:, 1024:2048]]] + [stage(f"x{i}") for i in range(1, 12)]
    xs.append(xo_h)
    NT = 1024

    def SL(emitfn, *args, **kw):
        mark = p.scope_begin()
        emitfn(p, cm, *args, **kw)
        p.sublayer_end(mark)

    si = [0]
    done_u = [0]

    def bump():
        si[0] += 1
        done_u[0] += 1
        if done_u[0] == n_units and n_units < 12:
            for h in range(2):
                mark = p.scope_begin()
                emit_copy_out(p, xs[si[0]][h], xo_h[h])
                p.sublayer_end(mark)
            return True
        return False

    def nx(h):
        return xs[si[0] + 1][h]

    def cur(h):
        return xs[si[0]][h]

    def hoff(h):
        return h * NT

    for h in range(2):
        SL(emit_l0_ffn, cur(h), bwT, bb2, a0T, b0T, r0T, nx(h), NT)
    if bump():
        return p
    qF = [p.dtmp(f"l0qF{h}", [D, NT], BF16) for h in range(2)]
    kF = [p.dtmp(f"l0kF{h}", [D, NT], BF16) for h in range(2)]
    vT = [p.dtmp(f"l0vT{h}", [NT, D], BF16) for h in range(2)]
    ao0 = p.dtmp("ao0", [D, S_], BF16)
    for h in range(2):
        SL(emit_l0_attn_A, cur(h), wq0T, wk0T, wv0T,
           cos0[:, hoff(h):hoff(h) + NT], sin0[:, hoff(h):hoff(h) + NT],
           qF[h], kF[h], vT[h], NT)
    SL(emit_attn_B, [(qF[0], 0), (qF[1], NT)], [(kF[0], 0), (kF[1], NT)],
       [(vT[0], 0), (vT[1], NT)], ao0, 32, 128, 1, S_)
    for h in range(2):
        SL(emit_attn_C, cur(h), ao0[:, hoff(h):hoff(h) + NT], wo0T, nx(h), NCH, NT)
    if bump():
        return p
    for h in range(2):
        SL(emit_l0_ffn, cur(h), bwT, bb2, a0T, b0T, r0T, nx(h), NT)
    if bump():
        return p

    for h in range(2):
        SL(emit_l1_ffn, cur(h), a1T, b1T, wr1T, nx(h), NT)
    if bump():
        return p
    qkv = [p.dtmp(f"l1qkv{h}", [1536, NT], BF16) for h in range(2)]
    for h in range(2):
        SL(emit_l1_attn_A1, cur(h), wqkv1T, qkv[h], NT)
    qhat = [p.dtmp(f"l1qh{h}", [1024, NT], BF16) for h in range(2)]
    khat = [p.dtmp(f"l1kh{h}", [256, NT], BF16) for h in range(2)]
    v1T = [p.dtmp(f"l1vT{h}", [NT, 256], BF16) for h in range(2)]
    ao1 = p.dtmp("ao1", [1024, S_], BF16)
    for h in range(2):
        SL(emit_l1_attn_A2, dict(
            qp=qkv[h][0:1024, :], kp=qkv[h][1024:1280, :],
            vc=qkv[h][1280:1408, :], vp=qkv[h][1408:1536, :],
            qph=(zq[0:1024, :] if h == 0 else qkv[0][0:1024, 1020:1024]),
            kph=(zq[1024:1280, :] if h == 0 else qkv[0][1024:1280, 1020:1024]),
            vph=(zq[1408:1536, 0:1] if h == 0 else qkv[0][1408:1536, 1023:1024]),
            qc1blk=qc1blk, kc1blk=kc1blk, qc2T=qc2T, kc2T=kc2T,
            qc1b=qc1b, kc1b=kc1b, qc2b=qc2b, kc2b=kc2b,
            mselk=mselk, mselq=mselq, hselq=hselq, hbcq=hbcq,
            hselk=hselk, hbck=hbck, kt=kt,
            cosd=cos1[:, hoff(h):hoff(h) + NT], sind=sin1[:, hoff(h):hoff(h) + NT],
            qhat=qhat[h], khat=khat[h], vT=v1T[h]), NT)
    SL(emit_attn_B, [(qhat[0], 0), (qhat[1], NT)], [(khat[0], 0), (khat[1], NT)],
       [(v1T[0], 0), (v1T[1], NT)], ao1, 16, 64, 4, S_)
    for h in range(2):
        SL(emit_attn_C, cur(h), ao1[:, hoff(h):hoff(h) + NT], wo1T, nx(h), 8, NT)
    if bump():
        return p
    for h in range(2):
        SL(emit_l1_ffn, cur(h), a1T, b1T, wr1T, nx(h), NT)
    if bump():
        return p

    for h in range(2):
        SL(emit_l2_ffn, cur(h), a2T, b2T, rt2T, nx(h), NT)
    if bump():
        return p
    q2F = [p.dtmp(f"l2qF{h}", [D, NT], BF16) for h in range(2)]
    k2F = [p.dtmp(f"l2kF{h}", [1024, NT], BF16) for h in range(2)]
    v2T = [p.dtmp(f"l2vT{h}", [NT, 1024], BF16) for h in range(2)]
    ao2 = p.dtmp("ao2", [D, S_], BF16)
    for h in range(2):
        SL(emit_l2_attn_A, cur(h), wqkv2T,
           cos2[:, hoff(h):hoff(h) + NT], sin2[:, hoff(h):hoff(h) + NT],
           q2F[h], k2F[h], v2T[h], NT)
    SL(emit_attn_B, [(q2F[0], 0), (q2F[1], NT)], [(k2F[0], 0), (k2F[1], NT)],
       [(v2T[0], 0), (v2T[1], NT)], ao2, 32, 128, 4, S_)
    for h in range(2):
        SL(emit_attn_C, cur(h), ao2[:, hoff(h):hoff(h) + NT], wo2T, nx(h), NCH, NT)
    if bump():
        return p
    for h in range(2):
        SL(emit_l2_ffn, cur(h), a2T, b2T, rt2T, nx(h), NT)
    if bump():
        return p

    for h in range(2):
        SL(emit_l3_ffn, cur(h), wlT, whT, bias2, nx(h), NT)
    if bump():
        return p
    q3F = [p.dtmp(f"l3qF{h}", [D, NT], BF16) for h in range(2)]
    k3F = [p.dtmp(f"l3kF{h}", [128, NT], BF16) for h in range(2)]
    v3T = [p.dtmp(f"l3vT{h}", [NT, 128], BF16) for h in range(2)]
    ao3 = p.dtmp("ao3", [D, S_], BF16)
    for h in range(2):
        SL(emit_l3_attn_A, cur(h), wq3T, wkv3T, q3F[h], k3F[h], v3T[h], NT)
    SL(emit_attn_B, [(q3F[0], 0), (q3F[1], NT)], [(k3F[0], 0), (k3F[1], NT)],
       [(v3T[0], 0), (v3T[1], NT)], ao3, 32, 128, 32, S_)
    for h in range(2):
        SL(emit_attn_C, cur(h), ao3[:, hoff(h):hoff(h) + NT], wo3T, nx(h), NCH, NT)
    if bump():
        return p
    for h in range(2):
        SL(emit_l3_ffn, cur(h), wlT, whT, bias2, nx(h), NT)
    return p


def mega_base(w, wlT, whT):
    return {
        "bwT": w["bwT"], "bb2": w["bb2"], "a0T": w["a0T"], "b0T": w["b0T"],
        "r0T": w["r0T"], "wq0T": w["l0_wqT"], "wk0T": w["l0_wkT"],
        "wv0T": w["l0_wvT"], "wo0T": w["l0_woT"], "cos0": w["l0_cos"],
        "sin0": w["l0_sin"], "a1T": w["a1T"], "b1T": w["b1T"], "wr1T": w["wr1T"],
        "wqkv1T": w["l1_wqkvT"], "wo1T": w["l1_woT"], "qc1blk": w["l1_qc1blk"],
        "kc1blk": w["l1_kc1blk"], "qc2T": w["l1_qc2T"], "kc2T": w["l1_kc2T"],
        "qc1b": w["l1_qc1b"], "kc1b": w["l1_kc1b"], "qc2b": w["l1_qc2b"],
        "kc2b": w["l1_kc2b"], "mselk": w["l1_mselk"], "mselq": w["l1_mselq"],
        "hselq": w["l1_hselq"], "hbcq": w["l1_hbcq"], "hselk": w["l1_hselk"],
        "hbck": w["l1_hbck"], "kt": w["l1_kt"], "cos1": w["l1_cos"],
        "sin1": w["l1_sin"], "a2T": w["a2T"], "b2T": w["b2T"], "rt2T": w["rt2T"],
        "wqkv2T": w["l2_wqkvT"], "wo2T": w["l2_woT"], "cos2": w["l2_cos"],
        "sin2": w["l2_sin"], "wq3T": w["l3_wqT"], "wkv3T": w["l3_wkvT"],
        "wo3T": w["l3_woT"], "wlT": wlT, "whT": whT, "bias2": w["l3_bias2"],
        "zq": np.zeros((1536, 4), BF),
    }


def kernel_mega(inputs):
    w = prep_weights(inputs)
    wlT, whT = l3_dequant_host(inputs)
    nc = _get_prog("mega", build_mega)
    base = mega_base(w, wlT, whT)
    x = inputs["L0_attn_x"]
    xb = [np.ascontiguousarray(x[b].reshape(S, D).T) for b in range(NB)]
    in_maps = [dict(base, x=xb[c // 2]) for c in range(NCORE)]
    res = _run(nc, in_maps)
    out = np.empty((NB, S, D), np.float32)
    for b in range(NB):
        r = res[2 * b]
        xf = np.concatenate([np.asarray(r["xo0"], np.float32),
                             np.asarray(r["xo1"], np.float32)], axis=1)
        out[b] = xf.T
    return out



def build_G0(nt=1024):
    p = Prog("G0")
    x_d = p.din("x", [D, nt], F32)
    bwT = p.din("bwT", [D, D], F32)
    bb2 = p.din("bb2", [128, NCH], F32)
    a0T = p.din("a0T", [D, 64], F32)
    b0T = p.din("b0T", [D, 1024], F32)
    r0T = p.din("r0T", [D, 16], F32)
    wq0T = p.din("wq0T", [D, D], F32)
    wk0T = p.din("wk0T", [D, D], F32)
    wv0T = p.din("wv0T", [D, D], F32)
    cos0 = p.din("cosd", [64, nt], F32)
    sin0 = p.din("sind", [64, nt], F32)
    x1 = p.dout("x1", [D, nt], F32)
    qF = p.dout("qF", [D, nt], BF16)
    kF = p.dout("kF", [D, nt], BF16)
    vT = p.dout("vT", [nt, D], BF16)
    cm = prewarm(p)
    mark = p.scope_begin()
    emit_l0_ffn(p, cm, x_d, bwT, bb2, a0T, b0T, r0T, x1, nt)
    p.sublayer_end(mark)
    mark = p.scope_begin()
    emit_l0_attn_A(p, cm, x1, wq0T, wk0T, wv0T, cos0, sin0, qF, kF, vT, nt)
    p.sublayer_end(mark)
    return p


def build_G1(nt=1024):
    p = Prog("G1")
    x1 = p.din("x1", [D, nt], F32)
    ao = p.din("ao", [D, nt], BF16)
    wo0T = p.din("wo0T", [D, D], F32)
    bwT = p.din("bwT", [D, D], F32)
    bb2 = p.din("bb2", [128, NCH], F32)
    a0T = p.din("a0T", [D, 64], F32)
    b0T = p.din("b0T", [D, 1024], F32)
    r0T = p.din("r0T", [D, 16], F32)
    a1T = p.din("a1T", [1024, D], F32)
    b1T = p.din("b1T", [D, 1024], F32)
    wr1T = p.din("wr1T", [D, 64], F32)
    wqkv1T = p.din("wqkv1T", [1536, D], F32)
    x4 = p.dout("x4", [D, nt], F32)
    qkv = p.dout("qkv", [1536, nt], BF16)
    x2 = p.dtmp("x2", [D, nt], F32)
    x3 = p.dtmp("x3", [D, nt], F32)
    cm = prewarm(p)
    for fn, args in ((emit_attn_C, (x1, ao, wo0T, x2, NCH, nt)),
                     (emit_l0_ffn, (x2, bwT, bb2, a0T, b0T, r0T, x3, nt)),
                     (emit_l1_ffn, (x3, a1T, b1T, wr1T, x4, nt)),
                     (emit_l1_attn_A1, (x4, wqkv1T, qkv, nt))):
        mark = p.scope_begin()
        fn(p, cm, *args)
        p.sublayer_end(mark)
    return p


def build_G3(nt=1024):
    p = Prog("G3")
    x4 = p.din("x4", [D, nt], F32)
    ao = p.din("ao", [1024, nt], BF16)
    wo1T = p.din("wo1T", [D, 1024], F32)
    a1T = p.din("a1T", [1024, D], F32)
    b1T = p.din("b1T", [D, 1024], F32)
    wr1T = p.din("wr1T", [D, 64], F32)
    a2T = p.din("a2T", [D, 128], F32)
    b2T = p.din("b2T", [D, 128], F32)
    rt2T = p.din("rt2T", [D, 8], F32)
    wqkv2T = p.din("wqkv2T", [6144, D], F32)
    cos2 = p.din("cosd", [64, nt], F32)
    sin2 = p.din("sind", [64, nt], F32)
    x7 = p.dout("x7", [D, nt], F32)
    qF = p.dout("qF", [D, nt], BF16)
    kF = p.dout("kF", [1024, nt], BF16)
    vT = p.dout("vT", [nt, 1024], BF16)
    x5 = p.dtmp("x5", [D, nt], F32)
    x6 = p.dtmp("x6", [D, nt], F32)
    cm = prewarm(p)
    for fn, args in ((emit_attn_C, (x4, ao, wo1T, x5, 8, nt)),
                     (emit_l1_ffn, (x5, a1T, b1T, wr1T, x6, nt)),
                     (emit_l2_ffn, (x6, a2T, b2T, rt2T, x7, nt)),
                     (emit_l2_attn_A, (x7, wqkv2T, cos2, sin2, qF, kF, vT, nt))):
        mark = p.scope_begin()
        fn(p, cm, *args)
        p.sublayer_end(mark)
    return p


def build_G4(nt=1024):
    p = Prog("G4")
    x7 = p.din("x7", [D, nt], F32)
    ao = p.din("ao", [D, nt], BF16)
    wo2T = p.din("wo2T", [D, D], F32)
    a2T = p.din("a2T", [D, 128], F32)
    b2T = p.din("b2T", [D, 128], F32)
    rt2T = p.din("rt2T", [D, 8], F32)
    wlT = p.din("wlT", [D, D], BF16)
    whT = p.din("whT", [D, D], BF16)
    bias2 = p.din("bias2", [128, NCH], F32)
    wq3T = p.din("wq3T", [D, D], F32)
    wkv3T = p.din("wkv3T", [256, D], F32)
    x10 = p.dout("x10", [D, nt], F32)
    qF = p.dout("qF", [D, nt], BF16)
    kF = p.dout("kF", [128, nt], BF16)
    vT = p.dout("vT", [nt, 128], BF16)
    x8 = p.dtmp("x8", [D, nt], F32)
    x9 = p.dtmp("x9", [D, nt], F32)
    cm = prewarm(p)
    for fn, args in ((emit_attn_C, (x7, ao, wo2T, x8, NCH, nt)),
                     (emit_l2_ffn, (x8, a2T, b2T, rt2T, x9, nt)),
                     (emit_l3_ffn, (x9, wlT, whT, bias2, x10, nt)),
                     (emit_l3_attn_A, (x10, wq3T, wkv3T, qF, kF, vT, nt))):
        mark = p.scope_begin()
        fn(p, cm, *args)
        p.sublayer_end(mark)
    return p


def build_G5(nt=1024):
    p = Prog("G5")
    x10 = p.din("x10", [D, nt], F32)
    ao = p.din("ao", [D, nt], BF16)
    wo3T = p.din("wo3T", [D, D], F32)
    wlT = p.din("wlT", [D, D], BF16)
    whT = p.din("whT", [D, D], BF16)
    bias2 = p.din("bias2", [128, NCH], F32)
    xo = p.dout("xo", [D, nt], F32)
    x11 = p.dtmp("x11", [D, nt], F32)
    cm = prewarm(p)
    for fn, args in ((emit_attn_C, (x10, ao, wo3T, x11, NCH, nt)),
                     (emit_l3_ffn, (x11, wlT, whT, bias2, xo, nt))):
        mark = p.scope_begin()
        fn(p, cm, *args)
        p.sublayer_end(mark)
    return p


def kernel_grouped(inputs):
    w = prep_weights(inputs)
    wlT, whT = l3_dequant_host(inputs)
    x = inputs["L0_attn_x"]
    xF = np.ascontiguousarray(x.reshape(TOK, D).T)
    shards = _shard_tokens(xF)

    nc = _get_prog("G0", build_G0)
    base = {"bwT": w["bwT"], "bb2": w["bb2"], "a0T": w["a0T"], "b0T": w["b0T"],
            "r0T": w["r0T"], "wq0T": w["l0_wqT"], "wk0T": w["l0_wkT"],
            "wv0T": w["l0_wvT"]}
    in_maps = []
    for b in range(NB):
        for r in range(2):
            sl = slice(r * 1024, (r + 1) * 1024)
            in_maps.append(dict(base, x=shards[2 * b + r],
                                cosd=np.ascontiguousarray(w["l0_cos"][:, sl]),
                                sind=np.ascontiguousarray(w["l0_sin"][:, sl])))
    res = _run(nc, in_maps)
    x1s = [np.asarray(r_["x1"]) for r_ in res]
    qF = [np.asarray(r_["qF"]) for r_ in res]
    kF = [np.asarray(r_["kF"]) for r_ in res]
    vT = [np.asarray(r_["vT"]) for r_ in res]

    ncB = _get_prog("l0atB", lambda: build_attn_B(16, 128, 1, tag="l0"))
    in_maps = []
    for b in range(NB):
        qb = np.concatenate([qF[2 * b], qF[2 * b + 1]], axis=1)
        kb = np.concatenate([kF[2 * b], kF[2 * b + 1]], axis=1)
        vb = np.concatenate([vT[2 * b], vT[2 * b + 1]], axis=0)
        for s_ in range(2):
            rows = slice(2048 * s_, 2048 * (s_ + 1))
            in_maps.append({"q": np.ascontiguousarray(qb[rows]),
                            "k": np.ascontiguousarray(kb[rows]),
                            "vT": np.ascontiguousarray(vb[:, rows])})
    ao = [np.asarray(r_["ao"]) for r_ in _run(ncB, in_maps)]

    nc = _get_prog("G1", build_G1)
    base = {"wo0T": w["l0_woT"], "bwT": w["bwT"], "bb2": w["bb2"], "a0T": w["a0T"],
            "b0T": w["b0T"], "r0T": w["r0T"], "a1T": w["a1T"], "b1T": w["b1T"],
            "wr1T": w["wr1T"], "wqkv1T": w["l1_wqkvT"]}
    in_maps = []
    for b in range(NB):
        aob = np.concatenate([ao[2 * b], ao[2 * b + 1]], axis=0)
        for r in range(2):
            in_maps.append(dict(base, x1=x1s[2 * b + r],
                                ao=np.ascontiguousarray(aob[:, r * 1024:(r + 1) * 1024])))
    res = _run(nc, in_maps)
    x4s = [np.asarray(r_["x4"]) for r_ in res]
    qkv = [np.asarray(r_["qkv"]) for r_ in res]

    ncA2 = _get_prog("l1atA2", build_l1_attn_A2)
    zh1024 = np.zeros((1024, 4), BF)
    zh256 = np.zeros((256, 4), BF)
    zh128 = np.zeros((128, 1), BF)
    base2 = {"qc1blk": w["l1_qc1blk"], "kc1blk": w["l1_kc1blk"], "qc2T": w["l1_qc2T"],
             "kc2T": w["l1_kc2T"], "qc1b": w["l1_qc1b"], "kc1b": w["l1_kc1b"],
             "qc2b": w["l1_qc2b"], "kc2b": w["l1_kc2b"], "mselk": w["l1_mselk"],
             "mselq": w["l1_mselq"], "hselq": w["l1_hselq"], "hbcq": w["l1_hbcq"],
             "hselk": w["l1_hselk"], "hbck": w["l1_hbck"], "kt": w["l1_kt"]}
    in_maps = []
    for b in range(NB):
        for r in range(2):
            sl = slice(r * 1024, (r + 1) * 1024)
            qkv_m = qkv[2 * b + r]
            qkv_p = qkv[2 * b]
            m = dict(base2,
                     qp=np.ascontiguousarray(qkv_m[0:1024]),
                     kp=np.ascontiguousarray(qkv_m[1024:1280]),
                     vc=np.ascontiguousarray(qkv_m[1280:1408]),
                     vp=np.ascontiguousarray(qkv_m[1408:1536]),
                     qph=(np.ascontiguousarray(qkv_p[0:1024, 1020:1024]) if r else zh1024),
                     kph=(np.ascontiguousarray(qkv_p[1024:1280, 1020:1024]) if r else zh256),
                     vph=(np.ascontiguousarray(qkv_p[1408:1536, 1023:1024]) if r else zh128),
                     cosd=np.ascontiguousarray(w["l1_cos"][:, sl]),
                     sind=np.ascontiguousarray(w["l1_sin"][:, sl]))
            in_maps.append(m)
    resA2 = _run(ncA2, in_maps)
    qh = [np.asarray(r_["qhat"]) for r_ in resA2]
    kh = [np.asarray(r_["khat"]) for r_ in resA2]
    vt1 = [np.asarray(r_["vT"]) for r_ in resA2]

    ncB1 = _get_prog("l1atB", lambda: build_attn_B(8, 64, 4, tag="l1"))
    in_maps = []
    for b in range(NB):
        qb = np.concatenate([qh[2 * b], qh[2 * b + 1]], axis=1)
        kb = np.concatenate([kh[2 * b], kh[2 * b + 1]], axis=1)
        vb = np.concatenate([vt1[2 * b], vt1[2 * b + 1]], axis=0)
        for s_ in range(2):
            in_maps.append({"q": np.ascontiguousarray(qb[512 * s_:512 * (s_ + 1)]),
                            "k": np.ascontiguousarray(kb[128 * s_:128 * (s_ + 1)]),
                            "vT": np.ascontiguousarray(vb[:, 128 * s_:128 * (s_ + 1)])})
    ao1 = [np.asarray(r_["ao"]) for r_ in _run(ncB1, in_maps)]

    nc = _get_prog("G3", build_G3)
    base = {"wo1T": w["l1_woT"], "a1T": w["a1T"], "b1T": w["b1T"], "wr1T": w["wr1T"],
            "a2T": w["a2T"], "b2T": w["b2T"], "rt2T": w["rt2T"],
            "wqkv2T": w["l2_wqkvT"]}
    in_maps = []
    for b in range(NB):
        aob = np.concatenate([ao1[2 * b], ao1[2 * b + 1]], axis=0)
        for r in range(2):
            sl = slice(r * 1024, (r + 1) * 1024)
            in_maps.append(dict(base, x4=x4s[2 * b + r],
                                ao=np.ascontiguousarray(aob[:, r * 1024:(r + 1) * 1024]),
                                cosd=np.ascontiguousarray(w["l2_cos"][:, sl]),
                                sind=np.ascontiguousarray(w["l2_sin"][:, sl])))
    res = _run(nc, in_maps)
    x7s = [np.asarray(r_["x7"]) for r_ in res]
    qF = [np.asarray(r_["qF"]) for r_ in res]
    kF = [np.asarray(r_["kF"]) for r_ in res]
    vT = [np.asarray(r_["vT"]) for r_ in res]

    ncB2 = _get_prog("l2atB", lambda: build_attn_B(16, 128, 4, tag="l2"))
    in_maps = []
    for b in range(NB):
        qb = np.concatenate([qF[2 * b], qF[2 * b + 1]], axis=1)
        kb = np.concatenate([kF[2 * b], kF[2 * b + 1]], axis=1)
        vb = np.concatenate([vT[2 * b], vT[2 * b + 1]], axis=0)
        for s_ in range(2):
            in_maps.append({"q": np.ascontiguousarray(qb[2048 * s_:2048 * (s_ + 1)]),
                            "k": np.ascontiguousarray(kb[512 * s_:512 * (s_ + 1)]),
                            "vT": np.ascontiguousarray(vb[:, 512 * s_:512 * (s_ + 1)])})
    ao2 = [np.asarray(r_["ao"]) for r_ in _run(ncB2, in_maps)]

    nc = _get_prog("G4", build_G4)
    base = {"wo2T": w["l2_woT"], "a2T": w["a2T"], "b2T": w["b2T"], "rt2T": w["rt2T"],
            "wlT": wlT, "whT": whT, "bias2": w["l3_bias2"],
            "wq3T": w["l3_wqT"], "wkv3T": w["l3_wkvT"]}
    in_maps = []
    for b in range(NB):
        aob = np.concatenate([ao2[2 * b], ao2[2 * b + 1]], axis=0)
        for r in range(2):
            in_maps.append(dict(base, x7=x7s[2 * b + r],
                                ao=np.ascontiguousarray(aob[:, r * 1024:(r + 1) * 1024])))
    res = _run(nc, in_maps)
    x10s = [np.asarray(r_["x10"]) for r_ in res]
    qF = [np.asarray(r_["qF"]) for r_ in res]
    kF = [np.asarray(r_["kF"]) for r_ in res]
    vT = [np.asarray(r_["vT"]) for r_ in res]

    ncB3 = _get_prog("l3atB", lambda: build_attn_B(16, 128, 16, tag="l3"))
    in_maps = []
    for b in range(NB):
        qb = np.concatenate([qF[2 * b], qF[2 * b + 1]], axis=1)
        kb = np.ascontiguousarray(np.concatenate([kF[2 * b], kF[2 * b + 1]], axis=1))
        vb = np.ascontiguousarray(np.concatenate([vT[2 * b], vT[2 * b + 1]], axis=0))
        for s_ in range(2):
            in_maps.append({"q": np.ascontiguousarray(qb[2048 * s_:2048 * (s_ + 1)]),
                            "k": kb, "vT": vb})
    ao3 = [np.asarray(r_["ao"]) for r_ in _run(ncB3, in_maps)]

    nc = _get_prog("G5", build_G5)
    base = {"wo3T": w["l3_woT"], "wlT": wlT, "whT": whT, "bias2": w["l3_bias2"]}
    in_maps = []
    for b in range(NB):
        aob = np.concatenate([ao3[2 * b], ao3[2 * b + 1]], axis=0)
        for r in range(2):
            in_maps.append(dict(base, x10=x10s[2 * b + r],
                                ao=np.ascontiguousarray(aob[:, r * 1024:(r + 1) * 1024])))
    res = _run(nc, in_maps)
    xo = np.concatenate([np.asarray(r_["xo"], np.float32) for r_ in res], axis=1)
    return np.ascontiguousarray(xo.T).reshape(NB, S, D)


MODE = "grouped"


def kernel(**inputs):
    if MODE == "mega":
        return kernel_mega(inputs)
    if MODE == "grouped":
        return kernel_grouped(inputs)
    return kernel_unfused(**inputs)


def kernel_unfused(**inputs):
    w = prep_weights(inputs)
    x = inputs["L0_attn_x"]
    xF = np.ascontiguousarray(x.reshape(TOK, D).T)
    xF = l0_ffn_host(xF, w)
    if N_SUB >= 2:
        xF = attn_host(xF, w, "l0_wqT", "l0_wkT", "l0_wvT", "l0_woT",
                       "l0_cos", "l0_sin",
                       _get_prog("l0atA", build_l0_attn_A),
                       _get_prog("l0atB", lambda: build_attn_B(16, 128, 1, tag="l0")),
                       _get_prog("l0atC", lambda: build_attn_C(NCH, tag="l0")))
    if N_SUB >= 3:
        xF = l0_ffn_host(xF, w)
    if N_SUB >= 4:
        xF = l1_ffn_host(xF, w)
    if N_SUB >= 5:
        xF = l1_attn_host(xF, w)
    if N_SUB >= 6:
        xF = l1_ffn_host(xF, w)
    if N_SUB >= 7:
        xF = l2_ffn_host(xF, w)
    if N_SUB >= 8:
        xF = l2_attn_host(xF, w)
    if N_SUB >= 9:
        xF = l2_ffn_host(xF, w)
    if N_SUB >= 10:
        wlT, whT = l3_dequant_host(inputs)
        xF = l3_ffn_host(xF, w, wlT, whT)
    if N_SUB >= 11:
        xF = l3_attn_host(xF, w)
    if N_SUB >= 12:
        xF = l3_ffn_host(xF, w, wlT, whT)
    return np.ascontiguousarray(xF.T).reshape(NB, S, D)



def emit_load_act(p, act_dram, nk, nt, dst_sb, sem_name="la"):
    sL = p.sem(f"{sem_name}_d")
    w = None
    for c in range(nk):
        w = p.dma("sync", dst_sb[:, c * nt:(c + 1) * nt],
                  act_dram[c * 128:(c + 1) * 128, :], sem=sL)
    return sL, w



def emit_proj_resid(p, cm, wT_dram, act_sb, nk, nt, x_d, xo_d, scale,
                    tag, bias_sb=None, bias_scale=1.0):
    ntile = nt // TT
    ws = WStream(p, f"pr_{tag}", wT_dram, nk)
    ps_y = [p.psum(f"pr_ps_{tag}", [128, TT]) for _ in range(2)]
    lastE = [0, 0]
    tl = [p.sbuf(f"pr_tl_{tag}", [128, TT], F32) for _ in range(2)]
    xstage = [p.sbuf(f"pr_xs_{tag}", [128, TT], F32) for _ in range(2)]
    sDi = [p.sem(f"pr_di_{tag}{i}") for i in range(2)]
    sDo = p.sem(f"pr_do_{tag}")
    sV2 = p.sem(f"pr_v_{tag}")
    sT2 = p.sem(f"pr_t_{tag}")
    O = wT_dram.shape[0]
    noc = O // 128
    loads = {0: ws.load(0)}
    if noc > 1:
        loads[1] = ws.load(1)
    it_count = [0]
    outw = {}
    for oc in range(noc):
        b1, g1 = loads[oc]
        if ws.src_bf16:
            p.wait("tensor", g1[0], g1[1])
        else:
            p.wait("tensor", ws.sG, g1)
        for it in range(ntile):
            t0 = it * TT
            i2 = it_count[0] % 2
            it_count[0] += 1
            ps = ps_y[i2]
            if it_count[0] - 1 >= 2:
                p.wait("sync", sDo, outw.get(it_count[0] - 3, 0))
            w = p.dma("sync", xstage[i2][:, :], x_d[oc * 128:(oc + 1) * 128, t0:t0 + TT],
                      sem=sDi[i2])
            p.wait("tensor", sV2, lastE[i2])
            for c in range(nk):
                p.emit("tensor", "matmul", ps[:, :], ws.wb[b1][:, c * 128:(c + 1) * 128],
                       act_sb[:, c * nt + t0:c * nt + t0 + TT], start=(c == 0),
                       stop=(c == nk - 1), sem=(sT2 if c == nk - 1 else None))
            p.wait("vector", sT2, sT2.v)
            p.wait("vector", sDi[i2], sDi[i2].v)
            if bias_sb is not None:
                p.emit("vector", "tensor_scalar", tl[i2][:, :], ps[:, :], scale,
                       bias_sb[:, oc:oc + 1], ALU.mult, ALU.add)
                v = p.emit("vector", "tensor_add", xstage[i2][:, :], xstage[i2][:, :],
                           tl[i2][:, :], sem=sV2)
            else:
                v = p.emit("vector", "scalar_tensor_tensor", out=xstage[i2][:, :],
                           in0=ps[:, :], scalar=scale, in1=xstage[i2][:, :],
                           op0=ALU.mult, op1=ALU.add, sem=sV2)
            lastE[i2] = v
            p.wait("scalar", sV2, v)
            outw[it_count[0] - 1] = p.dma("scalar", xo_d[oc * 128:(oc + 1) * 128, t0:t0 + TT],
                                          xstage[i2][:, :], sem=sDo)
        ws.free((sT2.h, sT2.v))
        if oc + 2 < noc:
            loads[oc + 2] = ws.load(oc + 2)
    p.wait("scalar", sDo, sDo.v)
    p.barrier()



def build_l0_attn_A(nt=1024):
    p = Prog("l0atA")
    x_d = p.din("x", [D, nt], F32)
    wqT = p.din("wqT", [D, D], F32)
    wkT = p.din("wkT", [D, D], F32)
    wvT = p.din("wvT", [D, D], F32)
    cos_d = p.din("cosd", [64, nt], F32)
    sin_d = p.din("sind", [64, nt], F32)
    qF_d = p.dout("qF", [D, nt], BF16)
    kF_d = p.dout("kF", [D, nt], BF16)
    vT_d = p.dout("vT", [nt, D], BF16)
    cm = prewarm(p)
    emit_l0_attn_A(p, cm, x_d, wqT, wkT, wvT, cos_d, sin_d, qF_d, kF_d, vT_d, nt)
    return p


def emit_l0_attn_A(p, cm, x_d, wqT, wkT, wvT, cos_d, sin_d, qF_d, kF_d, vT_d,
                   nt=1024):
    ntile = nt // TT
    xhat = p.sbuf("xhat", [128, NCH * nt], BF16)
    stats = p.sbuf("stats", [1, 4 * nt], F32)
    emit_ln(p, cm, x_d, nt, xhat, stats)

    cosb = p.sbuf("cosb", [64, nt], F32)
    sinb = p.sbuf("sinb", [64, nt], F32)
    sC = p.sem("a0_c")
    p.dma("sync", cosb[:, :], cos_d, sem=sC)
    w = p.dma("sync", sinb[:, :], sin_d, sem=sC)
    p.wait("vector", sC, w)
    p.wait("scalar", sC, w)

    sT = p.sem("a0_t")
    sV = p.sem("a0_v")
    sA = p.sem("a0_a")
    sDo = p.sem("a0_do")
    t1 = p.sbuf("ro_t1", [64, TT], F32)
    t2 = p.sbuf("ro_t2", [64, TT], F32)
    stg = p.sbuf("ro_stg", [128, TT], BF16)
    vstg = p.sbuf("v_stg", [128, TT], BF16)
    vtt = p.sbuf("v_tt", [128, 128], BF16)
    ps_y = [p.psum("a0_ps", [128, TT]) for _ in range(2)]
    ps_tr = p.psum("a0_ptr", [128, 128], BF16)
    lastE = [0, 0]
    lastVV = [0]

    def proj_pass(wT, out_is_rope, out_dram):
        ws = WStream(p, f"a0w{p.uid()}", wT, NCH)
        lds = {0: ws.load(0), 1: ws.load(1)}
        for oc in range(NCH):
            b1, g1 = lds[oc]
            p.wait("tensor", ws.sG, g1)
            for it in range(ntile):
                t0 = it * TT
                pi = (oc * ntile + it) % 2
                ps = ps_y[pi]
                p.wait("tensor", sV, lastE[pi])
                for c in range(NCH):
                    p.emit("tensor", "matmul", ps[:, :], ws.wb[b1][:, c * 128:(c + 1) * 128],
                           xhat[:, c * nt + t0:c * nt + t0 + TT], start=(c == 0),
                           stop=(c == NCH - 1), sem=(sT if c == NCH - 1 else None))
                p.wait("vector", sT, sT.v)
                if out_is_rope:
                    cs = cosb[:, t0:t0 + TT]
                    sn = sinb[:, t0:t0 + TT]
                    pse = ps[0:64, :]
                    pso = ps[64:128, :]
                    p.emit("vector", "tensor_mul", t1[:, :], pse, cs)
                    p.emit("vector", "tensor_mul", t2[:, :], pso, sn)
                    p.emit("vector", "tensor_sub", stg[0:64, :], t1[:, :], t2[:, :])
                    p.emit("vector", "tensor_mul", t1[:, :], pse, sn)
                    p.emit("vector", "tensor_mul", t2[:, :], pso, cs)
                    v = p.emit("vector", "tensor_add", stg[64:128, :], t1[:, :], t2[:, :], sem=sV)
                    lastE[pi] = v
                    p.wait("scalar", sV, v)
                    ow = p.dma("scalar", out_dram[oc * 128:(oc + 1) * 128, t0:t0 + TT],
                               stg[:, :], sem=sDo)
                    p.wait("vector", sDo, ow)
                else:
                    v = p.emit("vector", "tensor_copy", vstg[:, :], ps[:, :], sem=sV)
                    lastE[pi] = v
                    for g in range(TT // 128):
                        p.wait("tensor", sV, v)
                        p.wait("tensor", sV, lastVV[0])
                        tv = p.emit("tensor", "transpose", ps_tr[:, :],
                                    vstg[:, g * 128:(g + 1) * 128], cm.ident_bf[:, :], sem=sT)
                        p.wait("vector", sT, tv)
                        vv = p.emit("vector", "tensor_copy", vtt[:, :], ps_tr[:, :], sem=sV)
                        lastVV[0] = vv
                        p.wait("scalar", sV, vv)
                        ow = p.dma("scalar", vT_d[t0 + g * 128:t0 + (g + 1) * 128,
                                                  oc * 128:(oc + 1) * 128], vtt[:, :], sem=sDo)
                        p.wait("vector", sDo, ow)
            ws.free((sT.h, sT.v))
            if oc + 2 < NCH:
                lds[oc + 2] = ws.load(oc + 2)
        ws.done()

    proj_pass(wqT, True, qF_d)
    proj_pass(wkT, True, kF_d)
    proj_pass(wvT, False, None)
    p.wait("scalar", sDo, sDo.v)
    p.barrier()


def build_attn_C(nk=NCH, nt=1024, tag="c"):
    p = Prog(f"attC_{tag}")
    x_d = p.din("x", [D, nt], F32)
    ao_d = p.din("ao", [nk * 128, nt], BF16)
    woT = p.din("woT", [D, nk * 128], F32)
    xo_d = p.dout("xo", [D, nt], F32)
    cm = prewarm(p)
    emit_attn_C(p, cm, x_d, ao_d, woT, xo_d, nk, nt)
    return p


def emit_attn_C(p, cm, x_d, ao_d, woT, xo_d, nk=NCH, nt=1024):
    act = p.sbuf("act", [128, nk * nt], BF16)
    sL, w = emit_load_act(p, ao_d, nk, nt, act)
    p.wait("tensor", sL, w)
    p.wait("vector", sL, w)
    emit_proj_resid(p, cm, woT, act, nk, nt, x_d, xo_d, 1.0, "c")



def build_l1_ffn(nt=1024):
    p = Prog("l1ffn")
    x_d = p.din("x", [D, nt], F32)
    a1T = p.din("a1T", [1024, D], F32)
    b1T = p.din("b1T", [D, 1024], F32)
    wrT = p.din("wrT", [D, 64], F32)
    xo_d = p.dout("xo", [D, nt], F32)
    cm = prewarm(p)
    emit_l1_ffn(p, cm, x_d, a1T, b1T, wrT, xo_d, nt)
    return p


def emit_l1_ffn(p, cm, x_d, a1T, b1T, wrT, xo_d, nt=1024):
    ntile = nt // TT

    xhat = p.sbuf("xhat", [128, NCH * nt], BF16)
    stats = p.sbuf("stats", [1, 4 * nt], F32)
    emit_ln(p, cm, x_d, nt, xhat, stats)

    sW = p.sem("f1_w")
    casts = []
    wr_f, wr_b = load_const_weight(p, "wr1", wrT, NCH, 64, sW, casts)
    sG = p.sem("f1_g")
    p.wait("gpsimd", sW, sW.v)
    for f, bq, wv_ in casts:
        p.emit("gpsimd", "tensor_copy", bq[:, :], f[:, :], sem=sG)
    p.barrier()

    gf_bf = p.sbuf("gf_bf", [64, nt], BF16)
    hb8 = p.sbuf("hb8", [128, 8 * nt], BF16)
    ghb = p.sbuf("ghb", [128, 8 * nt], BF16)
    selb1 = p.sbuf("selb1", [64, 1024], BF16)
    sI = p.sem("f1_i")
    v = p.emit("vector", "memset", selb1[:, :], 1.0, sem=sI)
    p.wait("gpsimd", sI, v)
    p.emit("gpsimd", "affine_select", selb1[:, :], selb1[:, :],
           [[8, 8], [1, 8], [0, 16]], ALU.is_equal, 0.0, base=0, channel_multiplier=-1,
           sem=sI)
    p.barrier()

    ps_r = p.psum("f1_pr", [64, TT])
    ps_h = p.psum("f1_ph", [128, TT])
    ps_tr = p.psum("f1_ptr", [128, 64])
    ps_g = p.psum("f1_pg", [64, 128])
    ps_ge = p.psum("f1_pge", [128, TT])
    lsb = p.sbuf("lsb", [64, TT], F32)
    ltm = p.sbuf("ltm", [128, 64], F32)
    ex = p.sbuf("ex", [128, 64], F32)
    lx = p.sbuf("lx", [128, 64], F32)
    gtm = p.sbuf("gtm", [128, 64], F32)
    top8a = p.sbuf("top8a", [128, 8], F32)
    top8b = p.sbuf("top8b", [128, 8], F32)
    ssum = p.sbuf("ssum", [128, 1], F32)

    wsa = WStream(p, "a1", a1T, NCH)
    sT = p.sem("f1_t")
    sV = p.sem("f1_v")
    sA = p.sem("f1_a")
    lds = {0: wsa.load(0), 1: wsa.load(1)}
    for oc in range(8):
        b1_, g1 = lds[oc]
        p.wait("tensor", wsa.sG, g1)
        for it in range(ntile):
            t0 = it * TT
            for c in range(NCH):
                p.emit("tensor", "matmul", ps_h[:, :], wsa.wb[b1_][:, c * 128:(c + 1) * 128],
                       xhat[:, c * nt + t0:c * nt + t0 + TT], start=(c == 0),
                       stop=(c == NCH - 1), sem=(sT if c == NCH - 1 else None))
            p.wait("vector", sT, sT.v)
            v = p.emit("vector", "tensor_copy", hb8[:, oc * nt + t0:oc * nt + t0 + TT],
                       ps_h[:, :], sem=sV)
            p.wait("tensor", sV, v)
        wsa.free((sT.h, sT.v))
        if oc + 2 < 8:
            lds[oc + 2] = wsa.load(oc + 2)
    wsa.done()

    for it in range(ntile):
        t0 = it * TT
        for c in range(NCH):
            p.emit("tensor", "matmul", ps_r[:, :], wr_b[:, c * 64:(c + 1) * 64],
                   xhat[:, c * nt + t0:c * nt + t0 + TT], start=(c == 0), stop=(c == NCH - 1),
                   sem=(sT if c == NCH - 1 else None))
        p.wait("vector", sT, sT.v)
        v = p.emit("vector", "tensor_copy", lsb[:, :], ps_r[:, :], sem=sV)
        p.barrier()
        for g in range(TT // 128):
            gt = t0 + g * 128
            v = p.emit("tensor", "transpose", ps_tr[:, :], lsb[0:64, g * 128:(g + 1) * 128],
                       cm.ident_f[0:64, 0:64], sem=sT)
            p.wait("vector", sT, v)
            v = p.emit("vector", "tensor_copy", ltm[:, :], ps_tr[:, 0:64], sem=sV)
            p.emit("vector", "max", out=top8a[:, :], in_=ltm[:, :])
            p.emit("vector", "match_replace", out=lx[:, :], in_to_replace=top8a[:, :],
                   in_values=ltm[:, :], imm_value=-1e30)
            v = p.emit("vector", "max", out=top8b[:, :], in_=lx[:, :], sem=sV)
            p.wait("scalar", sV, v)
            a = p.emit("scalar", "activation", ex[:, :], ltm[:, :], AF.Exp,
                       bias=p.constf(-20.0), scale=1.0, sem=sA)
            p.wait("vector", sA, a)
            p.emit("vector", "tensor_scalar", lx[:, :], ltm[:, :], top8b[:, 7:8], None,
                   ALU.is_ge, ALU.bypass)
            p.emit("vector", "tensor_mul", ex[:, :], ex[:, :], lx[:, :])
            p.emit("vector", "reduce_sum", ssum[:, :], ex[:, :], axis=AX.X)
            p.emit("vector", "reciprocal", ssum[:, :], ssum[:, :])
            v = p.emit("vector", "tensor_scalar", gtm[:, :], ex[:, :], ssum[:, :], None,
                       ALU.mult, ALU.bypass, sem=sV)
            p.wait("tensor", sV, v)
            v = p.emit("tensor", "transpose", ps_g[:, :], gtm[:, :], cm.ident_f[:, :], sem=sT)
            p.wait("vector", sT, v)
            v = p.emit("vector", "tensor_copy", gf_bf[:, gt:gt + 128], ps_g[0:64, :], sem=sV)
            p.wait("tensor", sV, v)
        p.barrier()
        for c in range(8):
            v = p.emit("tensor", "matmul", ps_ge[:, :], selb1[:, c * 128:(c + 1) * 128],
                       gf_bf[:, t0:t0 + TT], start=True, stop=True, sem=sT)
            p.wait("vector", sT, v)
            v = p.emit("vector", "tensor_mul", ghb[:, c * nt + t0:c * nt + t0 + TT],
                       hb8[:, c * nt + t0:c * nt + t0 + TT], ps_ge[:, :], sem=sV)
            p.wait("tensor", sV, v)
        p.barrier()
    p.release(ps_r, ps_h, ps_tr, ps_g, ps_ge, lsb, ltm, ex, lx, gtm, top8a, top8b, ssum)

    emit_proj_resid(p, cm, b1T, ghb, 8, nt, x_d, xo_d, 1.0, "f1")



def build_attn_B(nh, hd, rep, Sq=2048, hpl=None, tag="b"):
    p = Prog(f"attB_{tag}")
    nkv = nh // rep
    q_d = p.din("q", [nh * hd, Sq], BF16)
    k_d = p.din("k", [nkv * hd, Sq], BF16)
    vT_d = p.din("vT", [Sq, nkv * hd], BF16)
    ao_d = p.dout("ao", [nh * hd, Sq], BF16)
    cm = prewarm(p)
    emit_attn_B(p, cm, [(q_d, 0)], [(k_d, 0)], [(vT_d, 0)], ao_d,
                nh, hd, rep, Sq, hpl)
    return p


def emit_attn_B(p, cm, qs, ks, vs, ao_d, nh, hd, rep, Sq=2048, hpl=None):
    nkv = nh // rep
    scale = 1.0 / math.sqrt(hd)
    sc_ap = p.constf(scale)
    if hpl is None:
        hpl = 2 if hd == 128 else nh
    band = p.sbuf("band", [128, 1024], BF16)
    sB = p.sem("b_band")
    v = p.emit("vector", "memset", band[:, :], 1.0, sem=sB)
    p.wait("gpsimd", sB, v)
    p.emit("gpsimd", "affine_select", band[:, :], band[:, :], [[1, 1024]],
           ALU.is_ge, 0.0, base=-512, channel_multiplier=-1, sem=sB)
    p.barrier()

    sD = p.sem("b_d")
    sT = p.sem("b_t")
    sV = p.sem("b_v")
    sA = p.sem("b_a")
    sDo = p.sem("b_do")

    stg1 = [p.sbuf("b_s1", [128, Sq], BF16) for _ in range(2)]
    sq2 = p.sbuf("b_sq2", [128, Sq], BF16)
    red = p.sbuf("b_red", [1, 16], F32)
    negc = p.sbuf("b_negc", [128, 1], F32)
    crow = p.sbuf("b_crow", [1, 1], BF16)
    ps_n = p.psum("b_pn", [1, TT])
    ps_b = p.psum("b_pb", [128, 1])
    nqt = Sq // TT
    rv = 0
    sD1 = [p.sem("b_d1a"), p.sem("b_d1b")]
    chunks = [(qs, c, 0) for c in range(nh * hd // 128)] + \
             [(ks, c, 8) for c in range(nkv * hd // 128)]
    seen = set()
    for gi, (segs, c, ri) in enumerate(chunks):
        b = gi % 2
        if gi >= 2:
            p.wait("sync", sA, sA.v)
        for (seg, off) in segs:
            w = p.dma("sync", stg1[b][:, off:off + seg.shape[1]],
                      seg[c * 128:(c + 1) * 128, :], sem=sD1[b])
        p.wait("scalar", sD1[b], w)
        a = p.emit("scalar", "activation", sq2[:, :], stg1[b][:, :], AF.Square, sem=sA)
        p.wait("tensor", sA, a)
        for it in range(nqt):
            p.wait("tensor", sV, rv)
            p.emit("tensor", "matmul", ps_n[0:1, :], cm.ones128_bf[:, :],
                   sq2[:, it * TT:(it + 1) * TT], start=True, stop=True, sem=sT)
            p.wait("vector", sT, sT.v)
            if ri not in seen:
                seen.add(ri)
                rv = p.emit("vector", "reduce_max", red[0:1, ri:ri + 1], ps_n[0:1, :],
                            axis=AX.X, sem=sV)
            else:
                p.emit("vector", "reduce_max", red[0:1, ri + 1:ri + 2], ps_n[0:1, :],
                       axis=AX.X)
                rv = p.emit("vector", "tensor_max", red[0:1, ri:ri + 1],
                            red[0:1, ri:ri + 1], red[0:1, ri + 1:ri + 2], sem=sV)
        p.wait("scalar", sT, sT.v)
    p.wait("scalar", sV, rv)
    p.emit("scalar", "activation", red[0:1, 2:3], red[0:1, 0:1], AF.Sqrt)
    a = p.emit("scalar", "activation", red[0:1, 3:4], red[0:1, 8:9], AF.Sqrt, sem=sA)
    p.wait("vector", sA, a)
    p.emit("vector", "tensor_mul", red[0:1, 2:3], red[0:1, 2:3], red[0:1, 3:4])
    p.emit("vector", "tensor_scalar_mul", red[0:1, 2:3], red[0:1, 2:3], -scale)
    v = p.emit("vector", "tensor_copy", crow[0:1, 0:1], red[0:1, 2:3], sem=sV)
    p.wait("tensor", sV, v)
    v = p.emit("tensor", "matmul", ps_b[:, :], cm.ones1_bf[0:1, :], crow[0:1, 0:1],
               start=True, stop=True, sem=sT)
    p.wait("vector", sT, v)
    v = p.emit("vector", "tensor_copy", negc[:, :], ps_b[:, :], sem=sV)
    p.wait("scalar", sV, v)
    p.barrier()
    p.release(ps_n, ps_b)

    NSL = 4
    ps_sc = [p.psum("b_psc", [128, TT]) for _ in range(NSL)]
    ps_o = [p.psum("b_po", [128, TT]) for _ in range(2)]
    ps_sr = [p.psum("b_psr", [128, TT]) for _ in range(2)]
    nqch = max(1, (hpl * hd) // 128)
    kvset_sz = max(1, hpl // rep)
    ntk = Sq // 128
    qg = p.sbuf("qg", [128, nqch * Sq], BF16)
    kg = p.sbuf("kg", [128, kvset_sz * Sq], BF16)
    vg = p.sbuf("vg", [128, ntk * kvset_sz * hd], BF16)
    probs = [p.sbuf("b_pr", [128, TT], BF16) for _ in range(NSL)]
    nrm = [p.sbuf("b_nrm", [1, TT], F32) for _ in range(2)]
    nrmb = [p.sbuf("b_nrmb", [1, TT], BF16) for _ in range(2)]
    rb = [p.sbuf("b_rb", [128, TT], F32) for _ in range(2)]
    aostg = [p.sbuf("b_ao", [128, TT], BF16) for _ in range(2)]
    sE = p.sem("b_e")
    sM = p.sem("b_m")
    sPV = p.sem("b_pv")
    sNm = p.sem("b_nm")
    exp_tok = {}
    pv_tok = {}
    last_evac = [0, 0]
    last_rbps = [0, 0]

    n_lg = nh // hpl
    glc = [0, 0]
    for lg in range(n_lg):
        p.wait("sync", sDo, sDo.v)
        h0 = lg * hpl
        kv0 = h0 // rep
        for c in range(nqch):
            for (seg, off) in qs:
                p.dma("sync", qg[:, c * Sq + off:c * Sq + off + seg.shape[1]],
                      seg[(h0 * hd) + c * 128:(h0 * hd) + (c + 1) * 128, :], sem=sD)
        if hd == 128:
            for c in range(kvset_sz):
                for (seg, off) in ks:
                    p.dma("sync", kg[:, c * Sq + off:c * Sq + off + seg.shape[1]],
                          seg[(kv0 + c) * hd:(kv0 + c + 1) * hd, :], sem=sD)
        else:
            for c in range(kvset_sz):
                for (seg, off) in ks:
                    p.dma("sync", kg[0:64, c * Sq + off:c * Sq + off + seg.shape[1]],
                          seg[(kv0 + c) * hd:(kv0 + c + 1) * hd, :], sem=sD)
                    p.dma("sync", kg[64:128, c * Sq + off:c * Sq + off + seg.shape[1]],
                          seg[(kv0 + c) * hd:(kv0 + c + 1) * hd, :], sem=sD)
        for (vseg, r0) in vs:
            ck0 = (r0 // 128) * kvset_sz * hd
            w = p.dma("sync",
                      vg[:, ck0:ck0 + (vseg.shape[0] // 128) * kvset_sz * hd]
                      .rearrange("p (kc c) -> p kc c", c=kvset_sz * hd),
                      vseg.rearrange("(kc p) c -> p kc c", p=128)
                      [:, :, kv0 * hd:(kv0 + kvset_sz) * hd], sem=sD)
        p.wait("scalar", sD, sD.v)
        p.wait("tensor", sD, sD.v)
        p.wait("vector", sD, sD.v)

        for hp in range(hpl // 2):
            lanes = []
            for L in range(2):
                hl = 2 * hp + L
                h = h0 + hl
                lanes.append((h, (hl * hd) // 128, (hl * hd) % 128, (h // rep) - kv0))
            for qt in range(nqt):
                q0 = qt * TT
                nkc = (q0 + TT) // 128

                def emit_qk(L, kc):
                    g = qk_g[(L, kc)]
                    sl = 2 * L + (g % 2)
                    if g >= 2:
                        t = exp_tok[(L, g - 2)]
                        p.wait("tensor", t[0], t[1])
                    h_, qch, qbase, kvl = lanes[L]
                    tok_qk[(L, g)] = p.emit("tensor", "matmul", ps_sc[sl][0:128, :],
                           kg[qbase:qbase + hd, kvl * Sq + kc * 128:kvl * Sq + (kc + 1) * 128],
                           qg[qbase:qbase + hd, qch * Sq + q0:qch * Sq + q0 + TT],
                           start=True, stop=True, sem=sT)

                qk_g = {}
                tok_qk = {}
                for kc in range(nkc):
                    for L in range(2):
                        qk_g[(L, kc)] = glc[L]
                        glc[L] += 1
                for L in range(2):
                    emit_qk(L, 0)
                for kc in range(nkc):
                    j = kc - 4 * qt
                    for L in range(2):
                        g = qk_g[(L, kc)]
                        sl = 2 * L + (g % 2)
                        p.wait("scalar", sT, tok_qk[(L, g)])
                        if g >= 2:
                            p.wait("scalar", sPV, pv_tok[(L, g - 2)])
                        a = p.emit("scalar", "activation", probs[sl][:, :], ps_sc[sl][:, :],
                                   AF.Exp, bias=negc[:, :], scale=sc_ap, sem=sE)
                        tok = (sE, a)
                        if j >= 0:
                            off = 512 - 128 * j
                            p.wait("vector", sE, a)
                            a = p.emit("vector", "tensor_mul", probs[sl][:, :], probs[sl][:, :],
                                       band[:, off:off + TT], sem=sM)
                            tok = (sM, a)
                        exp_tok[(L, g)] = tok
                    for L in range(2):
                        if kc + 1 < nkc:
                            emit_qk(L, kc + 1)
                    for L in range(2):
                        g = qk_g[(L, kc)]
                        sl = 2 * L + (g % 2)
                        kvl = lanes[L][3]
                        tok = exp_tok[(L, g)]
                        p.wait("tensor", tok[0], tok[1])
                        if kc == 0:
                            p.wait("tensor", sNm, last_evac[L])
                        p.emit("tensor", "matmul", ps_sr[L][0:1, :], cm.ones128_bf[:, :],
                               probs[sl][:, :], start=(kc == 0), stop=(kc == nkc - 1))
                        pv_tok[(L, g)] = p.emit("tensor", "matmul", ps_o[L][0:hd, :],
                               vg[:, kc * kvset_sz * hd + kvl * hd:kc * kvset_sz * hd + (kvl + 1) * hd],
                               probs[sl][:, :], start=(kc == 0), stop=(kc == nkc - 1), sem=sPV)
                for L in range(2):
                    h = lanes[L][0]
                    p.wait("vector", sPV, pv_tok[(L, qk_g[(L, nkc - 1)])])
                    p.wait("vector", sNm, last_rbps[L])
                    p.emit("vector", "tensor_copy", nrm[L][0:1, :], ps_sr[L][0:1, :])
                    p.emit("vector", "reciprocal", nrm[L][0:1, :], nrm[L][0:1, :])
                    v = p.emit("vector", "tensor_copy", nrmb[L][0:1, :], nrm[L][0:1, :], sem=sV)
                    p.wait("tensor", sV, v)
                    v = p.emit("tensor", "matmul", ps_sr[L][:, :], cm.ones1_bf[:, :],
                               nrmb[L][0:1, :], start=True, stop=True, sem=sT)
                    p.wait("vector", sT, v)
                    p.emit("vector", "tensor_copy", rb[L][:, :], ps_sr[L][:, :])
                    v = p.emit("vector", "tensor_mul", aostg[L][0:hd, :], ps_o[L][0:hd, :],
                               rb[L][0:hd, :], sem=sNm)
                    last_evac[L] = v
                    p.wait("scalar", sNm, v)
                    ow = p.dma("scalar", ao_d[h * hd:(h + 1) * hd, q0:q0 + TT],
                               aostg[L][0:hd, :], sem=sDo)
                    last_rbps[L] = last_evac[L]
                    p.wait("vector", sDo, ow)
    p.wait("scalar", sDo, sDo.v)
    p.barrier()



def build_l1_attn_A1(nt=1024):
    p = Prog("l1atA1")
    x_d = p.din("x", [D, nt], F32)
    wqkvT = p.din("wqkvT", [1536, D], F32)
    qkv_d = p.dout("qkv", [1536, nt], BF16)
    cm = prewarm(p)
    emit_l1_attn_A1(p, cm, x_d, wqkvT, qkv_d, nt)
    return p


def emit_l1_attn_A1(p, cm, x_d, wqkvT, qkv_d, nt=1024):
    ntile = nt // TT
    xhat = p.sbuf("xhat", [128, NCH * nt], BF16)
    stats = p.sbuf("stats", [1, 4 * nt], F32)
    emit_ln(p, cm, x_d, nt, xhat, stats)
    ws = WStream(p, "qkv1", wqkvT, NCH)
    ps_y = [p.psum("a1_ps", [128, TT]) for _ in range(2)]
    stg = p.sbuf("a1_stg", [128, TT], BF16)
    lastE = [0, 0]
    sT = p.sem("a1_t")
    sV = p.sem("a1_v")
    sDo = p.sem("a1_do")
    lds = {0: ws.load(0), 1: ws.load(1)}
    for oc in range(12):
        b1, g1 = lds[oc]
        p.wait("tensor", ws.sG, g1)
        for it in range(ntile):
            t0 = it * TT
            pi = (oc * ntile + it) % 2
            ps = ps_y[pi]
            p.wait("tensor", sV, lastE[pi])
            for c in range(NCH):
                p.emit("tensor", "matmul", ps[:, :], ws.wb[b1][:, c * 128:(c + 1) * 128],
                       xhat[:, c * nt + t0:c * nt + t0 + TT], start=(c == 0),
                       stop=(c == NCH - 1), sem=(sT if c == NCH - 1 else None))
            p.wait("vector", sT, sT.v)
            v = p.emit("vector", "tensor_copy", stg[:, :], ps[:, :], sem=sV)
            lastE[pi] = v
            p.wait("scalar", sV, v)
            ow = p.dma("scalar", qkv_d[oc * 128:(oc + 1) * 128, t0:t0 + TT], stg[:, :], sem=sDo)
            p.wait("vector", sDo, ow)
        ws.free((sT.h, sT.v))
        if oc + 2 < 12:
            lds[oc + 2] = ws.load(oc + 2)
    p.wait("scalar", sDo, sDo.v)
    p.barrier()



def build_l1_attn_A2(nt=1024):
    p = Prog("l1atA2")
    qp_d = p.din("qp", [1024, nt], BF16)
    qph_d = p.din("qph", [1024, 4], BF16)
    kp_d = p.din("kp", [256, nt], BF16)
    kph_d = p.din("kph", [256, 4], BF16)
    vc_d = p.din("vc", [128, nt], BF16)
    vp_d = p.din("vp", [128, nt], BF16)
    vph_d = p.din("vph", [128, 1], BF16)
    qc1blk_d = p.din("qc1blk", [24 * 128, 128], F32)
    kc1blk_d = p.din("kc1blk", [6 * 128, 128], F32)
    qc2T_d = p.din("qc2T", [1024, 3 * 1024], F32)
    kc2T_d = p.din("kc2T", [256, 3 * 256], F32)
    qc1b_d = p.din("qc1b", [128, 8], F32)
    kc1b_d = p.din("kc1b", [128, 2], F32)
    qc2b_d = p.din("qc2b", [128, 8], F32)
    kc2b_d = p.din("kc2b", [128, 2], F32)
    mselk_d = p.din("mselk", [256, 128], F32)
    mselq_d = p.din("mselq", [1024, 128], F32)
    hselq_d = p.din("hselq", [1024, 16], F32)
    hbcq_d = p.din("hbcq", [16, 1024], F32)
    hselk_d = p.din("hselk", [256, 4], F32)
    hbck_d = p.din("hbck", [4, 256], F32)
    kt_d = p.din("kt", [1, 1], F32)
    cos_d = p.din("cosd", [128, nt], F32)
    sin_d = p.din("sind", [128, nt], F32)
    qhat_d = p.dout("qhat", [1024, nt], BF16)
    khat_d = p.dout("khat", [256, nt], BF16)
    vT_d = p.dout("vT", [nt, 256], BF16)
    cm = prewarm(p)
    emit_l1_attn_A2(p, cm, dict(
        qp=qp_d, qph=qph_d, kp=kp_d, kph=kph_d, vc=vc_d, vp=vp_d, vph=vph_d,
        qc1blk=qc1blk_d, kc1blk=kc1blk_d, qc2T=qc2T_d, kc2T=kc2T_d,
        qc1b=qc1b_d, kc1b=kc1b_d, qc2b=qc2b_d, kc2b=kc2b_d,
        mselk=mselk_d, mselq=mselq_d, hselq=hselq_d, hbcq=hbcq_d,
        hselk=hselk_d, hbck=hbck_d, kt=kt_d, cosd=cos_d, sind=sin_d,
        qhat=qhat_d, khat=khat_d, vT=vT_d), nt)
    return p


def emit_l1_attn_A2(p, cm, t, nt=1024):
    (qp_d, qph_d, kp_d, kph_d, vc_d, vp_d, vph_d, qc1blk_d, kc1blk_d, qc2T_d,
     kc2T_d, qc1b_d, kc1b_d, qc2b_d, kc2b_d, mselk_d, mselq_d, hselq_d,
     hbcq_d, hselk_d, hbck_d, kt_d, cos_d, sin_d, qhat_d, khat_d, vT_d) = (
        t["qp"], t["qph"], t["kp"], t["kph"], t["vc"], t["vp"], t["vph"],
        t["qc1blk"], t["kc1blk"], t["qc2T"], t["kc2T"], t["qc1b"], t["kc1b"],
        t["qc2b"], t["kc2b"], t["mselk"], t["mselq"], t["hselq"], t["hbcq"],
        t["hselk"], t["hbck"], t["kt"], t["cosd"], t["sind"],
        t["qhat"], t["khat"], t["vT"])
    ntile = nt // TT
    EXT = 1032
    EY = 1028

    sW = p.sem("a2_w")
    casts = []
    qc1_f, qc1_b = load_const_weight(p, "qc1", qc1blk_d, 24, 128, sW, casts)
    kc1_f, kc1_b = load_const_weight(p, "kc1", kc1blk_d, 6, 128, sW, casts)
    mselk_f, mselk_b = load_const_weight(p, "mselk", mselk_d, 2, 128, sW, casts)
    mselq_f, mselq_b = load_const_weight(p, "mselq", mselq_d, 8, 128, sW, casts)
    hselq_f, hselq_b = load_const_weight(p, "hselq", hselq_d, 8, 16, sW, casts)
    hselk_f, hselk_b = load_const_weight(p, "hselk", hselk_d, 2, 4, sW, casts)
    hbcq_f = p.sbuf("hbcq_f", [16, 1024], F32)
    hbcq_b = p.sbuf("hbcq_b", [16, 1024], BF16)
    hbck_f = p.sbuf("hbck_f", [4, 256], F32)
    hbck_b = p.sbuf("hbck_b", [4, 256], BF16)
    biases = p.sbuf("biases", [128, 20], F32)
    ktv = p.sbuf("ktv", [1, 1], F32)
    cosb = p.sbuf("cosb", [128, nt], F32)
    sinb = p.sbuf("sinb", [128, nt], F32)
    p.dma("sync", hbcq_f[:, :], hbcq_d, sem=sW)
    p.dma("sync", hbck_f[:, :], hbck_d, sem=sW)
    p.dma("sync", biases[:, 0:8], qc1b_d, sem=sW)
    p.dma("sync", biases[:, 8:10], kc1b_d, sem=sW)
    p.dma("sync", biases[:, 10:18], qc2b_d, sem=sW)
    p.dma("sync", biases[:, 18:20], kc2b_d, sem=sW)
    p.dma("sync", ktv[:, :], kt_d, sem=sW)
    p.dma("sync", cosb[:, :], cos_d, sem=sW)
    p.dma("sync", sinb[:, :], sin_d, sem=sW)
    sG = p.sem("a2_g")
    p.wait("gpsimd", sW, sW.v)
    for f, bq, wv_ in casts:
        p.emit("gpsimd", "tensor_copy", bq[:, :], f[:, :], sem=sG)
    p.emit("gpsimd", "tensor_copy", hbcq_b[:, :], hbcq_f[:, :])
    p.emit("gpsimd", "tensor_copy", hbck_b[:, :], hbck_f[:, :], sem=sG)
    p.barrier()

    qpe = p.sbuf("qpe", [128, 8 * EXT], BF16)
    kpe = p.sbuf("kpe", [128, 2 * EXT], BF16)
    sD = p.sem("a2_d")
    for c in range(8):
        p.dma("sync", qpe[:, c * EXT:c * EXT + 4], qph_d[c * 128:(c + 1) * 128, :], sem=sD)
        p.dma("sync", qpe[:, c * EXT + 4:c * EXT + 4 + nt], qp_d[c * 128:(c + 1) * 128, :], sem=sD)
    for c in range(2):
        p.dma("sync", kpe[:, c * EXT:c * EXT + 4], kph_d[c * 128:(c + 1) * 128, :], sem=sD)
        w = p.dma("sync", kpe[:, c * EXT + 4:c * EXT + 4 + nt], kp_d[c * 128:(c + 1) * 128, :], sem=sD)
    p.wait("tensor", sD, sD.v)
    p.wait("vector", sD, sD.v)
    p.barrier()

    ps_m = p.psum("a2_pm", [128, TT])
    km = p.sbuf("km", [128, nt], F32)
    qm = p.sbuf("qm", [128, nt], F32)
    sT = p.sem("a2_t")
    sV = p.sem("a2_v")
    sA = p.sem("a2_a")
    for it in range(ntile):
        t0 = it * TT
        for c in range(2):
            p.emit("tensor", "matmul", ps_m[:, :], mselk_b[:, c * 128:(c + 1) * 128],
                   kpe[:, c * EXT + 4 + t0:c * EXT + 4 + t0 + TT], start=(c == 0),
                   stop=(c == 1), sem=(sT if c == 1 else None))
        p.wait("vector", sT, sT.v)
        v = p.emit("vector", "tensor_copy", km[:, t0:t0 + TT], ps_m[:, :], sem=sV)
        p.wait("tensor", sV, v)
        for c in range(8):
            p.emit("tensor", "matmul", ps_m[:, :], mselq_b[:, c * 128:(c + 1) * 128],
                   qpe[:, c * EXT + 4 + t0:c * EXT + 4 + t0 + TT], start=(c == 0),
                   stop=(c == 7), sem=(sT if c == 7 else None))
        p.wait("vector", sT, sT.v)
        v = p.emit("vector", "tensor_copy", qm[:, t0:t0 + TT], ps_m[:, :], sem=sV)
        p.wait("tensor", sV, v)
    p.barrier()

    y1q = p.sbuf("y1q", [128, 8 * EY], BF16)
    y1k = p.sbuf("y1k", [128, 2 * EY], BF16)
    ps_c = p.psum("a2_pc", [128, TT])
    ps_c2 = p.psum("a2_pc2", [128, 4])
    for (nch, xe, y1, blk, boff) in ((8, qpe, y1q, qc1_b, 0), (2, kpe, y1k, kc1_b, 8)):
        for c in range(nch):
            for d in range(3):
                p.emit("tensor", "matmul", ps_c2[:, 0:2],
                       blk[:, (d * nch + c) * 128:(d * nch + c + 1) * 128],
                       xe[:, c * EXT + d:c * EXT + d + 2], start=(d == 0), stop=(d == 2),
                       sem=(sT if d == 2 else None))
            p.wait("vector", sT, sT.v)
            v = p.emit("vector", "tensor_scalar", y1[:, c * EY:c * EY + 2], ps_c2[:, 0:2],
                       biases[:, boff + c:boff + c + 1], None, ALU.add, ALU.bypass, sem=sV)
            p.wait("tensor", sV, v)
            for it in range(ntile):
                t0 = it * TT
                for d in range(3):
                    p.emit("tensor", "matmul", ps_c[:, :],
                           blk[:, (d * nch + c) * 128:(d * nch + c + 1) * 128],
                           xe[:, c * EXT + t0 + d + 2:c * EXT + t0 + d + 2 + TT],
                           start=(d == 0), stop=(d == 2), sem=(sT if d == 2 else None))
                p.wait("vector", sT, sT.v)
                v = p.emit("vector", "tensor_scalar", y1[:, c * EY + 2 + t0:c * EY + 2 + t0 + TT],
                           ps_c[:, :], biases[:, boff + c:boff + c + 1], None,
                           ALU.add, ALU.bypass, sem=sV)
                p.wait("tensor", sV, v)
    p.barrier()

    qeff = p.sbuf("qeff", [128, 8 * nt], F32)
    keff = p.sbuf("keff", [128, 2 * nt], F32)
    tl2 = p.sbuf("a2_tl", [128, TT], F32)
    wsq2 = WStream(p, "qc2", qc2T_d, 24)
    lds = {0: wsq2.load(0), 1: wsq2.load(1)}
    for oc in range(8):
        b1, g1 = lds[oc]
        p.wait("tensor", wsq2.sG, g1)
        for it in range(ntile):
            t0 = it * TT
            for d in range(3):
                for kc in range(8):
                    i = d * 8 + kc
                    p.emit("tensor", "matmul", ps_c[:, :],
                           wsq2.wb[b1][:, i * 128:(i + 1) * 128],
                           y1q[:, kc * EY + t0 + d:kc * EY + t0 + d + TT],
                           start=(i == 0), stop=(i == 23), sem=(sT if i == 23 else None))
            p.wait("vector", sT, sT.v)
            p.emit("vector", "tensor_scalar", tl2[:, :], ps_c[:, :],
                   biases[:, 10 + oc:11 + oc], None, ALU.add, ALU.bypass)
            v = p.emit("vector", "tensor_add", tl2[:, :], tl2[:, :], km[:, t0:t0 + TT])
            v = p.emit("vector", "tensor_copy", qeff[:, oc * nt + t0:oc * nt + t0 + TT],
                       tl2[:, :], sem=sV)
            p.wait("tensor", sV, v)
        wsq2.free((sT.h, sT.v))
        if oc + 2 < 8:
            lds[oc + 2] = wsq2.load(oc + 2)
    wsq2.done()
    wsk2 = WStream(p, "kc2", kc2T_d, 6)
    lds = {0: wsk2.load(0), 1: wsk2.load(1)}
    for oc in range(2):
        b1, g1 = lds[oc]
        p.wait("tensor", wsk2.sG, g1)
        for it in range(ntile):
            t0 = it * TT
            for d in range(3):
                for kc in range(2):
                    i = d * 2 + kc
                    p.emit("tensor", "matmul", ps_c[:, :],
                           wsk2.wb[b1][:, i * 128:(i + 1) * 128],
                           y1k[:, kc * EY + t0 + d:kc * EY + t0 + d + TT],
                           start=(i == 0), stop=(i == 5), sem=(sT if i == 5 else None))
            p.wait("vector", sT, sT.v)
            p.emit("vector", "tensor_scalar", tl2[:, :], ps_c[:, :],
                   biases[:, 18 + oc:19 + oc], None, ALU.add, ALU.bypass)
            p.emit("vector", "tensor_add", tl2[:, :], tl2[:, :], qm[:, t0:t0 + TT])
            v = p.emit("vector", "tensor_copy", keff[:, oc * nt + t0:oc * nt + t0 + TT],
                       tl2[:, :], sem=sV)
            p.wait("tensor", sV, v)
        wsk2.free((sT.h, sT.v))
    wsk2.done()
    p.barrier()

    sq2 = p.sbuf("a2_sq", [128, TT], BF16)
    ps_n = p.psum("a2_pn", [16, TT])
    ps_e = p.psum("a2_pe", [128, TT])
    rq = p.sbuf("rq", [16, nt], F32)
    rqb = p.sbuf("rqb", [16, nt], BF16)
    rk = p.sbuf("rk", [4, nt], F32)
    rkb = p.sbuf("rkb", [4, nt], BF16)
    ktc = p.sbuf("ktc", [4, 1], F32)
    ktb = p.sbuf("ktb", [1, 1], BF16)
    ps_kt = p.psum("a2_pkt", [4, 1])
    t1 = p.sbuf("a2_t1", [64, TT], F32)
    t2 = p.sbuf("a2_t2", [64, TT], F32)
    stg = p.sbuf("a2_stg", [128, TT], BF16)
    sDo = p.sem("a2_do")
    eps12 = p.constf(1e-12)
    v = p.emit("vector", "tensor_copy", ktb[:, :], ktv[:, :], sem=sV)
    p.wait("tensor", sV, v)
    v = p.emit("tensor", "matmul", ps_kt[:, :], cm.ones1_bf[0:1, 0:4], ktb[0:1, 0:1],
               start=True, stop=True, sem=sT)
    p.wait("vector", sT, v)
    v = p.emit("vector", "tensor_copy", ktc[:, :], ps_kt[:, :], sem=sV)

    for (nch, eff, rr, rrb, hsel, hbc, nhh, outd) in (
            (8, qeff, rq, rqb, hselq_b, hbcq_b, 16, qhat_d),
            (2, keff, rk, rkb, hselk_b, hbck_b, 4, khat_d)):
        for it in range(ntile):
            t0 = it * TT
            for c in range(nch):
                if c > 0:
                    p.wait("scalar", sT, sT.v)
                a = p.emit("scalar", "activation", sq2[:, :],
                           eff[:, c * nt + t0:c * nt + t0 + TT], AF.Square, sem=sA)
                p.wait("tensor", sA, a)
                p.emit("tensor", "matmul", ps_n[0:nhh, :],
                       hsel[:, c * nhh:(c + 1) * nhh], sq2[:, :],
                       start=(c == 0), stop=(c == nch - 1), sem=sT)
            p.wait("vector", sT, sT.v)
            p.emit("vector", "tensor_copy", rr[0:nhh, t0:t0 + TT], ps_n[0:nhh, :], sem=sV)
            p.wait("scalar", sV, sV.v)
            a = p.emit("scalar", "activation", rr[0:nhh, t0:t0 + TT], rr[0:nhh, t0:t0 + TT],
                       AF.Sqrt, sem=sA)
            p.wait("vector", sA, a)
            p.emit("vector", "tensor_scalar", rr[0:nhh, t0:t0 + TT], rr[0:nhh, t0:t0 + TT],
                   eps12[0:nhh, :], None, ALU.max, ALU.bypass)
            p.emit("vector", "reciprocal", rr[0:nhh, t0:t0 + TT], rr[0:nhh, t0:t0 + TT])
            if rr is rk:
                p.emit("vector", "tensor_scalar", rr[0:nhh, t0:t0 + TT], rr[0:nhh, t0:t0 + TT],
                       ktc[0:4, :], None, ALU.mult, ALU.bypass)
            v = p.emit("vector", "tensor_copy", rrb[0:nhh, t0:t0 + TT], rr[0:nhh, t0:t0 + TT], sem=sV)
            p.wait("tensor", sV, v)
            for c in range(nch):
                v = p.emit("tensor", "matmul", ps_e[:, :], hbc[0:nhh, c * 128:(c + 1) * 128],
                           rrb[0:nhh, t0:t0 + TT], start=True, stop=True, sem=sT)
                p.wait("vector", sT, v)
                v = p.emit("vector", "tensor_mul", stg[:, :],
                           eff[:, c * nt + t0:c * nt + t0 + TT], ps_e[:, :], sem=sV)
                cse = cosb[0:64, t0:t0 + TT]
                cso = cosb[64:128, t0:t0 + TT]
                sne = sinb[0:64, t0:t0 + TT]
                sno = sinb[64:128, t0:t0 + TT]
                p.emit("vector", "tensor_mul", t1[:, :], stg[0:64, :], cse)
                p.emit("vector", "tensor_mul", t2[:, :], stg[64:128, :], sno)
                p.emit("vector", "tensor_sub", t1[:, :], t1[:, :], t2[:, :])
                p.emit("vector", "tensor_mul", t2[:, :], stg[0:64, :], sne)
                v2 = p.emit("vector", "tensor_copy", stg[0:64, :], t1[:, :], sem=sV)
                p.emit("vector", "tensor_mul", t1[:, :], stg[64:128, :], cso)
                p.emit("vector", "tensor_add", t1[:, :], t1[:, :], t2[:, :])
                v2 = p.emit("vector", "tensor_copy", stg[64:128, :], t1[:, :], sem=sV)
                p.wait("scalar", sV, v2)
                ow = p.dma("scalar", outd[c * 128:(c + 1) * 128, t0:t0 + TT], stg[:, :], sem=sDo)
                p.wait("vector", sDo, ow)
                p.wait("tensor", sDo, ow)
    p.barrier()

    vv = p.sbuf("a2_vv", [128, nt], BF16)
    vcs = p.sbuf("a2_vc", [128, nt], BF16)
    vtt = p.sbuf("a2_vtt", [128, 128], BF16)
    ps_tr = p.psum("a2_ptr", [128, 128], BF16)
    sD2 = p.sem("a2_d2")
    p.dma("sync", vcs[:, :], vc_d, sem=sD2)
    p.dma("sync", vv[:, 0:1], vph_d, sem=sD2, allow_slow_non_contiguous=True)
    w = p.dma("sync", vv[:, 1:nt], vp_d[:, 0:nt - 1], sem=sD2)
    p.wait("tensor", sD2, w)
    lastVV = [0]
    for ci, vsrc in ((0, vcs), (1, vv)):
        for g in range(nt // 128):
            p.wait("tensor", sV, lastVV[0])
            tv = p.emit("tensor", "transpose", ps_tr[:, :], vsrc[:, g * 128:(g + 1) * 128],
                        cm.ident_bf[:, :], sem=sT)
            p.wait("vector", sT, tv)
            v = p.emit("vector", "tensor_copy", vtt[:, :], ps_tr[:, :], sem=sV)
            lastVV[0] = v
            p.wait("scalar", sV, v)
            ow = p.dma("scalar", vT_d[g * 128:(g + 1) * 128, ci * 128:(ci + 1) * 128],
                       vtt[:, :], sem=sDo)
            p.wait("vector", sDo, ow)
    p.wait("scalar", sDo, sDo.v)
    p.barrier()



def build_l2_ffn(nt=1024):
    p = Prog("l2ffn")
    x_d = p.din("x", [D, nt], F32)
    a2T = p.din("a2T", [D, 128], F32)
    b2T = p.din("b2T", [D, 128], F32)
    rtT = p.din("rtT", [D, 8], F32)
    xo_d = p.dout("xo", [D, nt], F32)
    cm = prewarm(p)
    emit_l2_ffn(p, cm, x_d, a2T, b2T, rtT, xo_d, nt)
    return p


def emit_l2_ffn(p, cm, x_d, a2T, b2T, rtT, xo_d, nt=1024):
    ntile = nt // TT
    xhat = p.sbuf("xhat", [128, NCH * nt], BF16)
    stats = p.sbuf("stats", [1, 4 * nt], F32)
    emit_ln(p, cm, x_d, nt, xhat, stats)

    sW = p.sem("f2_w")
    casts = []
    rt_f, rt_b = load_const_weight(p, "rt2", rtT, NCH, 8, sW, casts)
    a2_f, a2_b = load_const_weight(p, "a2w", a2T, NCH, 128, sW, casts)
    sG = p.sem("f2_g")
    p.wait("gpsimd", sW, sW.v)
    for f, bq, wv_ in casts:
        p.emit("gpsimd", "tensor_copy", bq[:, :], f[:, :], sem=sG)
    p.barrier()

    whb = p.sbuf("whb", [128, nt], BF16)
    selb2 = p.sbuf("selb2", [8, 128], BF16)
    rwf = p.sbuf("rwf", [8, nt], BF16)
    sI = p.sem("f2_i")
    v = p.emit("vector", "memset", selb2[:, :], 1.0, sem=sI)
    p.wait("gpsimd", sI, v)
    p.emit("gpsimd", "affine_select", selb2[:, :], selb2[:, :], [[1, 8], [0, 16]],
           ALU.is_equal, 0.0, base=0, channel_multiplier=-1, sem=sI)
    p.barrier()

    ps_r = p.psum("f2_pr", [8, TT])
    ps_h = p.psum("f2_ph", [128, TT])
    ps_tr = p.psum("f2_ptr", [128, 8])
    ps_g = p.psum("f2_pg", [8, 128])
    ps_ge = p.psum("f2_pge", [128, TT])
    lsb = p.sbuf("lsb2", [8, TT], F32)
    ltm = p.sbuf("ltm2", [128, 8], F32)
    rwtm = p.sbuf("rwtm", [128, 8], F32)
    mx = p.sbuf("mx2", [128, 1], F32)
    hb = p.sbuf("hb2", [128, nt], BF16)
    sT = p.sem("f2_t")
    sV = p.sem("f2_v")
    for it in range(ntile):
        t0 = it * TT
        for c in range(NCH):
            p.emit("tensor", "matmul", ps_r[:, :], rt_b[:, c * 8:(c + 1) * 8],
                   xhat[:, c * nt + t0:c * nt + t0 + TT], start=(c == 0), stop=(c == NCH - 1))
        for c in range(NCH):
            p.emit("tensor", "matmul", ps_h[:, :], a2_b[:, c * 128:(c + 1) * 128],
                   xhat[:, c * nt + t0:c * nt + t0 + TT], start=(c == 0), stop=(c == NCH - 1),
                   sem=sT)
        p.wait("vector", sT, sT.v)
        p.emit("vector", "tensor_copy", lsb[:, :], ps_r[:, :])
        v = p.emit("vector", "tensor_copy", hb[:, t0:t0 + TT], ps_h[:, :], sem=sV)
        p.wait("tensor", sV, v)
        p.barrier()
        for g in range(TT // 128):
            gt = t0 + g * 128
            v = p.emit("tensor", "transpose", ps_tr[:, :], lsb[0:8, g * 128:(g + 1) * 128],
                       cm.ident_f[0:8, 0:8], sem=sT)
            p.wait("vector", sT, v)
            v = p.emit("vector", "tensor_copy", ltm[:, :], ps_tr[:, 0:8], sem=sV)
            p.wait("vector", sV, v)
            p.emit("vector", "reduce_max", mx[:, :], ltm[:, :], axis=AX.X)
            v = p.emit("vector", "tensor_scalar", rwtm[:, :], ltm[:, :], mx[:, :], mx[:, :],
                       ALU.is_ge, ALU.mult, sem=sV)
            p.wait("tensor", sV, v)
            v = p.emit("tensor", "transpose", ps_g[:, :], rwtm[:, :], cm.ident_f[:, :], sem=sT)
            p.wait("vector", sT, v)
            v = p.emit("vector", "tensor_copy", rwf[:, gt:gt + 128], ps_g[0:8, :], sem=sV)
            p.wait("tensor", sV, v)
        p.barrier()
        v = p.emit("tensor", "matmul", ps_ge[:, :], selb2[:, :], rwf[:, t0:t0 + TT],
                   start=True, stop=True, sem=sT)
        p.wait("vector", sT, v)
        v = p.emit("vector", "tensor_mul", whb[:, t0:t0 + TT], hb[:, t0:t0 + TT],
                   ps_ge[:, :], sem=sV)
        p.wait("tensor", sV, v)
    p.barrier()
    emit_proj_resid(p, cm, b2T, whb, 1, nt, x_d, xo_d, 0.5, "f2")



def build_l2_attn_A(nt=1024):
    p = Prog("l2atA")
    x_d = p.din("x", [D, nt], F32)
    wqkvT = p.din("wqkvT", [6144, D], F32)
    cos_d = p.din("cosd", [64, nt], F32)
    sin_d = p.din("sind", [64, nt], F32)
    qF_d = p.dout("qF", [D, nt], BF16)
    kF_d = p.dout("kF", [1024, nt], BF16)
    vT_d = p.dout("vT", [nt, 1024], BF16)
    cm = prewarm(p)
    emit_l2_attn_A(p, cm, x_d, wqkvT, cos_d, sin_d, qF_d, kF_d, vT_d, nt)
    return p


def emit_l2_attn_A(p, cm, x_d, wqkvT, cos_d, sin_d, qF_d, kF_d, vT_d, nt=1024):
    ntile = nt // TT
    xhat = p.sbuf("xhat", [128, NCH * nt], BF16)
    stats = p.sbuf("stats", [1, 4 * nt], F32)
    emit_ln(p, cm, x_d, nt, xhat, stats)

    cosb = p.sbuf("cosb", [64, nt], F32)
    sinb = p.sbuf("sinb", [64, nt], F32)
    sC = p.sem("a2l_c")
    p.dma("sync", cosb[:, :], cos_d, sem=sC)
    w = p.dma("sync", sinb[:, :], sin_d, sem=sC)
    p.wait("vector", sC, w)
    p.wait("scalar", sC, w)

    sT = p.sem("a2l_t")
    sV = p.sem("a2l_v")
    sDo = p.sem("a2l_do")
    t1 = p.sbuf("ro2_t1", [64, TT], F32)
    t2 = p.sbuf("ro2_t2", [64, TT], F32)
    stg = p.sbuf("ro2_stg", [128, TT], BF16)
    vstg = p.sbuf("v2_stg", [128, TT], BF16)
    vtt = p.sbuf("v2_tt", [128, 128], BF16)
    ps_y = [p.psum("a2l_ps", [128, TT]) for _ in range(2)]
    ps_tr = p.psum("a2l_ptr", [128, 128], BF16)
    lastE = [0, 0]
    lastVV = [0]

    ws = WStream(p, "qkv2", wqkvT, NCH)
    lds = {0: ws.load(0), 1: ws.load(1)}
    for oc in range(48):
        b1, g1 = lds[oc]
        p.wait("tensor", ws.sG, g1)
        for it in range(ntile):
            t0 = it * TT
            pi = (oc * ntile + it) % 2
            ps = ps_y[pi]
            p.wait("tensor", sV, lastE[pi])
            for c in range(NCH):
                p.emit("tensor", "matmul", ps[:, :], ws.wb[b1][:, c * 128:(c + 1) * 128],
                       xhat[:, c * nt + t0:c * nt + t0 + TT], start=(c == 0),
                       stop=(c == NCH - 1), sem=(sT if c == NCH - 1 else None))
            p.wait("vector", sT, sT.v)
            if oc < 40:
                cs = cosb[:, t0:t0 + TT]
                sn = sinb[:, t0:t0 + TT]
                pse = ps[0:64, :]
                pso = ps[64:128, :]
                p.emit("vector", "tensor_mul", t1[:, :], pse, cs)
                p.emit("vector", "tensor_mul", t2[:, :], pso, sn)
                p.emit("vector", "tensor_sub", stg[0:64, :], t1[:, :], t2[:, :])
                p.emit("vector", "tensor_mul", t1[:, :], pse, sn)
                p.emit("vector", "tensor_mul", t2[:, :], pso, cs)
                v = p.emit("vector", "tensor_add", stg[64:128, :], t1[:, :], t2[:, :], sem=sV)
                lastE[pi] = v
                p.wait("scalar", sV, v)
                if oc < 32:
                    ow = p.dma("scalar", qF_d[oc * 128:(oc + 1) * 128, t0:t0 + TT],
                               stg[:, :], sem=sDo)
                else:
                    ow = p.dma("scalar", kF_d[(oc - 32) * 128:(oc - 31) * 128, t0:t0 + TT],
                               stg[:, :], sem=sDo)
                p.wait("vector", sDo, ow)
            else:
                v = p.emit("vector", "tensor_copy", vstg[:, :], ps[:, :], sem=sV)
                lastE[pi] = v
                for g in range(TT // 128):
                    p.wait("tensor", sV, v)
                    p.wait("tensor", sV, lastVV[0])
                    tv = p.emit("tensor", "transpose", ps_tr[:, :],
                                vstg[:, g * 128:(g + 1) * 128], cm.ident_bf[:, :], sem=sT)
                    p.wait("vector", sT, tv)
                    vv = p.emit("vector", "tensor_copy", vtt[:, :], ps_tr[:, :], sem=sV)
                    lastVV[0] = vv
                    p.wait("scalar", sV, vv)
                    ow = p.dma("scalar", vT_d[t0 + g * 128:t0 + (g + 1) * 128,
                                              (oc - 40) * 128:(oc - 39) * 128], vtt[:, :], sem=sDo)
                    p.wait("vector", sDo, ow)
        ws.free((sT.h, sT.v))
        if oc + 2 < 48:
            lds[oc + 2] = ws.load(oc + 2)
    p.wait("scalar", sDo, sDo.v)
    p.barrier()



def build_l3_attn_A(nt=1024):
    p = Prog("l3atA")
    x_d = p.din("x", [D, nt], F32)
    wqT = p.din("wqT", [D, D], F32)
    wkvT = p.din("wkvT", [256, D], F32)
    qF_d = p.dout("qF", [D, nt], BF16)
    kF_d = p.dout("kF", [128, nt], BF16)
    vT_d = p.dout("vT", [nt, 128], BF16)
    cm = prewarm(p)
    emit_l3_attn_A(p, cm, x_d, wqT, wkvT, qF_d, kF_d, vT_d, nt)
    return p


def emit_l3_attn_A(p, cm, x_d, wqT, wkvT, qF_d, kF_d, vT_d, nt=1024):
    ntile = nt // TT
    xhat = p.sbuf("xhat", [128, NCH * nt], BF16)
    stats = p.sbuf("stats", [1, 4 * nt], F32)
    emit_ln(p, cm, x_d, nt, xhat, stats)

    sT = p.sem("a3_t")
    sV = p.sem("a3_v")
    sDo = p.sem("a3_do")
    stg = p.sbuf("a3_stg", [128, TT], BF16)
    vtt = p.sbuf("a3_vtt", [128, 128], BF16)
    ps_y = [p.psum("a3_ps", [128, TT]) for _ in range(2)]
    ps_tr = p.psum("a3_ptr", [128, 128], BF16)
    lastE = [0, 0]
    lastVV = [0]

    ws = WStream(p, "wq3", wqT, NCH)
    wskv = WStream(p, "wkv3", wkvT, NCH)

    def _ld(oc):
        return ws.load(oc) if oc < 32 else wskv.load(oc - 32)
    lds = {0: _ld(0), 1: _ld(1)}
    for oc in range(34):
        st = ws if oc < 32 else wskv
        b1, g1 = lds[oc]
        p.wait("tensor", st.sG, g1)
        wb = st.wb[b1]
        for it in range(ntile):
            t0 = it * TT
            pi = (oc * ntile + it) % 2
            ps = ps_y[pi]
            p.wait("tensor", sV, lastE[pi])
            for c in range(NCH):
                p.emit("tensor", "matmul", ps[:, :], wb[:, c * 128:(c + 1) * 128],
                       xhat[:, c * nt + t0:c * nt + t0 + TT], start=(c == 0),
                       stop=(c == NCH - 1), sem=(sT if c == NCH - 1 else None))
            p.wait("vector", sT, sT.v)
            if oc < 33:
                v = p.emit("vector", "tensor_copy", stg[:, :], ps[:, :], sem=sV)
                lastE[pi] = v
                p.wait("scalar", sV, v)
                dst = qF_d[oc * 128:(oc + 1) * 128, t0:t0 + TT] if oc < 32 else kF_d[:, t0:t0 + TT]
                ow = p.dma("scalar", dst, stg[:, :], sem=sDo)
                p.wait("vector", sDo, ow)
            else:
                v = p.emit("vector", "tensor_copy", stg[:, :], ps[:, :], sem=sV)
                lastE[pi] = v
                for g in range(TT // 128):
                    p.wait("tensor", sV, v)
                    p.wait("tensor", sV, lastVV[0])
                    tv = p.emit("tensor", "transpose", ps_tr[:, :],
                                stg[:, g * 128:(g + 1) * 128], cm.ident_bf[:, :], sem=sT)
                    p.wait("vector", sT, tv)
                    vv = p.emit("vector", "tensor_copy", vtt[:, :], ps_tr[:, :], sem=sV)
                    lastVV[0] = vv
                    p.wait("scalar", sV, vv)
                    ow = p.dma("scalar", vT_d[t0 + g * 128:t0 + (g + 1) * 128, :],
                               vtt[:, :], sem=sDo)
                    p.wait("vector", sDo, ow)
        st.free((sT.h, sT.v))
        if oc + 2 < 34:
            lds[oc + 2] = _ld(oc + 2)
    p.wait("scalar", sDo, sDo.v)
    p.barrier()



def build_l3_dequant(nrow=512):
    p = Prog("l3dq")
    qw_d = p.din("qw", [8 * nrow, 128], I32)
    lut4_d = p.din("lut4", [nrow, 16], F32)
    lut8_d = p.din("lut8", [nrow, 256], F32)
    wlT_d = p.dout("wlT", [D, nrow], BF16)
    whT_d = p.dout("whT", [D, nrow], BF16)
    cm = prewarm(p)
    emit_l3_dequant(p, cm, qw_d, lut4_d, lut8_d, wlT_d, whT_d, nrow)
    return p


def emit_l3_dequant(p, cm, qw_d, lut4_d, lut8_d, wlT_d, whT_d, nrow=512):
    nchk = nrow // 128
    qwp = p.sbuf("qwp", [128, 8 * 128], I32)
    idx = p.sbuf("idx", [128, D], I32)
    tmpb = p.sbuf("dq_tmpb", [128, 32 * 128], I32)
    idxb = p.sbuf("idxb", [128, D], BF16)
    acc = p.sbuf("dq_acc", [128, D], BF16)
    cands = [p.sbuf("dq_c", [128, D], BF16) for _ in range(6)]
    lut8s = p.sbuf("lut8s", [128, 256], F32)
    lut4s = p.sbuf("lut4s", [128, 16], F32)
    trr = p.sbuf("dq_tr", [128, 128], BF16)
    sD = p.sem("dq_d")
    sT = p.sem("dq_t")
    sV = p.sem("dq_v")
    sG = p.sem("dq_g")
    sA = p.sem("dq_a")
    sDo = p.sem("dq_do")
    sVX = p.sem("dq_vx")
    sGX = p.sem("dq_gx")

    for ck in range(nchk):
        p.barrier()

        w = None
        for pl in range(8):
            w = p.dma("sync", qwp[:, pl * 128:(pl + 1) * 128],
                      qw_d[pl * nrow + ck * 128:pl * nrow + (ck + 1) * 128, :], sem=sD)
        p.dma("sync", lut8s[:, :], lut8_d[ck * 128:(ck + 1) * 128, :], sem=sD)
        w = p.dma("sync", lut4s[:, :], lut4_d[ck * 128:(ck + 1) * 128, :], sem=sD)
        p.wait("vector", sD, sD.v)
        sR = p.sem("dq_r")
        for pl in range(8):
            for s in range(32):
                sh = s - pl
                op0 = ALU.logical_shift_right if sh >= 0 else ALU.logical_shift_left
                p.emit("vector", "tensor_scalar", tmpb[:, s * 128:(s + 1) * 128],
                       qwp[:, pl * 128:(pl + 1) * 128], abs(sh), 1 << pl, op0,
                       ALU.bitwise_and, nochain=True,
                       sem=(sR if s == 31 else None))
            p.wait("vector", sR, sR.v)
            for s in range(32):
                if pl == 0:
                    p.emit("vector", "tensor_copy", idx[:, s::32],
                           tmpb[:, s * 128:(s + 1) * 128], nochain=True,
                           sem=(sR if s == 31 else None))
                else:
                    p.emit("vector", "tensor_tensor", idx[:, s::32], idx[:, s::32],
                           tmpb[:, s * 128:(s + 1) * 128], ALU.bitwise_or, nochain=True,
                           sem=(sR if s == 31 else None))
            p.wait("vector", sR, sR.v)
        v = p.emit("vector", "tensor_copy", idxb[:, :], idx[:, :], sem=sV)
        for ti, (ncode, luts, outd) in enumerate(((256, lut8s, whT_d), (16, lut4s, wlT_d))):
            if ti == 1:
                p.emit("vector", "tensor_single_scalar", idx[:, :], idx[:, :], 15,
                       ALU.bitwise_and)
                v = p.emit("vector", "tensor_copy", idxb[:, :], idx[:, :], sem=sV)
            p.wait("vector", sV, v)
            ps_acc = p.psum(f"dq_pa", [128, D])
            sP = p.sem("dq_p")
            sPE = p.sem("dq_pe")
            baseP = sP.v
            basePE = sPE.v
            ncd = len(cands)

            def ts_op(c):
                if c >= ncd:
                    p.wait("vector", sPE, basePE + (c - ncd) + 1)
                p.emit("vector", "tensor_scalar", cands[c % ncd][:, :], idxb[:, :],
                       float(c), luts[:, c:c + 1], ALU.is_equal, ALU.mult,
                       nochain=True, sem=sP)

            for c in range(ncode):
                ts_op(c)
                p.wait("tensor", sP, baseP + c + 1)
                for t in range(8):
                    p.emit("tensor", "matmul", ps_acc[:, t * 512:(t + 1) * 512],
                           cm.ident_bf[:, :], cands[c % ncd][:, t * 512:(t + 1) * 512],
                           start=(c == 0), stop=(c == ncode - 1),
                           sem=(sPE if t == 7 else None))
            p.wait("vector", sPE, basePE + ncode)
            for t in range(8):
                p.emit("vector", "tensor_copy", acc[:, t * 512:(t + 1) * 512],
                       ps_acc[:, t * 512:(t + 1) * 512],
                       sem=(sV if t == 7 else None))
            v = sV.v
            p.release(ps_acc)
            ps_tr = p.psum("dq_ptr", [128, 128], BF16)
            p.wait("tensor", sV, v)
            lastVV = 0
            for g in range(D // 128):
                p.wait("tensor", sVX, lastVV)
                tv = p.emit("tensor", "transpose", ps_tr[:, :], acc[:, g * 128:(g + 1) * 128],
                            cm.ident_bf[:, :], sem=sT)
                p.wait("vector", sT, tv)
                vv = p.emit("vector", "tensor_copy", trr[:, :], ps_tr[:, :], sem=sVX)
                lastVV = vv
                p.wait("scalar", sVX, vv)
                ow = p.dma("scalar", outd[g * 128:(g + 1) * 128, ck * 128:(ck + 1) * 128],
                           trr[:, :], sem=sDo)
                p.wait("vector", sDo, ow)
            p.wait("tensor", sVX, lastVV)
            p.release(ps_tr)
    p.wait("scalar", sDo, sDo.v)
    p.barrier()



def build_l3_ffn(nt=1024):
    p = Prog("l3ffn")
    x_d = p.din("x", [D, nt], F32)
    wlT_d = p.din("wlT", [D, D], BF16)
    whT_d = p.din("whT", [D, D], BF16)
    bias_d = p.din("bias2", [128, NCH], F32)
    xo_d = p.dout("xo", [D, nt], F32)
    cm = prewarm(p)
    emit_l3_ffn(p, cm, x_d, wlT_d, whT_d, bias_d, xo_d, nt)
    return p


def emit_l3_ffn(p, cm, x_d, wlT_d, whT_d, bias_d, xo_d, nt=1024):
    ntile = nt // TT
    xhat = p.sbuf("xhat", [128, NCH * nt], BF16)
    stats = p.sbuf("stats", [1, 4 * nt], F32)
    emit_ln(p, cm, x_d, nt, xhat, stats)

    mb = p.sbuf("mb", [128, nt], BF16)
    m01 = p.sbuf("m01", [1, nt], BF16)
    bb_f = p.sbuf("bb3", [128, NCH], F32)
    ps_b = p.psum("f3_pb", [128, TT])
    sW = p.sem("f3_w")
    sT = p.sem("f3_t")
    sV = p.sem("f3_v")
    w = p.dma("sync", bb_f[:, :], bias_d, sem=sW)
    p.wait("vector", sW, w)
    p.emit("vector", "tensor_scalar_mul", bb_f[:, :], bb_f[:, :], 0.5)
    for it in range(ntile):
        t0 = it * TT
        e_sl = stats[0:1, 2 * nt + t0:2 * nt + t0 + TT]
        v = p.emit("vector", "tensor_scalar", m01[0:1, t0:t0 + TT], e_sl, 64.0, None,
                   ALU.is_gt, ALU.bypass, sem=sV)
        p.wait("tensor", sV, v)
        v = p.emit("tensor", "matmul", ps_b[:, :], cm.ones1_bf[:, :], m01[0:1, t0:t0 + TT],
                   start=True, stop=True, sem=sT)
        p.wait("vector", sT, v)
        v = p.emit("vector", "tensor_copy", mb[:, t0:t0 + TT], ps_b[:, :], sem=sV)
    p.barrier()

    wsl = WStream(p, "wl3", wlT_d, NCH, src_bf16=True)
    wsh = WStream(p, "wh3", whT_d, NCH, src_bf16=True)
    ps_l = [p.psum("f3_pl", [128, TT]) for _ in range(2)]
    ps_h = [p.psum("f3_ph", [128, TT]) for _ in range(2)]
    tl = p.sbuf("f3_tl", [128, TT], F32)
    th = p.sbuf("f3_th", [128, TT], F32)
    sel = p.sbuf("f3_sel", [128, TT], F32)
    xstage = p.sbuf("f3_xs", [128, TT], F32)
    sDi = p.sem("f3_di")
    sDo = p.sem("f3_do")
    sV2 = p.sem("f3_v2")
    sT2 = p.sem("f3_t2")
    lastE = [0, 0]
    lds = {0: (wsl.load(0), wsh.load(0)), 1: (wsl.load(1), wsh.load(1))}
    for oc in range(NCH):
        (bl, gl), (bh, gh_) = lds[oc]
        p.wait("tensor", gl[0], gl[1])
        p.wait("tensor", gh_[0], gh_[1])
        for it in range(ntile):
            t0 = it * TT
            pi = (oc * ntile + it) % 2
            p.wait("tensor", sV2, lastE[pi])
            for c in range(NCH):
                p.emit("tensor", "matmul", ps_l[pi][:, :], wsl.wb[bl][:, c * 128:(c + 1) * 128],
                       xhat[:, c * nt + t0:c * nt + t0 + TT], start=(c == 0),
                       stop=(c == NCH - 1))
            for c in range(NCH):
                p.emit("tensor", "matmul", ps_h[pi][:, :], wsh.wb[bh][:, c * 128:(c + 1) * 128],
                       xhat[:, c * nt + t0:c * nt + t0 + TT], start=(c == 0),
                       stop=(c == NCH - 1), sem=(sT2 if c == NCH - 1 else None))
            p.wait("vector", sT2, sT2.v)
            w = p.dma("sync", xstage[:, :], x_d[oc * 128:(oc + 1) * 128, t0:t0 + TT], sem=sDi)
            p.emit("vector", "tensor_scalar", tl[:, :], ps_l[pi][:, :], 0.5,
                   bb_f[:, oc:oc + 1], ALU.mult, ALU.add)
            p.emit("vector", "tensor_scalar", th[:, :], ps_h[pi][:, :], 0.5,
                   bb_f[:, oc:oc + 1], ALU.mult, ALU.add)
            p.emit("vector", "tensor_sub", th[:, :], th[:, :], tl[:, :])
            p.emit("vector", "tensor_mul", th[:, :], th[:, :], mb[:, t0:t0 + TT])
            p.emit("vector", "tensor_add", sel[:, :], tl[:, :], th[:, :])
            p.wait("vector", sDi, w)
            v = p.emit("vector", "tensor_add", xstage[:, :], xstage[:, :], sel[:, :], sem=sV2)
            lastE[pi] = v
            p.wait("scalar", sV2, v)
            ow = p.dma("scalar", xo_d[oc * 128:(oc + 1) * 128, t0:t0 + TT], xstage[:, :], sem=sDo)
            p.wait("sync", sDo, ow)
        wsl.free((sT2.h, sT2.v))
        wsh.free((sT2.h, sT2.v))
        if oc + 2 < NCH:
            lds[oc + 2] = (wsl.load(oc + 2), wsh.load(oc + 2))
    p.wait("scalar", sDo, sDo.v)
    p.barrier()
